# version 07
# baseline (speedup 1.0000x reference)
import math
import numpy as np
from contextlib import ExitStack

import concourse.bass as bass
import concourse.mybir as mybir
from concourse.bass_utils import run_bass_kernel_spmd

F32 = mybir.dt.float32
BF16 = mybir.dt.bfloat16
AF = mybir.ActivationFunctionType
ALU = mybir.AluOpType

NCORES = 8
NSEQ = 2
TFULL = 2048
CDIM = 1024
CCH = 8
NH = 16
KVH = 4
DH = 64
DIN = 2048
DCH = 16
DSTATE = 16
DTRANK = 64
NEXP = 8
DFF = 4096
EPS_LN = 1e-5
EPS_RMS = 1.1920929e-07
EPS_MOE = 1e-10
MASK_NEG = -1e30
MHAT_CLAMP = 30.0


class Ctx:
    NQP = 12

    def __init__(self, nc, blk, es):
        self.nc = nc
        self.blk = blk
        self.es = es
        self.sems = {}
        for name in ("eT", "eV", "eS", "eG", "qGa", "qGb", "qGc"):
            self.sems[name] = [es.enter_context(nc.semaphore(name)), 0]
        self.qpool = [es.enter_context(nc.semaphore(f"qP{i}")) for i in range(self.NQP)]
        self.qcnt = [0] * self.NQP
        self.qused = [False] * self.NQP
        self.qi = 0

    def dma_batch(self, y):
        i = self.qi % self.NQP
        self.qi += 1
        if self.qused[i]:
            y.wait_ge(self.qpool[i], self.qcnt[i])
        self.qused[i] = True
        return i

    def qinc(self, i, inst):
        self.qcnt[i] += 16
        inst.then_inc(self.qpool[i], 16)
        return ("Q", i, self.qcnt[i])

    def inc(self, name, inst, k=1):
        h = self.sems[name]
        inst.then_inc(h[0], k)
        h[1] += k
        return (name, h[1])

    def tok(self, name, delta=0):
        return (name, self.sems[name][1] + delta)

    def val(self, name):
        return self.sems[name][1]

    def wait(self, eng, *tokens):
        for tok in tokens:
            if tok[0] == "Q":
                _, i, v = tok
                if v > 0:
                    eng.wait_ge(self.qpool[i], v)
            else:
                name, v = tok
                if v > 0:
                    eng.wait_ge(self.sems[name][0], v)

    def T(self, fn): self.blk.tensor(fn)
    def V(self, fn): self.blk.vector(fn)
    def S(self, fn): self.blk.scalar(fn)
    def G(self, fn): self.blk.gpsimd(fn)
    def Y(self, fn): self.blk.sync(fn)


def build_program(T=TFULL, nseq=NSEQ, stages=(1, 2, 3, 4, 5, 6)):
    nc = bass.Bass("TRN2", target_bir_lowering=False)
    TC = T // 512 if T >= 512 else 1
    TQ = min(512, T)
    NFC = T // TQ
    NTT = T // 128

    dram = {}

    def din(name, shape, dt=F32):
        dram[name] = nc.dram_tensor(name, shape, dt, kind="ExternalInput").ap()
        return dram[name]

    def dint(name, shape, dt):
        dram[name] = nc.dram_tensor(name, shape, dt, kind="Internal").ap()
        return dram[name]

    x_in = din("x", [nseq, CCH, 128, T])
    out = nc.dram_tensor("out", [nseq, CCH, 128, T], F32, kind="ExternalOutput").ap()

    w_f32 = {}
    def wdecl(name, shape):
        w_f32[name] = din(name, shape)
        return w_f32[name]

    wdecl("l0f_wr_t", [CDIM, CDIM])
    wdecl("l0f_wk_t", [CDIM, CDIM])
    wdecl("l0f_wv_t", [CDIM, CDIM])
    wdecl("l0a_wq", [CDIM, CDIM])
    wdecl("l0a_wk", [CDIM, 256])
    wdecl("l0a_wv", [CDIM, 256])
    wdecl("l0a_wo", [CDIM, CDIM])
    wdecl("l1a_inproj_ra", [CDIM, DCH, 256])
    wdecl("l1a_xproj_t", [DIN, 96])
    wdecl("l1a_dtproj_t", [128, 1024])
    wdecl("l1a_outproj_t", [DIN, CDIM])
    wdecl("l1a_w1t", [CDIM, DFF])
    wdecl("l1a_w2t", [DFF, CDIM])
    wdecl("l1m_rwt", [CDIM, NEXP])
    wdecl("l1m_ew", [NEXP, CDIM, CDIM])
    din("l0f_mur", [128, CCH])
    din("l0f_muk", [128, CCH])
    din("l0a_normw", [128, CCH])
    din("l1a_convw", [128, 4 * DCH])
    din("l1a_convb", [128, DCH])
    din("l1a_dtb", [128, DCH])
    din("l1a_alog", [128, DSTATE * DCH])
    din("l1a_dvec", [128, DCH])
    din("l1a_b1", [128, 32])
    din("l1a_b2", [128, CCH])
    din("l1m_rb", [1, NEXP])
    din("pos_t", [1, T])
    din("dhexp", [128, 1])
    din("sgn_t", [1, T])
    din("ident_bf", [128, 128], BF16)
    din("ident_f", [128, 128])
    din("maskT_bf", [128, 128], BF16)
    din("sel16", [16, 2048])
    din("onespair", [128, 8], BF16)

    wbf = {}
    for name in w_f32:
        wbf[name] = dint("bf_" + name, list(w_f32[name].shape), BF16)

    spill_x = dint("spill_x", [CCH, 128, T], F32)
    spill_mc = dint("spill_mc", [CCH, 128, T], F32)

    es = ExitStack()

    def sb(name, shape, dt):
        return es.enter_context(nc.sbuf_tensor(name, shape, dt))

    def ps(name, shape, dt=F32):
        return es.enter_context(nc.psum_tensor(name, shape, dt))

    x_sb = sb("x_sb", [128, max(CCH * T, 32 * min(512, T))], F32)
    pool1 = sb("pool1", [128, 8192], F32)
    pool2 = sb("pool2", [128, 8192], F32)
    pool3 = sb("pool3", [128, 8192], F32)
    pool4 = sb("pool4", [128, 4096], F32)
    cvec = sb("cvec", [128, 2048], F32)
    stat = sb("stat", [128, 96], F32)
    rowsM = sb("rowsM", [65, T + 512], F32)
    cst = sb("cst", [128, 160], F32)
    ident_bf = sb("s_ident_bf", [128, 128], BF16)
    rw_sb = sb("s_rw", [128, 64], BF16)
    rw_lo = sb("s_rwlo", [128, 64], BF16)
    w5a = sb("s_w5a", [128, 2048], BF16)

    mask_bf = sb("s_maskT_bf", [128, 128], BF16)
    onespair = sb("s_onespair", [128, 8], BF16)
    sel16 = sb("s_sel16", [16, 2048], F32)

    pSC = ps("pSC", [128, 2048], F32)
    pA = ps("pA", [128, 512], F32)
    pB = ps("pB", [128, 512], F32)
    pC = ps("pC", [128, 512], F32)
    pD = ps("pD", [128, 512], F32)

    blk = es.enter_context(nc.Block())
    C = Ctx(nc, blk, es)

    with es:
        emit_all(C, nc, dram, wbf, x_in, out, spill_x, spill_mc,
                 x_sb, pool1, pool2, pool3, pool4,
                 cvec, stat, rowsM, cst, ident_bf, rw_sb, rw_lo, w5a, mask_bf, onespair, sel16,
                 pSC, pA, pB, pC, pD, T, nseq, stages)
    return nc


def emit_all(C, nc, dram, wbf, x_in, out, spill_x, spill_mc,
             x_sb, pool1, pool2, pool3, pool4,
             cvec, stat, rowsM, cst, ident_bf, rw_sb, rw_lo, w5a, mask_bf, onespair, sel16,
             pSC, pA, pB, pC, pD, T, nseq, stages):
    p1bf = pool1.bitcast(BF16)
    p2bf = pool2.bitcast(BF16)
    p3bf = pool3.bitcast(BF16)
    p4bf = pool4.bitcast(BF16)
    arA = pool1
    arB = pool2
    arC = pool3
    TQ = min(512, T)
    NFC = T // TQ

    CV = {}
    off = 0
    def cv(name, width):
        nonlocal off
        CV[name] = off
        off += width
        return CV[name]
    cv("mur", CCH)
    cv("muk", CCH)
    cv("normw", CCH)
    cv("convw", 4 * DCH)
    cv("convb", DCH)
    cv("dtb", DCH)
    cv("A", DSTATE * DCH)
    cv("D", DCH)
    cv("b1", 32)
    cv("b2", CCH)
    cv("mcols", NH)
    cv("theta", 1)
    cv("hcarry", 256)
    cv("convtail", 48)
    cv("gstat", 2)
    assert off <= 768

    CS = {}
    off2 = 0
    def cs(name, width):
        nonlocal off2
        CS[name] = off2
        off2 += width
        return CS[name]
    cs("ones_col", 1)
    cs("ones_row", 128)
    cs("zero_col", 1)
    cs("halfpi", 1)
    cs("ln1e4", 1)
    cs("tenk", 1)
    cs("rbbc", NEXP)
    cs("epsln", 1)
    cs("epsrms", 1)
    cs("scratch", 16)
    assert off2 <= 160

    eT = "eT"; eV = "eV"; eS = "eS"; eG = "eG"

    cast_tok = {}

    GROUP_A = ("l0f_wr_t", "l0f_wk_t", "l0f_wv_t", "l0a_wq", "l0a_wk", "l0a_wv", "l0a_wo")
    GROUP_B = ("l1a_inproj_ra", "l1a_xproj_t", "l1a_dtproj_t", "l1a_outproj_t",
               "l1a_w1t", "l1a_w2t", "l1m_rwt")

    def _(g):
        for name in GROUP_A:
            C.inc("qGa", g.dma_start(out=wbf[name], in_=dram[name]), 16)
        for name in GROUP_B:
            C.inc("qGb", g.dma_start(out=wbf[name], in_=dram[name]), 16)
        for e in range(NEXP):
            C.inc("qGc", g.dma_start(out=wbf["l1m_ew"][e], in_=dram["l1m_ew"][e]), 16)
    C.G(_)
    for name in GROUP_A:
        cast_tok["bf_" + name] = C.tok("qGa")
    for name in GROUP_B:
        cast_tok["bf_" + name] = C.tok("qGb")
    cast_tok["bf_l1m_ew"] = C.tok("qGc")

    def _(y):
        b = C.dma_batch(y)
        for dst, src in [
            (cvec[:, CV["mur"]:CV["mur"] + CCH], dram["l0f_mur"]),
            (cvec[:, CV["muk"]:CV["muk"] + CCH], dram["l0f_muk"]),
            (cvec[:, CV["normw"]:CV["normw"] + CCH], dram["l0a_normw"]),
            (cvec[:, CV["convw"]:CV["convw"] + 4 * DCH], dram["l1a_convw"]),
            (cvec[:, CV["convb"]:CV["convb"] + DCH], dram["l1a_convb"]),
            (cvec[:, CV["dtb"]:CV["dtb"] + DCH], dram["l1a_dtb"]),
            (cvec[:, CV["A"]:CV["A"] + DSTATE * DCH], dram["l1a_alog"]),
            (cvec[:, CV["D"]:CV["D"] + DCH], dram["l1a_dvec"]),
            (cvec[:, CV["b1"]:CV["b1"] + 32], dram["l1a_b1"]),
            (cvec[:, CV["b2"]:CV["b2"] + CCH], dram["l1a_b2"]),
            (stat[0:1, 0:NEXP], dram["l1m_rb"]),
            (cvec[:, CV["theta"]:CV["theta"] + 1], dram["dhexp"]),
            (ident_bf[:], dram["ident_bf"]),
            (mask_bf[:], dram["maskT_bf"]),
            (sel16[:], dram["sel16"]),
            (onespair[:], dram["onespair"]),
        ]:
            t = C.qinc(b, y.dma_start(out=dst, in_=src))
        cast_tok["consts"] = t
    C.Y(_)
    tok_consts = t_consts = cast_tok["consts"]

    rw_box = []
    def _(y):
        C.wait(y, cast_tok["bf_l1m_rwt"])
        b = C.dma_batch(y)
        C.qinc(b, y.dma_start(out=rw_sb[:, 0:64].rearrange("p (a d) -> p a d", a=CCH),
                              in_=dram["bf_l1m_rwt"].rearrange("(a p) d -> p a d", p=128)))
        rw_box.append(C.qinc(b, y.dma_start(out=pool2[:, 0:64].rearrange("p (a d) -> p a d", a=CCH),
                                            in_=dram["l1m_rwt"].rearrange("(a p) d -> p a d", p=128))))
    C.Y(_)
    def _(v, rt=rw_box[0]):
        C.wait(v, rt)
        C.inc(eV, v.tensor_tensor(out=rw_lo[:, 0:64], in0=pool2[:, 0:64], in1=rw_sb[:, 0:64],
                                  op=ALU.subtract))
    C.V(_)
    tok_rw = C.tok(eV)

    def _(v):
        C.wait(v, tok_consts)
        v.memset(cst[:, CS["ones_col"]:CS["ones_col"] + 1], 1.0)
        v.memset(cst[0:1, CS["ones_row"]:CS["ones_row"] + 128], 1.0)
        v.memset(cst[32:33, CS["ones_row"]:CS["ones_row"] + 128], 1.0)
        v.memset(cst[:, CS["zero_col"]:CS["zero_col"] + 1], 0.0)
        v.memset(cst[:, CS["halfpi"]:CS["halfpi"] + 1], math.pi / 2)
        v.memset(cst[:, CS["tenk"]:CS["tenk"] + 1], 10000.0)
        v.memset(cst[:, CS["epsln"]:CS["epsln"] + 1], EPS_LN)
        v.memset(cst[:, CS["epsrms"]:CS["epsrms"] + 1], EPS_RMS)
        t = C.inc(eV, v.engine_nop())
    C.V(_)
    tok_cst0 = C.tok(eV)

    def _(sc):
        C.wait(sc, tok_cst0)
        t = C.inc(eS, sc.activation(out=cst[:, CS["ln1e4"]:CS["ln1e4"] + 1],
                                    in_=cst[:, CS["tenk"]:CS["tenk"] + 1], func=AF.Ln))
        C.wait(sc, t)
        t = C.inc(eS, sc.activation(out=cvec[:, CV["theta"]:CV["theta"] + 1],
                                    in_=cvec[:, CV["theta"]:CV["theta"] + 1],
                                    func=AF.Exp, scale=cst[:, CS["ln1e4"]:CS["ln1e4"] + 1]))
    C.S(_)
    tok_theta = C.tok(eS)

    def _(s):
        C.wait(s, tok_consts)
        t = C.inc(eS, s.activation(out=cvec[:, CV["A"]:CV["A"] + DSTATE * DCH],
                                   in_=cvec[:, CV["A"]:CV["A"] + DSTATE * DCH], func=AF.Exp))
    C.S(_)
    def _(v):
        C.wait(v, C.tok(eS))
        t = C.inc(eV, v.tensor_scalar(out=cvec[:, CV["A"]:CV["A"] + DSTATE * DCH],
                                      in0=cvec[:, CV["A"]:CV["A"] + DSTATE * DCH],
                                      scalar1=-1.0, scalar2=None, op0=ALU.mult))
    C.V(_)
    tok_A = C.tok(eV)

    def _(t):
        C.wait(t, tok_consts)
        C.wait(t, tok_cst0)
        tk = C.inc(eT, t.matmul(out=pA[:, 0:NEXP], lhsT=cst[0:1, CS["ones_row"]:CS["ones_row"] + 128],
                                rhs=stat[0:1, 0:NEXP], start=True, stop=True))
    C.T(_)
    def _(s):
        C.wait(s, C.tok(eT))
        t = C.inc(eS, s.copy(out=cst[:, CS["rbbc"]:CS["rbbc"] + NEXP], in_=pA[:, 0:NEXP]))
    C.S(_)
    tok_rbbc = C.tok(eS)

    ones_bf = C.es.enter_context(nc.sbuf_tensor("ones_bf", [128, 2], BF16))

    def _(v):
        t = C.inc(eV, v.memset(ones_bf[:], 1.0))
    C.V(_)

    SQOFF = 8192

    def ln_stats(prev_x_tok):
        baseT0 = C.val(eT)
        def _(t):
            C.wait(t, *prev_x_tok)
            for j in range(NFC):
                for cc in range(CCH):
                    mm = t.matmul(out=pSC[0:1, j * TQ:(j + 1) * TQ],
                                  lhsT=cst[:, CS["ones_col"]:CS["ones_col"] + 1],
                                  rhs=x_sb[:, cc * T + j * TQ:cc * T + (j + 1) * TQ],
                                  start=(cc == 0), stop=(cc == CCH - 1))
            C.inc(eT, mm)
        C.T(_)
        baseS = C.val(eS)
        def _(s):
            C.wait(s, *prev_x_tok)
            for cc in range(CCH):
                if cc >= 2:
                    C.wait(s, (eT, baseT0 + 1 + cc - 1))
                C.inc(eS, s.activation(out=p3bf[:, SQOFF + (cc % 2) * T:SQOFF + (cc % 2) * T + T],
                                       in_=x_sb[:, cc * T:(cc + 1) * T], func=AF.Square))
        C.S(_)
        def _(t):
            for cc in range(CCH):
                C.wait(t, (eS, baseS + cc + 1))
                for j in range(NFC):
                    mm = t.matmul(out=pSC[32:33, j * TQ:(j + 1) * TQ],
                                  lhsT=ones_bf[:, 0:1],
                                  rhs=p3bf[:, SQOFF + (cc % 2) * T + j * TQ:SQOFF + (cc % 2) * T + (j + 1) * TQ],
                                  start=(cc == 0), stop=(cc == CCH - 1))
                C.inc(eT, mm)
        C.T(_)
        mm_tok = C.tok(eT)

        def _(v):
            C.wait(v, mm_tok)
            t = C.inc(eV, v.tensor_scalar(out=rowsM[32:33, 0:T], in0=pSC[0:1, 0:T],
                                          scalar1=1.0 / CDIM, scalar2=None, op0=ALU.mult))
            C.wait(v, t)
            t = C.inc(eV, v.tensor_tensor(out=rowsM[0:1, 0:T], in0=rowsM[32:33, 0:T],
                                          in1=rowsM[32:33, 0:T], op=ALU.mult))
            C.wait(v, t)
            st = v.scalar_tensor_tensor(out=pSC[64:65, 0:T], in0=pSC[32:33, 0:T],
                                        scalar=1.0 / CDIM, in1=rowsM[0:1, 0:T],
                                        op0=ALU.mult, op1=ALU.subtract)
            C.inc(eV, st)
        C.V(_)
        vt = C.tok(eV)

        def _(s):
            C.wait(s, vt)
            t = C.inc(eS, s.activation(out=rowsM[0:1, 0:T], in_=pSC[64:65, 0:T], func=AF.Ln,
                                       bias=cst[32:33, CS["epsln"]:CS["epsln"] + 1]))
            C.wait(s, t)
            C.inc(eS, s.activation(out=rowsM[0:1, 0:T], in_=rowsM[0:1, 0:T], func=AF.Exp, scale=-0.5))
        C.S(_)
        st_tok = C.tok(eS)

        def _(v):
            C.wait(v, st_tok)
            C.inc(eV, v.engine_nop())
        C.V(_)
        return C.tok(eV)

    def ln_bcast(t, j, stats_tok, want=True):
        C.wait(t, stats_tok)
        t.matmul(out=pSC[:, 0:TQ], lhsT=cst[32:33, CS["ones_row"]:CS["ones_row"] + 128],
                 rhs=rowsM[32:33, j * TQ:(j + 1) * TQ], start=True, stop=True)
        mm = t.matmul(out=pSC[:, TQ:2 * TQ], lhsT=cst[0:1, CS["ones_row"]:CS["ones_row"] + 128],
                      rhs=rowsM[0:1, j * TQ:(j + 1) * TQ], start=True, stop=True)
        return C.inc(eT, mm)

    def rwkv(seq, prev_toks):
        wtok_box = []
        def _(y):
            C.wait(y, cast_tok["bf_l0f_wr_t"], cast_tok["bf_l0f_wk_t"], cast_tok["bf_l0f_wv_t"])
            C.wait(y, *prev_toks)
            b = C.dma_batch(y)
            C.qinc(b, y.dma_start(out=p2bf[:, 0:8192].rearrange("p (a d) -> p a d", a=CCH),
                                  in_=dram["bf_l0f_wr_t"].rearrange("(a p) d -> p a d", p=128)))
            C.qinc(b, y.dma_start(out=p2bf[:, 8192:16384].rearrange("p (a d) -> p a d", a=CCH),
                                  in_=dram["bf_l0f_wk_t"].rearrange("(a p) d -> p a d", p=128)))
            wtok_box.append(C.qinc(b, y.dma_start(out=p3bf[:, 0:8192].rearrange("p (a d) -> p a d", a=CCH),
                                                  in_=dram["bf_l0f_wv_t"].rearrange("(a p) d -> p a d", p=128))))
        C.Y(_)
        w_tok = wtok_box[0]

        stats_tok = ln_stats(prev_toks)

        xn = pool1
        XROFF = 8192
        XKOFF = 12288
        r32 = pool4
        PCOL = CS["scratch"]

        prev_mix_tok = C.tok(eV)
        for j in range(NFC):
            def _(t, j=j, pm=prev_mix_tok):
                if j == 0:
                    C.wait(t, w_tok)
                else:
                    C.wait(t, pm)
                ln_bcast(t, j, stats_tok)
            C.T(_)
            bc_tok = C.tok(eT)

            def _(v, j=j, bc=bc_tok):
                C.wait(v, bc)
                DOFF = 4096
                tk = None
                for cc in range(CCH):
                    xo = cc * TQ
                    xs = x_sb[:, cc * T + j * TQ: cc * T + (j + 1) * TQ]
                    tk = v.tensor_tensor(out=xn[:, xo:xo + TQ], in0=xs, in1=pSC[:, 0:TQ], op=ALU.subtract)
                C.wait(v, C.inc(eV, tk))
                for cc in range(CCH):
                    xo = cc * TQ
                    tk = v.tensor_tensor(out=xn[:, xo:xo + TQ], in0=xn[:, xo:xo + TQ], in1=pSC[:, TQ:2 * TQ], op=ALU.mult)
                C.wait(v, C.inc(eV, tk))
                for cc in range(CCH):
                    xo = cc * TQ
                    tk = v.tensor_tensor(out=pool1[:, DOFF + xo + 1:DOFF + xo + TQ], in0=xn[:, xo + 1:xo + TQ],
                                    in1=xn[:, xo:xo + TQ - 1], op=ALU.subtract)
                for cc in range(CCH):
                    xo = cc * TQ
                    if j == 0:
                        tk = v.tensor_copy(out=pool1[:, DOFF + xo:DOFF + xo + 1], in_=xn[:, xo:xo + 1])
                    else:
                        tk = v.tensor_tensor(out=pool1[:, DOFF + xo:DOFF + xo + 1], in0=xn[:, xo:xo + 1],
                                        in1=cst[:, PCOL + cc:PCOL + cc + 1], op=ALU.subtract)
                C.wait(v, C.inc(eV, tk))
                for cc in range(CCH):
                    xo = cc * TQ
                    mur = cvec[:, CV["mur"] + cc:CV["mur"] + cc + 1]
                    v.scalar_tensor_tensor(out=p3bf[:, XROFF + xo + 1:XROFF + xo + TQ],
                                           in0=pool1[:, DOFF + xo + 1:DOFF + xo + TQ], scalar=mur,
                                           in1=xn[:, xo:xo + TQ - 1], op0=ALU.mult, op1=ALU.add)
                    if j == 0:
                        v.tensor_scalar(out=p3bf[:, XROFF + xo:XROFF + xo + 1],
                                        in0=pool1[:, DOFF + xo:DOFF + xo + 1], scalar1=mur,
                                        scalar2=None, op0=ALU.mult)
                    else:
                        v.scalar_tensor_tensor(out=p3bf[:, XROFF + xo:XROFF + xo + 1],
                                               in0=pool1[:, DOFF + xo:DOFF + xo + 1], scalar=mur,
                                               in1=cst[:, PCOL + cc:PCOL + cc + 1], op0=ALU.mult, op1=ALU.add)
                for cc in range(CCH):
                    xo = cc * TQ
                    muk = cvec[:, CV["muk"] + cc:CV["muk"] + cc + 1]
                    tk = v.scalar_tensor_tensor(out=p3bf[:, XKOFF + xo + 1:XKOFF + xo + TQ],
                                                in0=pool1[:, DOFF + xo + 1:DOFF + xo + TQ], scalar=muk,
                                                in1=xn[:, xo:xo + TQ - 1], op0=ALU.mult, op1=ALU.add)
                    if j == 0:
                        tk = v.tensor_scalar(out=p3bf[:, XKOFF + xo:XKOFF + xo + 1],
                                             in0=pool1[:, DOFF + xo:DOFF + xo + 1], scalar1=muk,
                                             scalar2=None, op0=ALU.mult)
                    else:
                        tk = v.scalar_tensor_tensor(out=p3bf[:, XKOFF + xo:XKOFF + xo + 1],
                                                    in0=pool1[:, DOFF + xo:DOFF + xo + 1], scalar=muk,
                                                    in1=cst[:, PCOL + cc:PCOL + cc + 1], op0=ALU.mult, op1=ALU.add)
                C.wait(v, C.inc(eV, tk))
                for cc in range(CCH):
                    xo = cc * TQ
                    mm = v.tensor_copy(out=cst[:, PCOL + cc:PCOL + cc + 1], in_=xn[:, xo + TQ - 1:xo + TQ])
                C.inc(eV, mm)
            C.V(_)
            mix_tok = C.tok(eV)

            sig_base = C.val(eS)
            baseT = C.val(eT)
            def _(t, j=j, mt=mix_tok, sig_base=sig_base):
                C.wait(t, mt)
                for dc in range(CCH):
                    if dc >= 2:
                        C.wait(t, (eS, sig_base + dc - 1))
                    pp = pA if dc % 2 == 0 else pB
                    for cc in range(CCH):
                        mm = t.matmul(out=pp[:, 0:TQ],
                                      lhsT=p2bf[:, cc * 1024 + dc * 128: cc * 1024 + (dc + 1) * 128],
                                      rhs=p3bf[:, XROFF + cc * TQ:XROFF + (cc + 1) * TQ],
                                      start=(cc == 0), stop=(cc == CCH - 1))
                    C.inc(eT, mm)
            C.T(_)
            def _(s, baseT=baseT):
                for dc in range(CCH):
                    C.wait(s, (eT, baseT + dc + 1))
                    pp = pA if dc % 2 == 0 else pB
                    C.inc(eS, s.activation(out=r32[:, dc * TQ:(dc + 1) * TQ], in_=pp[:, 0:TQ], func=AF.Sigmoid))
            C.S(_)

            k_base = C.val(eS)
            baseT2 = C.val(eT)
            def _(t, j=j, k_base=k_base):
                for dc in range(CCH):
                    if dc >= 2:
                        C.wait(t, (eS, k_base + 2 * dc - 3))
                    pp = pC if dc % 2 == 0 else pD
                    for cc in range(CCH):
                        mm = t.matmul(out=pp[:, 0:TQ],
                                      lhsT=p2bf[:, 8192 + cc * 1024 + dc * 128: 8192 + cc * 1024 + (dc + 1) * 128],
                                      rhs=p3bf[:, XKOFF + cc * TQ:XKOFF + (cc + 1) * TQ],
                                      start=(cc == 0), stop=(cc == CCH - 1))
                    C.inc(eT, mm)
            C.T(_)
            KROFF = 6144
            def _(s, baseT2=baseT2):
                for dc in range(CCH):
                    C.wait(s, (eT, baseT2 + dc + 1))
                    kr = pool1[:, KROFF + (dc % 2) * TQ:KROFF + (dc % 2) * TQ + TQ]
                    pp = pC if dc % 2 == 0 else pD
                    tk = C.inc(eS, s.activation(out=kr, in_=pp[:, 0:TQ], func=AF.Relu))
                    C.wait(s, tk)
                    C.inc(eS, s.activation(out=p1bf[:, 8192 + dc * TQ:8192 + (dc + 1) * TQ],
                                           in_=kr, func=AF.Square))
            C.S(_)
            kact_tok = C.tok(eS)

            v_base = C.val(eV)
            baseT3 = C.val(eT)
            def _(t, j=j, kt=kact_tok, v_base=v_base):
                C.wait(t, kt)
                for dc in range(CCH):
                    if dc >= 2:
                        C.wait(t, (eV, v_base + 2 * dc - 3))
                    pp = pA if dc % 2 == 0 else pB
                    for cc in range(CCH):
                        mm = t.matmul(out=pp[:, 0:TQ],
                                      lhsT=p3bf[:, cc * 1024 + dc * 128: cc * 1024 + (dc + 1) * 128],
                                      rhs=p1bf[:, 8192 + cc * TQ:8192 + (cc + 1) * TQ],
                                      start=(cc == 0), stop=(cc == CCH - 1))
                    C.inc(eT, mm)
            C.T(_)
            TMOFF = 7168
            def _(v, j=j, baseT3=baseT3):
                for dc in range(CCH):
                    C.wait(v, (eT, baseT3 + dc + 1))
                    pp = pA if dc % 2 == 0 else pB
                    tm = pool1[:, TMOFF + (dc % 2) * TQ:TMOFF + (dc % 2) * TQ + TQ]
                    tk = C.inc(eV, v.tensor_tensor(out=tm, in0=pp[:, 0:TQ],
                                                   in1=r32[:, dc * TQ:(dc + 1) * TQ], op=ALU.mult))
                    C.wait(v, tk)
                    xs = x_sb[:, dc * T + j * TQ: dc * T + (j + 1) * TQ]
                    C.inc(eV, v.scalar_tensor_tensor(out=xs, in0=tm, scalar=0.5, in1=xs,
                                                     op0=ALU.mult, op1=ALU.add))
            C.V(_)
            prev_mix_tok = C.tok(eV)
        return [C.tok(eV)]

    def build_rope(dst, dstoff, prev_tok):
        ldtok_box = []
        def _(y):
            C.wait(y, prev_tok)
            b = C.dma_batch(y)
            C.qinc(b, y.dma_start(out=pool3[0:1, 6144:6144 + T], in_=dram["pos_t"]))
            ldtok_box.append(C.qinc(b, y.dma_start(out=pool2[0:1, 0:T], in_=dram["sgn_t"])))
        C.Y(_)
        ld_tok = ldtok_box[0]

        def _(t):
            C.wait(t, ld_tok)
            for j in range(NFC):
                mm = t.matmul(out=pSC[:, j * TQ:(j + 1) * TQ],
                              lhsT=cst[0:1, CS["ones_row"]:CS["ones_row"] + 128],
                              rhs=pool3[0:1, 6144 + j * TQ:6144 + (j + 1) * TQ],
                              start=True, stop=True)
            C.inc(eT, mm)
        C.T(_)
        pb_tok = C.tok(eT)

        def _(v):
            C.wait(v, pb_tok)
            t = C.inc(eV, v.tensor_scalar(out=cst[:, CS["scratch"]:CS["scratch"] + 1],
                                          in0=cvec[:, CV["theta"]:CV["theta"] + 1],
                                          scalar1=1.0 / (2 * math.pi), scalar2=None, op0=ALU.mult))
            C.wait(v, t)
            v.tensor_scalar(out=pool3[:, 0:T], in0=pSC[:, 0:T], scalar1=cvec[:, CV["theta"]:CV["theta"] + 1],
                            scalar2=None, op0=ALU.mult)
            t = C.inc(eV, v.tensor_scalar(out=pool3[:, 2048:2048 + T], in0=pSC[:, 0:T],
                                          scalar1=cst[:, CS["scratch"]:CS["scratch"] + 1],
                                          scalar2=0.5, op0=ALU.mult, op1=ALU.add))
            C.wait(v, t)
            t = C.inc(eV, v.tensor_copy(out=pool3.bitcast(mybir.dt.int32)[:, 4096:4096 + T],
                                        in_=pool3[:, 2048:2048 + T]))
            C.wait(v, t)
            t = C.inc(eV, v.tensor_copy(out=pool3[:, 2048:2048 + T],
                                        in_=pool3.bitcast(mybir.dt.int32)[:, 4096:4096 + T]))
            C.wait(v, t)
            t = C.inc(eV, v.scalar_tensor_tensor(out=pool3[:, 0:T], in0=pool3[:, 2048:2048 + T],
                                                 scalar=-2 * math.pi, in1=pool3[:, 0:T],
                                                 op0=ALU.mult, op1=ALU.add))
            C.wait(v, t)
            t = C.inc(eV, v.tensor_scalar(out=pool3[:, 2048:2048 + T], in0=pool3[:, 0:T],
                                          scalar1=math.pi, scalar2=None, op0=ALU.is_gt))
            C.wait(v, t)
            t = C.inc(eV, v.scalar_tensor_tensor(out=pool3[:, 0:T], in0=pool3[:, 2048:2048 + T],
                                                 scalar=-2 * math.pi, in1=pool3[:, 0:T],
                                                 op0=ALU.mult, op1=ALU.add))
            C.wait(v, t)
            t = C.inc(eV, v.tensor_scalar(out=pool3[:, 2048:2048 + T], in0=pool3[:, 0:T],
                                          scalar1=-math.pi, scalar2=None, op0=ALU.is_lt))
            C.wait(v, t)
            t = C.inc(eV, v.scalar_tensor_tensor(out=pool3[:, 0:T], in0=pool3[:, 2048:2048 + T],
                                                 scalar=2 * math.pi, in1=pool3[:, 0:T],
                                                 op0=ALU.mult, op1=ALU.add))
            C.wait(v, t)
            t = C.inc(eV, v.tensor_scalar(out=pool3[:, 0:T], in0=pool3[:, 0:T], scalar1=math.pi,
                                          scalar2=-math.pi, op0=ALU.min, op1=ALU.max))
        C.V(_)
        ang_tok = C.tok(eV)

        def _(s):
            C.wait(s, ang_tok)
            s.activation(out=pool3[:, 2048:2048 + T], in_=pool3[:, 0:T], func=AF.Sin)
            t = C.inc(eS, s.activation(out=pool3[:, 4096:4096 + T], in_=pool3[:, 0:T], func=AF.Abs))
            C.wait(s, t)
            t = C.inc(eS, s.activation(out=pool3[:, 4096:4096 + T], in_=pool3[:, 4096:4096 + T], func=AF.Sin,
                                       scale=-1.0, bias=cst[:, CS["halfpi"]:CS["halfpi"] + 1]))
        C.S(_)
        sc_tok = C.tok(eS)

        def _(t):
            C.wait(t, sc_tok)
            for j in range(NFC):
                mm = t.matmul(out=pSC[:, j * TQ:(j + 1) * TQ],
                              lhsT=cst[0:1, CS["ones_row"]:CS["ones_row"] + 128],
                              rhs=pool2[0:1, j * TQ:(j + 1) * TQ],
                              start=True, stop=True)
            C.inc(eT, mm)
        C.T(_)
        sb_tok = C.tok(eT)

        def _(v):
            C.wait(v, sb_tok)
            t = C.inc(eV, v.tensor_tensor(out=dst[:, dstoff:dstoff + T], in0=pool3[:, 2048:2048 + T],
                                          in1=pSC[:, 0:T], op=ALU.mult))
            C.wait(v, t)
            t = C.inc(eV, v.tensor_tensor(out=dst[:, dstoff:dstoff + T], in0=dst[:, dstoff:dstoff + T],
                                          in1=pool3[:, 4096:4096 + T], op=ALU.add))
        C.V(_)
        return C.tok(eV)

    def attention(seq, prev_toks):
        NTT = T // 128
        wtok_box = []
        def _(y):
            C.wait(y, cast_tok["bf_l0a_wq"], cast_tok["bf_l0a_wk"], cast_tok["bf_l0a_wv"])
            C.wait(y, *prev_toks)
            b = C.dma_batch(y)
            C.qinc(b, y.dma_start(out=p4bf[:, 0:8192].rearrange("p (a d) -> p a d", a=CCH),
                                  in_=dram["bf_l0a_wq"].rearrange("(a p) d -> p a d", p=128)))
            C.qinc(b, y.dma_start(out=p2bf[:, 6144:8192].rearrange("p (a d) -> p a d", a=CCH),
                                  in_=dram["bf_l0a_wk"].rearrange("(a p) d -> p a d", p=128)))
            wtok_box.append(C.qinc(b, y.dma_start(out=p2bf[:, 8192:10240].rearrange("p (a d) -> p a d", a=CCH),
                                                  in_=dram["bf_l0a_wv"].rearrange("(a p) d -> p a d", p=128))))
        C.Y(_)
        w_tok = wtok_box[0]

        stats_tok = ln_stats(prev_toks)

        RFOFF = 4096
        rope_tok = build_rope(pool1, RFOFF, stats_tok)

        XGOFF = 0
        SQOFF2 = 4096
        VBOFF = 10240
        KOFF = 12288

        for j in range(NFC):
            def _(t, j=j):
                if j == 0:
                    C.wait(t, rope_tok)
                ln_bcast(t, j, stats_tok)
            C.T(_)
            bc_tok = C.tok(eT)

            def _(v, j=j, bc=bc_tok):
                C.wait(v, bc)
                tk = None
                for cc in range(CCH):
                    xs = x_sb[:, cc * T + j * TQ: cc * T + (j + 1) * TQ]
                    tk = v.tensor_tensor(out=pool1[:, cc * TQ:(cc + 1) * TQ], in0=xs, in1=pSC[:, 0:TQ], op=ALU.subtract)
                C.wait(v, C.inc(eV, tk))
                for cc in range(CCH):
                    tk = v.tensor_tensor(out=pool1[:, cc * TQ:(cc + 1) * TQ], in0=pool1[:, cc * TQ:(cc + 1) * TQ],
                                         in1=pSC[:, TQ:2 * TQ], op=ALU.mult)
                C.wait(v, C.inc(eV, tk))
                for cc in range(CCH):
                    xs = x_sb[:, cc * T + j * TQ: cc * T + (j + 1) * TQ]
                    tk = v.tensor_tensor(out=xs, in0=xs, in1=pool1[:, cc * TQ:(cc + 1) * TQ], op=ALU.add)
                C.inc(eV, tk)
            C.V(_)
            xn_tok = C.tok(eV)

            sq_base = C.val(eS)
            rmsT0 = C.val(eT)
            def _(s, j=j, xt=xn_tok, rmsT0=rmsT0):
                C.wait(s, xt)
                for cc in range(CCH):
                    if cc >= 2:
                        C.wait(s, (eT, rmsT0 + cc - 1))
                    C.inc(eS, s.activation(out=p2bf[:, SQOFF2 + (cc % 2) * TQ:SQOFF2 + (cc % 2) * TQ + TQ],
                                           in_=pool1[:, cc * TQ:(cc + 1) * TQ], func=AF.Square))
            C.S(_)
            def _(t, j=j, sq_base=sq_base):
                for cc in range(CCH):
                    C.wait(t, (eS, sq_base + cc + 1))
                    mm = t.matmul(out=pB[0:1, 0:TQ], lhsT=ones_bf[:, 0:1],
                                  rhs=p2bf[:, SQOFF2 + (cc % 2) * TQ:SQOFF2 + (cc % 2) * TQ + TQ],
                                  start=(cc == 0), stop=(cc == CCH - 1))
                    C.inc(eT, mm)
            C.T(_)
            rmsmm_tok = C.tok(eT)

            def _(s, j=j, mt=rmsmm_tok):
                C.wait(s, mt)
                t = C.inc(eS, s.activation(out=rowsM[0:1, T:T + TQ], in_=pB[0:1, 0:TQ], func=AF.Ln,
                                           scale=1.0 / CDIM, bias=cst[0:1, CS["epsrms"]:CS["epsrms"] + 1]))
                C.wait(s, t)
                C.inc(eS, s.activation(out=rowsM[0:1, T:T + TQ], in_=rowsM[0:1, T:T + TQ],
                                       func=AF.Exp, scale=-0.5))
            C.S(_)
            r2_tok = C.tok(eS)

            def _(t, j=j, rt=r2_tok):
                C.wait(t, rt)
                mm = t.matmul(out=pSC[:, 2 * TQ:3 * TQ], lhsT=cst[0:1, CS["ones_row"]:CS["ones_row"] + 128],
                              rhs=rowsM[0:1, T:T + TQ], start=True, stop=True)
                C.inc(eT, mm)
            C.T(_)
            r2bc_tok = C.tok(eT)

            def _(v, j=j, rt=r2bc_tok):
                C.wait(v, rt)
                tk = None
                for cc in range(CCH):
                    nw = cvec[:, CV["normw"] + cc:CV["normw"] + cc + 1]
                    tk = v.scalar_tensor_tensor(out=p2bf[:, XGOFF + cc * TQ:XGOFF + (cc + 1) * TQ],
                                                in0=pool1[:, cc * TQ:(cc + 1) * TQ], scalar=nw,
                                                in1=pSC[:, 2 * TQ:3 * TQ], op0=ALU.mult, op1=ALU.mult)
                C.inc(eV, tk)
            C.V(_)
            xg_tok = C.tok(eV)

            qe_base = C.val(eV)
            baseT = C.val(eT)
            def _(t, j=j, xg_tok=xg_tok, w_tok=w_tok, qe_base=qe_base):
                C.wait(t, xg_tok)
                if j == 0:
                    C.wait(t, w_tok)
                nmm = 0
                for dc in range(CCH):
                    if nmm >= 2:
                        C.wait(t, (eV, qe_base + nmm - 1))
                    pp = pA if nmm % 2 == 0 else pC
                    for cc in range(CCH):
                        mm = t.matmul(out=pp[:, 0:TQ],
                                      lhsT=p4bf[:, cc * 1024 + dc * 128: cc * 1024 + (dc + 1) * 128],
                                      rhs=p2bf[:, XGOFF + cc * TQ:XGOFF + (cc + 1) * TQ],
                                      start=(cc == 0), stop=(cc == CCH - 1))
                    C.inc(eT, mm)
                    nmm += 1
                for dc in range(2):
                    if nmm >= 2:
                        C.wait(t, (eV, qe_base + nmm - 1))
                    pp = pA if nmm % 2 == 0 else pC
                    for cc in range(CCH):
                        mm = t.matmul(out=pp[:, 0:TQ],
                                      lhsT=p2bf[:, 6144 + cc * 256 + dc * 128: 6144 + cc * 256 + (dc + 1) * 128],
                                      rhs=p2bf[:, XGOFF + cc * TQ:XGOFF + (cc + 1) * TQ],
                                      start=(cc == 0), stop=(cc == CCH - 1))
                    C.inc(eT, mm)
                    nmm += 1
                for dc in range(2):
                    if nmm >= 2:
                        C.wait(t, (eV, qe_base + nmm - 1))
                    pp = pA if nmm % 2 == 0 else pC
                    for cc in range(CCH):
                        mm = t.matmul(out=pp[:, 0:TQ],
                                      lhsT=p2bf[:, 8192 + cc * 256 + dc * 128: 8192 + cc * 256 + (dc + 1) * 128],
                                      rhs=p2bf[:, XGOFF + cc * TQ:XGOFF + (cc + 1) * TQ],
                                      start=(cc == 0), stop=(cc == CCH - 1))
                    C.inc(eT, mm)
                    nmm += 1
            C.T(_)

            def _(v, j=j, baseT=baseT):
                nmm = 0
                for dc in range(CCH):
                    C.wait(v, (eT, baseT + nmm + 1))
                    pp = pA if nmm % 2 == 0 else pC
                    C.inc(eV, v.tensor_tensor(out=p3bf[:, dc * T + j * TQ:dc * T + (j + 1) * TQ],
                                              in0=pp[:, 0:TQ],
                                              in1=pool1[:, RFOFF + j * TQ:RFOFF + (j + 1) * TQ],
                                              op=ALU.mult))
                    nmm += 1
                for dc in range(2):
                    C.wait(v, (eT, baseT + nmm + 1))
                    pp = pA if nmm % 2 == 0 else pC
                    C.inc(eV, v.tensor_tensor(out=p1bf[:, KOFF + dc * T + j * TQ:KOFF + dc * T + (j + 1) * TQ],
                                              in0=pp[:, 0:TQ],
                                              in1=pool1[:, RFOFF + j * TQ:RFOFF + (j + 1) * TQ],
                                              op=ALU.mult))
                    nmm += 1
                for dc in range(2):
                    C.wait(v, (eT, baseT + nmm + 1))
                    pp = pA if nmm % 2 == 0 else pC
                    C.inc(eV, v.tensor_copy(out=p2bf[:, VBOFF + dc * T + j * TQ:VBOFF + dc * T + (j + 1) * TQ],
                                            in_=pp[:, 0:TQ]))
                    nmm += 1
            C.V(_)
        proj_tok = C.tok(eV)
        if _ATT_UPTO == 1:
            return [proj_tok]

        KSW = 4608
        VT = 0
        wotok_box = []
        def _(y):
            C.wait(y, cast_tok["bf_l0a_wo"])
            C.wait(y, proj_tok)
            b = C.dma_batch(y)
            wotok_box.append(C.qinc(b, y.dma_start(out=p4bf[:, 0:8192].rearrange("p (a d) -> p a d", a=CCH),
                                                   in_=dram["bf_l0a_wo"].rearrange("(a p) d -> p a d", p=128))))
        C.Y(_)
        wo_tok = wotok_box[0]

        if _ATT_UPTO == 15:
            ksw_tok = vt_tok = None
        def _(v):
            C.wait(v, proj_tok)
            for c in range(2):
                v.tensor_copy(out=p2bf[0:64, KSW + c * T:KSW + (c + 1) * T],
                              in_=p1bf[64:128, KOFF + c * T:KOFF + (c + 1) * T])
                v.tensor_copy(out=p2bf[64:128, KSW + c * T:KSW + (c + 1) * T],
                              in_=p1bf[0:64, KOFF + c * T:KOFF + (c + 1) * T])
            for tt in range(NTT):
                for kv4 in range(4):
                    o = VT + tt * 260 + kv4 * 65 + 64
                    tk = v.memset(p2bf[:, o:o + 1], 1.0)
            C.inc(eV, tk)
        if _ATT_UPTO not in (15, 17):
            C.V(_)
            ksw_tok = C.tok(eV)
        elif _ATT_UPTO == 17:
            ksw_tok = proj_tok

        tb = C.val(eT)
        ev_base = C.val(eS)
        def _(t, ksw=ksw_tok, ev_base=ev_base):
            C.wait(t, ksw)
            n = 0
            for dc in range(2):
                for tt in range(NTT):
                    if n >= 2:
                        C.wait(t, (eS, ev_base + n - 1))
                    pp = pA if n % 2 == 0 else pC
                    C.inc(eT, t.matmul(out=pp[:, 0:128],
                                       lhsT=p2bf[:, VBOFF + dc * T + tt * 128:VBOFF + dc * T + (tt + 1) * 128],
                                       rhs=ident_bf[:], start=True, stop=True))
                    n += 1
        if _ATT_UPTO not in (15, 16):
            C.T(_)
        def _(s, tb=tb):
            n = 0
            for dc in range(2):
                for tt in range(NTT):
                    C.wait(s, (eT, tb + n + 1))
                    pp = pA if n % 2 == 0 else pC
                    s.activation(out=p2bf[:, VT + tt * 260 + (2 * dc) * 65:VT + tt * 260 + (2 * dc) * 65 + 64],
                                 in_=pp[:, 0:64], func=AF.Copy)
                    C.inc(eS, s.activation(out=p2bf[:, VT + tt * 260 + (2 * dc + 1) * 65:VT + tt * 260 + (2 * dc + 1) * 65 + 64],
                                           in_=pp[:, 64:128], func=AF.Copy))
                    n += 1
        if _ATT_UPTO not in (15, 16):
            C.S(_)
            vt_tok = C.tok(eS)
        elif _ATT_UPTO == 16:
            vt_tok = ksw_tok

        QMAX = 16
        KMAX = 33
        KM16 = 40
        MROW = 56
        sqs_base = C.val(eS)
        redv_base = C.val(eV)
        qmT = C.val(eT)
        ev_tok = {}
        def _(s, pt=proj_tok, qmT=qmT, redv_base=redv_base):
            C.wait(s, pt)
            n = 0
            for cc in range(CCH + 2):
                for fc in range(NFC):
                    if n >= 2:
                        C.wait(s, (eT, qmT + (n - 1) * 2))
                    src = (p3bf[:, cc * T + fc * TQ:cc * T + (fc + 1) * TQ] if cc < CCH else
                           p1bf[:, KOFF + (cc - CCH) * T + fc * TQ:KOFF + (cc - CCH) * T + (fc + 1) * TQ])
                    C.inc(eS, s.activation(out=p1bf[:, (n % 2) * TQ:(n % 2) * TQ + TQ], in_=src, func=AF.Square))
                    n += 1
                C.wait(s, (eT, qmT + 2 * NFC * (cc + 1)))
                if cc >= 1:
                    C.wait(s, (eV, redv_base + 2 * cc))
                s.copy(out=rowsM[0:1, 0:T], in_=pSC[0:1, 0:T])
                ev_tok[cc] = C.inc(eS, s.copy(out=rowsM[32:33, 0:T], in_=pSC[32:33, 0:T]), 1)
        C.S(_)
        def _(t, sqs_base=sqs_base):
            n = 0
            for cc in range(CCH + 2):
                for fc in range(NFC):
                    C.wait(t, (eS, sqs_base + n + 1 + cc))
                    if fc == 0 and cc >= 1:
                        C.wait(t, ev_tok[cc - 1])
                    t.matmul(out=pSC[0:1, fc * TQ:(fc + 1) * TQ], lhsT=onespair[:, 0:1],
                             rhs=p1bf[:, (n % 2) * TQ:(n % 2) * TQ + TQ], start=True, stop=True)
                    mm = t.matmul(out=pSC[32:33, fc * TQ:(fc + 1) * TQ], lhsT=onespair[:, 4:5],
                                  rhs=p1bf[:, (n % 2) * TQ:(n % 2) * TQ + TQ], start=True, stop=True)
                    C.inc(eT, mm, 2)
                    n += 1
        C.T(_)
        def _(v):
            for cc in range(CCH + 2):
                C.wait(v, ev_tok[cc])
                col = QMAX + 2 * cc if cc < CCH else KMAX + 2 * (cc - CCH)
                v.tensor_reduce(out=stat[0:1, col:col + 1], in_=rowsM[0:1, 0:T],
                                axis=mybir.AxisListType.X, op=ALU.max)
                C.inc(eV, v.tensor_reduce(out=stat[0:1, col + 1:col + 2], in_=rowsM[32:33, 0:T],
                                          axis=mybir.AxisListType.X, op=ALU.max), 2)
        C.V(_)
        red_tok = C.tok(eV)

        def _(v, rt=red_tok):
            C.wait(v, rt)
            for h in range(NH):
                t = v.tensor_copy(out=stat[0:1, KM16 + h:KM16 + h + 1],
                                  in_=stat[0:1, KMAX + h % 4:KMAX + h % 4 + 1])
            t = C.inc(eV, t)
            C.wait(v, t)
            t = C.inc(eV, v.tensor_tensor(out=stat[0:1, MROW:MROW + 16], in0=stat[0:1, QMAX:QMAX + 16],
                                          in1=stat[0:1, KM16:KM16 + 16], op=ALU.add))
            C.wait(v, t)
            t = C.inc(eV, v.tensor_scalar(out=stat[0:1, MROW:MROW + 16], in0=stat[0:1, MROW:MROW + 16],
                                          scalar1=1.0 / 16, scalar2=MHAT_CLAMP, op0=ALU.mult, op1=ALU.min))
            C.wait(v, t)
            t = C.inc(eV, v.tensor_scalar(out=stat[0:1, MROW:MROW + 16], in0=stat[0:1, MROW:MROW + 16],
                                          scalar1=-1.0, scalar2=None, op0=ALU.mult))
        C.V(_)
        mrow_tok = C.tok(eV)

        def _(t, mt=mrow_tok):
            C.wait(t, mt)
            C.inc(eT, t.matmul(out=pB[:, 0:16], lhsT=cst[0:1, CS["ones_row"]:CS["ones_row"] + 128],
                               rhs=stat[0:1, MROW:MROW + 16], start=True, stop=True))
        C.T(_)
        mc_T = C.tok(eT)
        def _(s, mt=mc_T):
            C.wait(s, mt)
            C.inc(eS, s.copy(out=cvec[:, CV["mcols"]:CV["mcols"] + NH], in_=pB[:, 0:16]))
        C.S(_)
        mcols_tok = C.tok(eS)
        if _ATT_UPTO in (2, 16, 17):
            return [mcols_tok, vt_tok, ksw_tok]
        if _ATT_UPTO == 15:
            return [mcols_tok]

        pOs = [pA, pB, pC, pD]
        PT0 = 0
        OS0 = 2048
        LR0 = 3072
        NG = (NTT + 3) // 4

        def glast(g):
            return min(4 * g + 3, NTT - 1)

        scores_idx = {}
        pv_idx = {}
        lbc_idx = {}
        n = 0
        for tk in range(NTT):
            n += 1; scores_idx[tk] = n
            n += 1; pv_idx[tk] = n
            g = tk // 4
            if tk == glast(g):
                n += 1; lbc_idx[g] = n
        exp_idx = {}
        evac_idx = {}
        m = 0
        for tk in range(NTT):
            m += 1; exp_idx[tk] = m
            g = tk // 4
            if tk == glast(g):
                evac_idx[g] = m + 3
                m += 3

        for h in range(NH):
            kv = h % 4
            qc = h // 2
            qrow = (h % 2) * 64
            aligned = (h % 2) == (kv % 2)
            if aligned:
                koff, krow = KOFF + (kv // 2) * T, (kv % 2) * 64
                ksrc = p1bf
            else:
                koff, krow = KSW + (kv // 2) * T, (h % 2) * 64
                ksrc = p2bf
            Sbase = C.val(eS)
            Tbase = C.val(eT)
            Vbase = C.val(eV)

            def _(t, h=h, koff=koff, krow=krow, qc=qc, qrow=qrow, ksrc=ksrc,
                  Sbase=Sbase, Tbase=Tbase, Vbase=Vbase):
                if h == 0:
                    C.wait(t, vt_tok, mcols_tok)
                for tk in range(NTT):
                    qj0 = (tk * 128) // TQ
                    if tk > 0:
                        C.wait(t, (eS, Sbase + exp_idx[tk - 1]))
                    for qj in range(qj0, NFC):
                        diag_in_chunk = (qj == qj0)
                        mm = t.matmul(out=pSC[:, qj * TQ:(qj + 1) * TQ],
                                      lhsT=ksrc[krow:krow + 64, koff + tk * 128:koff + (tk + 1) * 128],
                                      rhs=p3bf[qrow:qrow + 64, qc * T + qj * TQ:qc * T + (qj + 1) * TQ],
                                      start=True, stop=not diag_in_chunk)
                        if diag_in_chunk:
                            mm = t.matmul(out=pSC[:, tk * 128:(tk + 1) * 128],
                                          lhsT=ident_bf[:], rhs=mask_bf[:],
                                          start=False, stop=True)
                    C.inc(eT, mm)
                    C.wait(t, (eS, Sbase + exp_idx[tk]))
                    if tk == 0:
                        C.wait(t, (eV, Vbase))
                    first_bank = set()
                    for qt in range(tk, NTT):
                        g = qt // 4
                        st = (tk == 0) and (g not in first_bank)
                        first_bank.add(g)
                        sp = (tk == glast(g)) and (qt == glast(g))
                        mm = t.matmul(out=pOs[g][0:65, (qt % 4) * 128:(qt % 4) * 128 + 128],
                                      lhsT=p2bf[:, VT + tk * 260 + kv * 65:VT + tk * 260 + (kv + 1) * 65],
                                      rhs=p1bf[:, PT0 + (tk % 2) * 2048 + qt * 128:PT0 + (tk % 2) * 2048 + (qt + 1) * 128],
                                      start=st, stop=sp)
                    C.inc(eT, mm)
                    g = tk // 4
                    if tk == glast(g):
                        C.wait(t, (eS, Sbase + evac_idx[g]))
                        gw = min(4, NTT - 4 * g) * 128
                        mm = t.matmul(out=pOs[g][0:64, 0:gw], lhsT=cst[0:1, CS["ones_row"]:CS["ones_row"] + 64],
                                      rhs=pool1[0:1, LR0 + (g % 2) * TQ:LR0 + (g % 2) * TQ + gw],
                                      start=True, stop=True)
                        C.inc(eT, mm)
            C.T(_)

            def _(s, h=h, Sbase=Sbase, Tbase=Tbase, Vbase=Vbase):
                for tk in range(NTT):
                    qj0 = (tk * 128) // TQ
                    C.wait(s, (eT, Tbase + scores_idx[tk]))
                    C.inc(eS, s.activation(out=p1bf[:, PT0 + (tk % 2) * 2048 + qj0 * TQ:PT0 + (tk % 2) * 2048 + T],
                                           in_=pSC[:, qj0 * TQ:T], func=AF.Exp, scale=0.125,
                                           bias=cvec[:, CV["mcols"] + h:CV["mcols"] + h + 1]))
                    g = tk // 4
                    if tk == glast(g):
                        gw = min(4, NTT - 4 * g) * 128
                        C.wait(s, (eT, Tbase + pv_idx[tk]))
                        if g >= 2:
                            C.wait(s, (eV, Vbase + g - 1))
                        C.inc(eS, s.copy(out=pool1[:64, OS0 + (g % 2) * TQ:OS0 + (g % 2) * TQ + gw],
                                         in_=pOs[g][0:64, 0:gw]))
                        tkn = C.inc(eS, s.activation(out=pool1[0:1, LR0 + (g % 2) * TQ:LR0 + (g % 2) * TQ + gw],
                                                     in_=pOs[g][64:65, 0:gw], func=AF.Ln))
                        C.wait(s, tkn)
                        C.inc(eS, s.activation(out=pool1[0:1, LR0 + (g % 2) * TQ:LR0 + (g % 2) * TQ + gw],
                                               in_=pool1[0:1, LR0 + (g % 2) * TQ:LR0 + (g % 2) * TQ + gw],
                                               func=AF.Exp, scale=-1.0))
            C.S(_)

            def _(v, h=h, qc=qc, qrow=qrow, Sbase=Sbase, Tbase=Tbase):
                for g in range(NG):
                    gw = min(4, NTT - 4 * g) * 128
                    C.wait(v, (eT, Tbase + lbc_idx[g]))
                    C.inc(eV, v.tensor_tensor(out=p3bf[qrow:qrow + 64, qc * T + g * TQ:qc * T + g * TQ + gw],
                                              in0=pool1[0:64, OS0 + (g % 2) * TQ:OS0 + (g % 2) * TQ + gw],
                                              in1=pOs[g][0:64, 0:gw], op=ALU.mult))
            C.V(_)
        heads_tok = C.tok(eV)
        if _ATT_UPTO == 3:
            return [heads_tok, wo_tok]

        baseT4 = C.val(eT)
        v4_base = C.val(eV)
        def _(t, ht=heads_tok, wt=wo_tok, v4_base=v4_base):
            C.wait(t, ht, wt)
            n = 0
            for dc in range(CCH):
                for fc in range(NFC):
                    if n >= 2:
                        C.wait(t, (eV, v4_base + n - 1))
                    pp = pA if n % 2 == 0 else pB
                    for cc in range(CCH):
                        mm = t.matmul(out=pp[:, 0:TQ],
                                      lhsT=p4bf[:, cc * 1024 + dc * 128: cc * 1024 + (dc + 1) * 128],
                                      rhs=p3bf[:, cc * T + fc * TQ:cc * T + (fc + 1) * TQ],
                                      start=(cc == 0), stop=(cc == CCH - 1))
                    C.inc(eT, mm)
                    n += 1
        C.T(_)
        def _(v, baseT4=baseT4):
            n = 0
            for dc in range(CCH):
                for fc in range(NFC):
                    C.wait(v, (eT, baseT4 + n + 1))
                    pp = pA if n % 2 == 0 else pB
                    xs = x_sb[:, dc * T + fc * TQ: dc * T + (fc + 1) * TQ]
                    C.inc(eV, v.scalar_tensor_tensor(out=xs, in0=pp[:, 0:TQ], scalar=1.0, in1=xs,
                                                     op0=ALU.mult, op1=ALU.add))
                    n += 1
        C.V(_)
        return [C.tok(eV)]

    def moe(seq, prev_toks):
        NTT = T // 128
        IDF = 4096 + 8 * NTT + 128
        ident_f = pool1[:, IDF:IDF + 128]
        stats_tok = ln_stats(prev_toks)

        prev_b = C.tok(eV)
        for j in range(NFC):
            def _(t, j=j, pb=prev_b):
                C.wait(t, pb)
                ln_bcast(t, j, stats_tok)
            C.T(_)
            bc_tok = C.tok(eT)
            def _(v, j=j, bc=bc_tok):
                C.wait(v, bc)
                tk = None
                for cc in range(CCH):
                    xs = x_sb[:, cc * T + j * TQ: cc * T + (j + 1) * TQ]
                    tk = v.tensor_tensor(out=pool1[:, cc * TQ:(cc + 1) * TQ], in0=xs, in1=pSC[:, 0:TQ], op=ALU.subtract)
                C.wait(v, C.inc(eV, tk))
                for cc in range(CCH):
                    tk = v.tensor_tensor(out=pool1[:, cc * TQ:(cc + 1) * TQ],
                                         in0=pool1[:, cc * TQ:(cc + 1) * TQ],
                                         in1=pSC[:, TQ:2 * TQ], op=ALU.mult)
                C.inc(eV, tk)
            C.V(_)
            t2a = C.tok(eV)
            def _(s, j=j, ta=t2a):
                C.wait(s, ta)
                tk = None
                for cc in range(CCH):
                    tk = s.activation(out=p2bf[:, cc * T + j * TQ:cc * T + (j + 1) * TQ],
                                      in_=pool1[:, cc * TQ:(cc + 1) * TQ], func=AF.Copy)
                C.inc(eS, tk)
            C.S(_)
            t2b = C.tok(eS)
            def _(v, j=j, tb=t2b):
                C.wait(v, tb)
                tk = None
                for cc in range(CCH):
                    tk = v.tensor_tensor(out=p3bf[:, cc * T + j * TQ:cc * T + (j + 1) * TQ],
                                         in0=pool1[:, cc * TQ:(cc + 1) * TQ],
                                         in1=p2bf[:, cc * T + j * TQ:cc * T + (j + 1) * TQ],
                                         op=ALU.subtract)
                C.inc(eV, tk)
            C.V(_)
            prev_b = C.tok(eV)
        xn_tok = C.tok(eV)

        G05 = 4096
        SCR = 4096 + 8 * NTT
        L_, P_, I1_, PM_, I2_, G_ = SCR, SCR + 8, SCR + 16, SCR + 24, SCR + 32, SCR + 40
        (MX, SM, RS, M1, M2, S12, SD, RSD, SW, AC, TT_, F1, T2, RAC) = [SCR + 48 + k for k in range(14)]
        V_PER = 20
        S_PER = 7

        lgT = C.val(eT)
        vbase = C.val(eV)
        sbase = C.val(eS)

        def _(t, xt=xn_tok, vbase=vbase):
            C.wait(t, xt, tok_rw)
            for tt in range(NTT):
                pp = pA if tt % 2 == 0 else pB
                if tt >= 2:
                    C.wait(t, (eV, vbase + V_PER * (tt - 2) + 1))
                for cc in range(CCH):
                    hi = p2bf[:, cc * T + tt * 128:cc * T + (tt + 1) * 128]
                    lo = p3bf[:, cc * T + tt * 128:cc * T + (tt + 1) * 128]
                    t.matmul(out=pp[:, 0:NEXP], lhsT=hi, rhs=rw_sb[:, cc * 8:(cc + 1) * 8],
                             start=(cc == 0), stop=False)
                    t.matmul(out=pp[:, 0:NEXP], lhsT=hi, rhs=rw_lo[:, cc * 8:(cc + 1) * 8],
                             start=False, stop=False)
                    t.matmul(out=pp[:, 0:NEXP], lhsT=lo, rhs=rw_sb[:, cc * 8:(cc + 1) * 8],
                             start=False, stop=False)
                    mm = t.matmul(out=pp[:, 0:NEXP], lhsT=lo, rhs=rw_lo[:, cc * 8:(cc + 1) * 8],
                                  start=False, stop=(cc == CCH - 1))
                C.inc(eT, mm)
        C.T(_)

        def _(s, vbase=vbase):
            p1 = pool1
            for tt in range(NTT):
                C.wait(s, (eV, vbase + V_PER * tt + 3))
                tkn = C.inc(eS, s.activation(out=p1[:, P_:P_ + 8], in_=p1[:, L_:L_ + 8], func=AF.Exp,
                                             bias=p1[:, MX:MX + 1], accum_out=p1[:, SM:SM + 1]))
                C.wait(s, tkn)
                tkn = C.inc(eS, s.activation(out=p1[:, RS:RS + 1], in_=p1[:, SM:SM + 1], func=AF.Ln))
                C.wait(s, tkn)
                C.inc(eS, s.activation(out=p1[:, RS:RS + 1], in_=p1[:, RS:RS + 1], func=AF.Exp, scale=-1.0))
                C.wait(s, (eV, vbase + V_PER * tt + 10))
                tkn = C.inc(eS, s.activation(out=p1[:, RSD:RSD + 1], in_=p1[:, SD:SD + 1], func=AF.Ln))
                C.wait(s, tkn)
                C.inc(eS, s.activation(out=p1[:, RSD:RSD + 1], in_=p1[:, RSD:RSD + 1], func=AF.Exp, scale=-1.0))
                C.wait(s, (eV, vbase + V_PER * tt + 15))
                tkn = C.inc(eS, s.activation(out=p1[:, RAC:RAC + 1], in_=p1[:, AC:AC + 1], func=AF.Ln))
                C.wait(s, tkn)
                C.inc(eS, s.activation(out=p1[:, RAC:RAC + 1], in_=p1[:, RAC:RAC + 1], func=AF.Exp, scale=-1.0))
        C.S(_)

        def _(v, lgT=lgT, sbase=sbase):
            p1 = pool1
            for tt in range(NTT):
                def ed(instr):
                    C.wait(v, C.inc(eV, instr))
                C.wait(v, (eT, lgT + tt + 1))
                C.wait(v, (eS, sbase + S_PER * tt))
                pp = pA if tt % 2 == 0 else pB
                ed(v.tensor_tensor(out=p1[:, L_:L_ + 8], in0=pp[:, 0:NEXP],
                                   in1=cst[:, CS["rbbc"]:CS["rbbc"] + NEXP], op=ALU.add))
                ed(v.tensor_reduce(out=p1[:, MX:MX + 1], in_=p1[:, L_:L_ + 8],
                                   axis=mybir.AxisListType.X, op=ALU.max))
                ed(v.tensor_scalar(out=p1[:, MX:MX + 1], in0=p1[:, MX:MX + 1],
                                   scalar1=-1.0, scalar2=None, op0=ALU.mult))
                C.wait(v, (eS, sbase + S_PER * tt + 3))
                ed(v.tensor_scalar(out=p1[:, P_:P_ + 8], in0=p1[:, P_:P_ + 8],
                                   scalar1=p1[:, RS:RS + 1], scalar2=None, op0=ALU.mult))
                ed(v.tensor_reduce(out=p1[:, M1:M1 + 1], in_=p1[:, P_:P_ + 8],
                                   axis=mybir.AxisListType.X, op=ALU.max))
                ed(v.tensor_scalar(out=p1[:, I1_:I1_ + 8], in0=p1[:, P_:P_ + 8],
                                   scalar1=p1[:, M1:M1 + 1], scalar2=None, op0=ALU.is_ge))
                ed(v.scalar_tensor_tensor(out=p1[:, PM_:PM_ + 8], in0=p1[:, I1_:I1_ + 8],
                                          scalar=-1e30, in1=p1[:, P_:P_ + 8],
                                          op0=ALU.mult, op1=ALU.add))
                ed(v.tensor_reduce(out=p1[:, M2:M2 + 1], in_=p1[:, PM_:PM_ + 8],
                                   axis=mybir.AxisListType.X, op=ALU.max))
                ed(v.tensor_tensor(out=p1[:, S12:S12 + 1], in0=p1[:, M1:M1 + 1],
                                   in1=p1[:, M2:M2 + 1], op=ALU.add))
                ed(v.tensor_scalar(out=p1[:, SD:SD + 1], in0=p1[:, S12:S12 + 1],
                                   scalar1=EPS_MOE, scalar2=None, op0=ALU.add))
                ed(v.tensor_scalar(out=p1[:, I2_:I2_ + 8], in0=p1[:, PM_:PM_ + 8],
                                   scalar1=p1[:, M2:M2 + 1], scalar2=None, op0=ALU.is_ge))
                ed(v.tensor_tensor(out=p1[:, I1_:I1_ + 8], in0=p1[:, I1_:I1_ + 8],
                                   in1=p1[:, I2_:I2_ + 8], op=ALU.add))
                ed(v.tensor_tensor(out=p1[:, G_:G_ + 8], in0=p1[:, I1_:I1_ + 8],
                                   in1=p1[:, P_:P_ + 8], op=ALU.mult))
                C.wait(v, (eS, sbase + S_PER * tt + 5))
                ed(v.tensor_tensor(out=p1[:, SW:SW + 1], in0=p1[:, S12:S12 + 1],
                                   in1=p1[:, RSD:RSD + 1], op=ALU.mult))
                ed(v.tensor_scalar(out=p1[:, AC:AC + 1], in0=p1[:, SW:SW + 1],
                                   scalar1=EPS_MOE, scalar2=None, op0=ALU.max))
                ed(v.tensor_scalar(out=p1[:, TT_:TT_ + 1], in0=p1[:, SW:SW + 1],
                                   scalar1=EPS_MOE, scalar2=1.0, op0=ALU.mult, op1=ALU.add))
                C.wait(v, (eS, sbase + S_PER * tt + 7))
                ed(v.tensor_tensor(out=p1[:, F1:F1 + 1], in0=p1[:, TT_:TT_ + 1],
                                   in1=p1[:, RAC:RAC + 1], op=ALU.mult))
                ed(v.tensor_scalar(out=p1[:, F1:F1 + 1], in0=p1[:, F1:F1 + 1],
                                   scalar1=0.5, scalar2=None, op0=ALU.mult))
                ed(v.tensor_tensor(out=p1[:, T2:T2 + 1], in0=p1[:, RSD:RSD + 1],
                                   in1=p1[:, F1:F1 + 1], op=ALU.mult))
                ed(v.tensor_scalar(out=p1[:, G05 + tt * 8:G05 + (tt + 1) * 8], in0=p1[:, G_:G_ + 8],
                                   scalar1=p1[:, T2:T2 + 1], scalar2=None, op0=ALU.mult))
        C.V(_)
        gate_tok = C.tok(eV)

        last_tok = gate_tok
        for half in range(2):
            NH2 = NTT // 2
            tt0 = half * NH2
            mmT_guard = {}
            vcomb0 = None
            for e in range(NEXP):
                box = []
                def _(y, e=e, half=half, lt=last_tok):
                    C.wait(y, cast_tok["bf_l1m_ew"])
                    if e < 2:
                        C.wait(y, xn_tok)
                        C.wait(y, lt)
                    else:
                        C.wait(y, (eT, mmT_guard[e - 2]))
                    b = C.dma_batch(y)
                    if half == 0 and e == 0:
                        C.qinc(b, y.dma_start(out=ident_f, in_=dram["ident_f"]))
                    dst = p4bf[:, 0:8192] if e % 2 == 0 else p1bf[:, 0:8192]
                    box.append(C.qinc(b, y.dma_start(out=dst.rearrange("p (a d) -> p a d", a=CCH),
                                                     in_=dram["bf_l1m_ew"][e].rearrange("(a p) d -> p a d", p=128))))
                C.Y(_)
                wt = box[0]

                if vcomb0 is None:
                    vcomb0 = C.val(eV)
                baseT = C.val(eT)
                def _(t, e=e, half=half, wt=wt, vcomb0=vcomb0, tt0=tt0, NH2=NH2):
                    C.wait(t, wt)
                    wsrc = p4bf if e % 2 == 0 else p1bf
                    for i in range(NH2):
                        tt = tt0 + i
                        n = e * NH2 + i
                        if n >= 2:
                            C.wait(t, (eV, vcomb0 + n - 1))
                        pp0 = pA if n % 2 == 0 else pC
                        pp1 = pB if n % 2 == 0 else pD
                        for cc in range(CCH):
                            t.matmul(out=pp0[:, 0:512],
                                     lhsT=p2bf[:, cc * T + tt * 128:cc * T + (tt + 1) * 128],
                                     rhs=wsrc[:, cc * 1024:cc * 1024 + 512],
                                     start=(cc == 0), stop=(cc == CCH - 1))
                            mm = t.matmul(out=pp1[:, 0:512],
                                          lhsT=p2bf[:, cc * T + tt * 128:cc * T + (tt + 1) * 128],
                                          rhs=wsrc[:, cc * 1024 + 512:cc * 1024 + 1024],
                                          start=(cc == 0), stop=(cc == CCH - 1))
                        C.inc(eT, mm)
                C.T(_)
                mmT_guard[e] = C.val(eT)

                def _(v, e=e, baseT=baseT, tt0=tt0, NH2=NH2, vcomb0=vcomb0):
                    for i in range(NH2):
                        tt = tt0 + i
                        C.wait(v, (eT, baseT + i + 1))
                        if e > 0:
                            C.wait(v, (eV, vcomb0 + (e - 1) * NH2 + i + 1))
                        n = e * NH2 + i
                        pp0 = pA if n % 2 == 0 else pC
                        pp1 = pB if n % 2 == 0 else pD
                        gcol = pool1[:, G05 + tt * 8 + e:G05 + tt * 8 + e + 1]
                        do = i * 1024
                        if e == 0:
                            v.tensor_scalar(out=pool3[:, do:do + 512], in0=pp0[:, 0:512],
                                            scalar1=gcol, scalar2=None, op0=ALU.mult)
                            mm = v.tensor_scalar(out=pool3[:, do + 512:do + 1024], in0=pp1[:, 0:512],
                                                 scalar1=gcol, scalar2=None, op0=ALU.mult)
                        else:
                            v.scalar_tensor_tensor(out=pool3[:, do:do + 512], in0=pp0[:, 0:512],
                                                   scalar=gcol, in1=pool3[:, do:do + 512],
                                                   op0=ALU.mult, op1=ALU.add)
                            mm = v.scalar_tensor_tensor(out=pool3[:, do + 512:do + 1024], in0=pp1[:, 0:512],
                                                        scalar=gcol, in1=pool3[:, do + 512:do + 1024],
                                                        op0=ALU.mult, op1=ALU.add)
                        C.inc(eV, mm)
                C.V(_)
            comb_tok = C.tok(eV)

            trT = C.val(eT)
            trV = C.val(eV)
            def _(t, ct=comb_tok, trV=trV, NH2=NH2):
                C.wait(t, ct)
                n = 0
                for i in range(NH2):
                    for blk in range(CCH):
                        if n >= 2:
                            C.wait(t, (eV, trV + n - 1))
                        po = (n % 2) * 512
                        C.inc(eT, t.matmul(out=pSC[:, po:po + 128],
                                           lhsT=pool3[:, i * 1024 + blk * 128:i * 1024 + (blk + 1) * 128],
                                           rhs=ident_f[:], start=True, stop=True))
                        n += 1
            C.T(_)
            def _(v, trT=trT, tt0=tt0, NH2=NH2):
                n = 0
                for i in range(NH2):
                    tt = tt0 + i
                    for blk in range(CCH):
                        C.wait(v, (eT, trT + n + 1))
                        po = (n % 2) * 512
                        xs = x_sb[:, blk * T + tt * 128:blk * T + (tt + 1) * 128]
                        C.inc(eV, v.scalar_tensor_tensor(out=xs, in0=pSC[:, po:po + 128],
                                                         scalar=1.0, in1=xs, op0=ALU.mult, op1=ALU.add))
                        n += 1
            C.V(_)
            last_tok = C.tok(eV)
        return [last_tok]

    def mamba(seq, prev_toks):
        TQ5 = min(512, T)
        TC5 = T // TQ5
        cvbf = cvec.bitcast(BF16)
        XPOFF = 1536
        DTOFF = 3072
        HCAR = CV["hcarry"]
        xsbbf = x_sb.bitcast(BF16)
        DUO = 0
        YO = 16 * TQ5
        UBF = 0
        XMS = 0
        CVS = 2 * (TQ5 + 3)
        SRS = CVS + 2 * TQ5
        BCO = SRS + TQ5
        CCO = BCO + TQ5
        DT2 = 2 * (CCO + TQ5)
        MB = 0

        stats_tok = ln_stats(prev_toks)

        prev_b = C.tok(eV)
        for j in range(NFC):
            def _(t, j=j, pb=prev_b):
                C.wait(t, pb)
                ln_bcast(t, j, stats_tok)
            C.T(_)
            bc_tok = C.tok(eT)
            def _(v, j=j, bc=bc_tok):
                C.wait(v, bc)
                tk = None
                for cc in range(CCH):
                    xs = x_sb[:, cc * T + j * TQ: cc * T + (j + 1) * TQ]
                    tk = v.tensor_tensor(out=pool1[:, cc * TQ:(cc + 1) * TQ], in0=xs, in1=pSC[:, 0:TQ], op=ALU.subtract)
                C.wait(v, C.inc(eV, tk))
                for cc in range(CCH):
                    tk = v.tensor_tensor(out=p2bf[:, cc * T + j * TQ:cc * T + (j + 1) * TQ],
                                         in0=pool1[:, cc * TQ:(cc + 1) * TQ],
                                         in1=pSC[:, TQ:2 * TQ], op=ALU.mult)
                C.inc(eV, tk)
            C.V(_)
            prev_b = C.tok(eV)
        xn_tok = C.tok(eV)

        spill_box = []
        def _(y):
            C.wait(y, xn_tok)
            C.wait(y, cast_tok["bf_l1a_xproj_t"], cast_tok["bf_l1a_dtproj_t"])
            b = C.dma_batch(y)
            C.qinc(b, y.dma_start(out=spill_x.rearrange("a p t -> p a t"),
                                  in_=x_sb[:, 0:CCH * T].rearrange("p (a t) -> p a t", a=CCH)))
            C.qinc(b, y.dma_start(out=cvbf[:, XPOFF:XPOFF + 16 * 96].rearrange("p (a d) -> p a d", a=16),
                                  in_=dram["bf_l1a_xproj_t"].rearrange("(a p) d -> p a d", p=128)))
            spill_box.append(C.qinc(b, y.dma_start(
                out=cvbf[:, DTOFF:DTOFF + 1024],
                in_=dram["bf_l1a_dtproj_t"])))
        C.Y(_)
        spill_tok = spill_box[0]

        def _(v, st=spill_tok):
            C.wait(v, st)
            v.memset(cvec[:, HCAR:HCAR + 256], 0.0)
            C.inc(eV, v.memset(cvec[:, CV["convtail"]:CV["convtail"] + 48], 0.0))
        C.V(_)
        init_tok = C.tok(eV)

        phase_tok = init_tok
        a_vg = {}
        for tc in range(TC5):
            ts0 = tc * TQ5
            ld_toks = []
            for dc in range(DCH):
                box = []
                def _(y, dc=dc, tc=tc, pt=phase_tok):
                    if dc < 2:
                        C.wait(y, pt)
                        C.wait(y, cast_tok["bf_l1a_inproj_ra"])
                    else:
                        C.wait(y, (eT, a_guard[dc - 2]))
                    b = C.dma_batch(y)
                    box.append(C.qinc(b, y.dma_start(
                        out=w5a[:, (dc % 2) * 1024:(dc % 2) * 1024 + 1024].rearrange("p (a d) -> p a d", a=CCH),
                        in_=dram["bf_l1a_inproj_ra"][:, dc, 0:128].rearrange("(a p) d -> p a d", p=128))))
                C.Y(_)
                ld_toks.append(box[0])
                if dc == 0 and tc == 0:
                    a_guard = {}

                baseT = C.val(eT)
                def _(t, dc=dc, tc=tc, wt=ld_toks[dc], pt=phase_tok):
                    C.wait(t, wt)
                    if dc == 0:
                        C.wait(t, pt)
                    pp = pA if dc % 2 == 0 else pB
                    if dc >= 2:
                        C.wait(t, (eS, a_sguard[dc - 2]))
                    for cc in range(CCH):
                        mm = t.matmul(out=pp[:, 0:TQ5],
                                      lhsT=w5a[:, (dc % 2) * 1024 + cc * 128:(dc % 2) * 1024 + (cc + 1) * 128],
                                      rhs=p2bf[:, cc * T + ts0:cc * T + ts0 + TQ5],
                                      start=(cc == 0), stop=(cc == CCH - 1))
                    C.inc(eT, mm)
                C.T(_)
                a_guard[dc] = C.val(eT)
                if dc == 0:
                    a_sguard = {}

                sA = C.val(eS)
                vA = C.val(eV)
                def _(s, dc=dc, tc=tc, baseT=baseT, vA=vA):
                    C.wait(s, (eT, baseT + 1))
                    k = tc * 16 + dc
                    if k >= 2:
                        C.wait(s, a_vg[k - 2])
                    pp = pA if dc % 2 == 0 else pB
                    C.inc(eS, s.copy(out=pool4[:, XMS + (dc % 2) * (TQ5 + 3) + 3:XMS + (dc % 2) * (TQ5 + 3) + 3 + TQ5],
                                     in_=pp[:, 0:TQ5]))
                C.S(_)
                a_sguard[dc] = C.val(eS)

                def _(v, dc=dc, tc=tc, sA=sA):
                    C.wait(v, (eS, sA + 1))
                    xb = XMS + (dc % 2) * (TQ5 + 3)
                    cb = CVS + (dc % 2) * TQ5
                    tl = CV["convtail"] + dc * 3
                    def ed(instr):
                        C.wait(v, C.inc(eV, instr))
                    ed(v.tensor_copy(out=pool4[:, xb:xb + 3], in_=cvec[:, tl:tl + 3]))
                    ed(v.tensor_scalar(out=pool4[:, cb:cb + TQ5], in0=pool4[:, xb:xb + TQ5],
                                       scalar1=cvec[:, CV["convw"] + dc * 4:CV["convw"] + dc * 4 + 1],
                                       scalar2=None, op0=ALU.mult))
                    for jj in range(1, 4):
                        ed(v.scalar_tensor_tensor(out=pool4[:, cb:cb + TQ5], in0=pool4[:, xb + jj:xb + jj + TQ5],
                                                  scalar=cvec[:, CV["convw"] + dc * 4 + jj:CV["convw"] + dc * 4 + jj + 1],
                                                  in1=pool4[:, cb:cb + TQ5], op0=ALU.mult, op1=ALU.add))
                    ed(v.tensor_copy(out=cvec[:, tl:tl + 3], in_=pool4[:, xb + TQ5:xb + TQ5 + 3]))
                C.V(_)
                a_vg[tc * 16 + dc] = C.tok(eV)

                sA2 = C.val(eS)
                def _(s, dc=dc, vt=C.tok(eV)):
                    C.wait(s, vt)
                    tkn = C.inc(eS, s.activation(out=pool1[:, dc * TQ5:(dc + 1) * TQ5],
                                                 in_=pool4[:, CVS + (dc % 2) * TQ5:CVS + (dc % 2) * TQ5 + TQ5],
                                                 func=AF.Silu,
                                                 bias=cvec[:, CV["convb"] + dc:CV["convb"] + dc + 1]))
                    C.wait(s, tkn)
                    C.inc(eS, s.copy(out=xsbbf[:, UBF + dc * TQ5:UBF + (dc + 1) * TQ5],
                                     in_=pool1[:, dc * TQ5:(dc + 1) * TQ5]))
                C.S(_)
                def _(v, dc=dc, st=C.tok(eS)):
                    C.wait(v, st)
                    C.inc(eV, v.tensor_scalar(out=x_sb[:, YO + dc * TQ5:YO + (dc + 1) * TQ5],
                                              in0=pool1[:, dc * TQ5:(dc + 1) * TQ5],
                                              scalar1=cvec[:, CV["D"] + dc:CV["D"] + dc + 1],
                                              scalar2=None, op0=ALU.mult))
                C.V(_)
            A_tok = [C.tok(eV), C.tok(eS), C.tok(eT)]

            def _(t, tc=tc, at=A_tok):
                C.wait(t, *at)
                for cc2 in range(DCH):
                    t.matmul(out=pC[0:64, 0:TQ5],
                             lhsT=cvbf[:, XPOFF + cc2 * 96:XPOFF + cc2 * 96 + 64],
                             rhs=xsbbf[:, UBF + cc2 * TQ5:UBF + (cc2 + 1) * TQ5],
                             start=(cc2 == 0), stop=(cc2 == DCH - 1))
                for cc2 in range(DCH):
                    t.matmul(out=pD[0:16, 0:TQ5],
                             lhsT=cvbf[:, XPOFF + cc2 * 96 + 64:XPOFF + cc2 * 96 + 80],
                             rhs=xsbbf[:, UBF + cc2 * TQ5:UBF + (cc2 + 1) * TQ5],
                             start=(cc2 == 0), stop=(cc2 == DCH - 1))
                for cc2 in range(DCH):
                    mm = t.matmul(out=pSC[0:16, 0:TQ5],
                                  lhsT=cvbf[:, XPOFF + cc2 * 96 + 80:XPOFF + cc2 * 96 + 96],
                                  rhs=xsbbf[:, UBF + cc2 * TQ5:UBF + (cc2 + 1) * TQ5],
                                  start=(cc2 == 0), stop=(cc2 == DCH - 1))
                C.inc(eT, mm)
            C.T(_)
            B_T = C.tok(eT)

            def _(s, tc=tc, bt=B_T):
                C.wait(s, bt)
                s.copy(out=p4bf[0:64, DT2:DT2 + TQ5], in_=pC[0:64, 0:TQ5])
                s.copy(out=p4bf[64:128, DT2:DT2 + TQ5], in_=pC[0:64, 0:TQ5])
                s.copy(out=pool4[0:16, BCO:BCO + TQ5], in_=pD[0:16, 0:TQ5])
                C.inc(eS, s.copy(out=pool4[0:16, CCO:CCO + TQ5], in_=pSC[0:16, 0:TQ5]))
            C.S(_)
            B_tok = C.tok(eS)

            cT0 = C.val(eT)
            cS0 = C.val(eS)
            cV0 = C.val(eV)
            def _(t, tc=tc, bt=B_tok, cS0=cS0):
                C.wait(t, bt)
                for dc in range(DCH):
                    pp = pA if dc % 2 == 0 else pB
                    if dc >= 2:
                        C.wait(t, (eS, cS0 + (dc - 2) * 4 + 4))
                    mm = t.matmul(out=pp[:, 0:TQ5],
                                  lhsT=cvbf[(dc % 2) * 64:(dc % 2) * 64 + 64,
                                            DTOFF + (dc // 2) * 128:DTOFF + (dc // 2) * 128 + 128],
                                  rhs=p4bf[(dc % 2) * 64:(dc % 2) * 64 + 64, DT2:DT2 + TQ5],
                                  start=True, stop=True)
                    C.inc(eT, mm)
            C.T(_)

            def _(s, tc=tc, cT0=cT0, cV0=cV0):
                for dc in range(DCH):
                    C.wait(s, (eT, cT0 + dc + 1))
                    pp = pA if dc % 2 == 0 else pB
                    dtb = cvec[:, CV["dtb"] + dc:CV["dtb"] + dc + 1]
                    aslot = pool4[:, CVS + (dc % 2) * TQ5:CVS + (dc % 2) * TQ5 + TQ5]
                    if dc >= 2:
                        C.wait(s, (eV, cV0 + (dc - 2) * 2 + 2))
                    tkn = C.inc(eS, s.activation(out=aslot, in_=pp[:, 0:TQ5], func=AF.Abs, bias=dtb))
                    C.wait(s, tkn)
                    tkn = C.inc(eS, s.activation(out=aslot, in_=aslot, func=AF.Exp, scale=-1.0))
                    C.wait(s, tkn)
                    tkn = C.inc(eS, s.activation(out=aslot, in_=aslot, func=AF.Ln,
                                                 bias=cst[:, CS["ones_col"]:CS["ones_col"] + 1]))
                    C.wait(s, tkn)
                    C.inc(eS, s.activation(out=pool3[:, dc * TQ5:(dc + 1) * TQ5], in_=pp[:, 0:TQ5],
                                           func=AF.Relu, bias=dtb))
            C.S(_)

            def _(v, tc=tc, cS0=cS0):
                for dc in range(DCH):
                    C.wait(v, (eS, cS0 + dc * 4 + 4))
                    aslot = pool4[:, CVS + (dc % 2) * TQ5:CVS + (dc % 2) * TQ5 + TQ5]
                    tkn = C.inc(eV, v.tensor_tensor(out=pool3[:, dc * TQ5:(dc + 1) * TQ5],
                                                    in0=pool3[:, dc * TQ5:(dc + 1) * TQ5],
                                                    in1=aslot, op=ALU.add))
                    C.wait(v, tkn)
                    C.inc(eV, v.tensor_tensor(out=x_sb[:, DUO + dc * TQ5:DUO + (dc + 1) * TQ5],
                                              in0=pool3[:, dc * TQ5:(dc + 1) * TQ5],
                                              in1=pool1[:, dc * TQ5:(dc + 1) * TQ5], op=ALU.mult))
            C.V(_)
            C_tok = [C.tok(eV), C.tok(eS)]

            DA0, DB0, H0, HC0 = 0, 2 * TQ5, 4 * TQ5, 6 * TQ5
            dT0 = C.val(eT)
            dS0 = C.val(eS)
            dV0 = C.val(eV)
            dG0 = C.val(eG)

            def _(t, tc=tc, ct=C_tok, dV0=dV0):
                C.wait(t, *ct)
                for n in range(DSTATE):
                    ppB = pA if n % 2 == 0 else pB
                    ppC = pC if n % 2 == 0 else pD
                    if n >= 2:
                        C.wait(t, (eV, dV0 + (n - 2 + 1) * 64))
                    t.matmul(out=ppB[:, 0:TQ5], lhsT=sel16[:, n * 128:(n + 1) * 128],
                             rhs=pool4[0:16, BCO:BCO + TQ5], start=True, stop=True)
                    mm = t.matmul(out=ppC[:, 0:TQ5], lhsT=sel16[:, n * 128:(n + 1) * 128],
                                  rhs=pool4[0:16, CCO:CCO + TQ5], start=True, stop=True)
                    C.inc(eT, mm)
            C.T(_)

            def _(s, tc=tc, ct=C_tok, dV0=dV0):
                C.wait(s, *ct)
                for n in range(DSTATE):
                    for dc in range(DCH):
                        k = n * 16 + dc
                        if k >= 2:
                            C.wait(s, (eV, dV0 + (k - 2) * 4 + 2))
                        C.inc(eS, s.activation(out=pool1[:, DA0 + (k % 2) * TQ5:DA0 + (k % 2) * TQ5 + TQ5],
                                               in_=pool3[:, dc * TQ5:(dc + 1) * TQ5], func=AF.Exp,
                                               scale=cvec[:, CV["A"] + dc * 16 + n:CV["A"] + dc * 16 + n + 1]))
            C.S(_)

            def _(v, tc=tc, dT0=dT0, dS0=dS0, dG0=dG0):
                for n in range(DSTATE):
                    C.wait(v, (eT, dT0 + n + 1))
                    ppB = pA if n % 2 == 0 else pB
                    ppC = pC if n % 2 == 0 else pD
                    for dc in range(DCH):
                        k = n * 16 + dc
                        def ed(instr):
                            C.wait(v, C.inc(eV, instr))
                        C.wait(v, (eS, dS0 + k + 1))
                        if k >= 2:
                            C.wait(v, (eG, dG0 + k - 1))
                        ed(v.tensor_tensor(out=pool1[:, DB0 + (k % 2) * TQ5:DB0 + (k % 2) * TQ5 + TQ5],
                                           in0=x_sb[:, DUO + dc * TQ5:DUO + (dc + 1) * TQ5],
                                           in1=ppB[:, 0:TQ5], op=ALU.mult))
                        ed(v.tensor_tensor_scan(out=pool1[:, H0 + (k % 2) * TQ5:H0 + (k % 2) * TQ5 + TQ5],
                                                data0=pool1[:, DA0 + (k % 2) * TQ5:DA0 + (k % 2) * TQ5 + TQ5],
                                                data1=pool1[:, DB0 + (k % 2) * TQ5:DB0 + (k % 2) * TQ5 + TQ5],
                                                initial=cvec[:, HCAR + k:HCAR + k + 1],
                                                op0=ALU.mult, op1=ALU.add))
                        ed(v.tensor_copy(out=cvec[:, HCAR + k:HCAR + k + 1],
                                         in_=pool1[:, H0 + (k % 2) * TQ5 + TQ5 - 1:H0 + (k % 2) * TQ5 + TQ5]))
                        ed(v.tensor_tensor(out=pool1[:, HC0 + (k % 2) * TQ5:HC0 + (k % 2) * TQ5 + TQ5],
                                           in0=pool1[:, H0 + (k % 2) * TQ5:H0 + (k % 2) * TQ5 + TQ5],
                                           in1=ppC[:, 0:TQ5], op=ALU.mult))
            C.V(_)

            def _(g, tc=tc, dV0=dV0):
                for n in range(DSTATE):
                    for dc in range(DCH):
                        k = n * 16 + dc
                        C.wait(g, (eV, dV0 + k * 4 + 4))
                        C.inc(eG, g.tensor_tensor(out=x_sb[:, YO + dc * TQ5:YO + (dc + 1) * TQ5],
                                                  in0=x_sb[:, YO + dc * TQ5:YO + (dc + 1) * TQ5],
                                                  in1=pool1[:, HC0 + (k % 2) * TQ5:HC0 + (k % 2) * TQ5 + TQ5],
                                                  op=ALU.add))
            C.G(_)
            D_tok = [C.tok(eG), C.tok(eV), C.tok(eS), C.tok(eT)]

            e_ld = []
            for dc in range(DCH):
                box = []
                def _(y, dc=dc, tc=tc, dt=D_tok):
                    if dc < 2:
                        C.wait(y, *dt)
                    else:
                        C.wait(y, (eT, e_guard[dc - 2]))
                    b = C.dma_batch(y)
                    box.append(C.qinc(b, y.dma_start(
                        out=w5a[:, (dc % 2) * 1024:(dc % 2) * 1024 + 1024].rearrange("p (a d) -> p a d", a=CCH),
                        in_=dram["bf_l1a_inproj_ra"][:, dc, 128:256].rearrange("(a p) d -> p a d", p=128))))
                C.Y(_)
                e_ld.append(box[0])
                if dc == 0:
                    e_guard = {}

                eT0 = C.val(eT)
                def _(t, dc=dc, tc=tc, wt=e_ld[dc], dt=D_tok, eS0v=None):
                    C.wait(t, wt)
                    if dc == 0:
                        C.wait(t, *dt)
                    if dc >= 2:
                        C.wait(t, (eS, e_sguard[dc - 2]))
                    pp = pA if dc % 2 == 0 else pB
                    for cc in range(CCH):
                        mm = t.matmul(out=pp[:, 0:TQ5],
                                      lhsT=w5a[:, (dc % 2) * 1024 + cc * 128:(dc % 2) * 1024 + (cc + 1) * 128],
                                      rhs=p2bf[:, cc * T + ts0:cc * T + ts0 + TQ5],
                                      start=(cc == 0), stop=(cc == CCH - 1))
                    C.inc(eT, mm)
                C.T(_)
                e_guard[dc] = C.val(eT)
                if dc == 0:
                    e_sguard = {}

                eV0 = C.val(eV)
                def _(s, dc=dc, eT0=eT0, eV0=eV0):
                    C.wait(s, (eT, eT0 + 1))
                    if dc >= 1:
                        C.wait(s, (eV, e_vguard[dc - 1]))
                    pp = pA if dc % 2 == 0 else pB
                    C.inc(eS, s.activation(out=pool4[:, SRS:SRS + TQ5],
                                           in_=pp[:, 0:TQ5], func=AF.Silu))
                C.S(_)
                e_sguard[dc] = C.val(eS)
                if dc == 0:
                    e_vguard = {}
                sE = C.val(eS)
                def _(v, dc=dc, sE=sE):
                    C.wait(v, (eS, sE))
                    C.inc(eV, v.tensor_tensor(out=p3bf[:, dc * TQ5:(dc + 1) * TQ5],
                                              in0=x_sb[:, YO + dc * TQ5:YO + (dc + 1) * TQ5],
                                              in1=pool4[:, SRS:SRS + TQ5],
                                              op=ALU.mult))
                C.V(_)
                e_vguard[dc] = C.val(eV)
            E_tok = [C.tok(eV), C.tok(eS)]

            f_ld = []
            for mc in range(CCH):
                box = []
                def _(y, mc=mc, tc=tc, et=E_tok):
                    if mc == 0:
                        C.wait(y, *et)
                        C.wait(y, cast_tok["bf_l1a_outproj_t"])
                    else:
                        C.wait(y, (eT, f_guard[mc - 1]))
                    b = C.dma_batch(y)
                    box.append(C.qinc(b, y.dma_start(
                        out=w5a[:, 0:2048].rearrange("p (a d) -> p a d", a=DCH),
                        in_=dram["bf_l1a_outproj_t"][:, mc * 128:(mc + 1) * 128].rearrange("(a p) d -> p a d", p=128))))
                C.Y(_)
                f_ld.append(box[0])
                if mc == 0:
                    f_guard = {}
                fT0 = C.val(eT)
                def _(t, mc=mc, wt=f_ld[mc], et=E_tok, fS=None):
                    C.wait(t, wt)
                    if mc == 0:
                        C.wait(t, *et)
                    if mc >= 2:
                        C.wait(t, (eS, f_sguard[mc - 2]))
                    pp = pA if mc % 2 == 0 else pB
                    for dc in range(DCH):
                        mm = t.matmul(out=pp[:, 0:TQ5],
                                      lhsT=w5a[:, dc * 128:(dc + 1) * 128],
                                      rhs=p3bf[:, dc * TQ5:(dc + 1) * TQ5],
                                      start=(dc == 0), stop=(dc == DCH - 1))
                    C.inc(eT, mm)
                C.T(_)
                f_guard[mc] = C.val(eT)
                if mc == 0:
                    f_sguard = {}
                def _(s, mc=mc, fT0=fT0):
                    C.wait(s, (eT, fT0 + 1))
                    C.inc(eS, s.copy(out=pool4[:, MB + mc * TQ5:MB + (mc + 1) * TQ5], in_=pA[:, 0:TQ5] if mc % 2 == 0 else pB[:, 0:TQ5]))
                C.S(_)
                f_sguard[mc] = C.val(eS)
            F_tok = C.tok(eS)

            mbox = []
            def _(y, tc=tc, ft=F_tok):
                C.wait(y, ft)
                b = C.dma_batch(y)
                mbox.append(C.qinc(b, y.dma_start(
                    out=spill_mc[:, :, ts0:ts0 + TQ5].rearrange("a p t -> p a t"),
                    in_=pool4[:, MB:MB + CCH * TQ5].rearrange("p (a t) -> p a t", a=CCH))))
            C.Y(_)
            phase_tok = mbox[0]

        scan_done = phase_tok
        return mamba_ffn(seq, scan_done)

    def mamba_ffn(seq, scan_done):
        xsbbf5 = x_sb.bitcast(BF16)
        TF = min(1024, T)
        NTC2 = T // TF
        NHALF = max(1, TF // 512)
        SW = min(512, TF)
        SC_C = 16
        SC_Q = 48
        scnt = [0]
        _sq_junk_toks = {}

        last = scan_done
        for tc2 in range(NTC2):
            tf0 = tc2 * TF
            box = []
            def _(y, tc2=tc2, lt=last):
                C.wait(y, lt)
                b = C.dma_batch(y)
                box.append(C.qinc(b, y.dma_start(
                    out=pool1[:, 0:CCH * TF].rearrange("p (a t) -> p a t", a=CCH),
                    in_=spill_mc[:, :, tf0:tf0 + TF].rearrange("a p t -> p a t"))))
            C.Y(_)
            m_tok = box[0]

            def _(s, mt=m_tok):
                C.wait(s, mt)
                for cc in range(CCH):
                    t = C.inc(eS, s.copy(out=p2bf[:, cc * TF:(cc + 1) * TF],
                                         in_=pool1[:, cc * TF:(cc + 1) * TF]))
            C.S(_)
            mbf_tok = C.tok(eS)

            h1_ld = []
            for hc in range(32):
                bx = []
                def _(y, hc=hc, mt=mbf_tok):
                    if hc < 2:
                        C.wait(y, mt)
                        C.wait(y, cast_tok["bf_l1a_w1t"])
                    else:
                        C.wait(y, (eT, h1_guard[hc - 2]))
                    b = C.dma_batch(y)
                    bx.append(C.qinc(b, y.dma_start(
                        out=w5a[:, (hc % 2) * 1024:(hc % 2) * 1024 + 1024].rearrange("p (a d) -> p a d", a=CCH),
                        in_=dram["bf_l1a_w1t"][:, hc * 128:(hc + 1) * 128].rearrange("(a p) d -> p a d", p=128))))
                C.Y(_)
                h1_ld.append(bx[0])
                if hc == 0:
                    h1_guard = {}
                    h1_sg = {}
                hT0 = C.val(eT)
                def _(t, hc=hc, wt=h1_ld[hc], mt=mbf_tok):
                    C.wait(t, wt)
                    if hc == 0:
                        C.wait(t, mt)
                    if hc >= 2:
                        C.wait(t, (eS, h1_sg[hc - 2]))
                    for hf in range(NHALF):
                        pp = (pA, pB)[hf] if hc % 2 == 0 else (pC, pD)[hf]
                        for cc in range(CCH):
                            mm = t.matmul(out=pp[:, 0:SW],
                                          lhsT=w5a[:, (hc % 2) * 1024 + cc * 128:(hc % 2) * 1024 + (cc + 1) * 128],
                                          rhs=p2bf[:, cc * TF + hf * SW:cc * TF + hf * SW + SW],
                                          start=(cc == 0), stop=(cc == CCH - 1))
                        C.inc(eT, mm)
                C.T(_)
                h1_guard[hc] = C.val(eT)
                def _(s, hc=hc, hT0=hT0):
                    for hf in range(NHALF):
                        C.wait(s, (eT, hT0 + hf + 1))
                        pp = (pA, pB)[hf] if hc % 2 == 0 else (pC, pD)[hf]
                        C.inc(eS, s.activation(out=xsbbf5[:, hc * TF + hf * SW:hc * TF + hf * SW + SW],
                                               in_=pp[:, 0:SW], func=AF.Gelu,
                                               bias=cvec[:, CV["b1"] + hc:CV["b1"] + hc + 1]))
                C.S(_)
                h1_sg[hc] = C.val(eS)
            h1_tok = C.tok(eS)

            f_prevT = None
            f_vg = {}
            for mc in range(CCH):
                for sl in range(2):
                    bx = []
                    def _(y, mc=mc, sl=sl, ht=h1_tok, fp=f_prevT):
                        if mc == 0 and sl == 0:
                            C.wait(y, ht)
                            C.wait(y, cast_tok["bf_l1a_w2t"])
                        else:
                            C.wait(y, (eT, fp))
                        b = C.dma_batch(y)
                        bx.append(C.qinc(b, y.dma_start(
                            out=w5a[:, 0:2048].rearrange("p (a d) -> p a d", a=16),
                            in_=dram["bf_l1a_w2t"][sl * 2048:(sl + 1) * 2048, mc * 128:(mc + 1) * 128].rearrange("(a p) d -> p a d", p=128))))
                    C.Y(_)
                    def _(t, mc=mc, sl=sl, wt=bx[0]):
                        C.wait(t, wt)
                        if mc >= 1 and sl == 0:
                            C.wait(t, f_vg[mc - 1])
                        for hh in range(16):
                            hc = sl * 16 + hh
                            for hf in range(NHALF):
                                pp = (pC, pD)[hf]
                                mm = t.matmul(out=pp[:, 0:SW],
                                              lhsT=w5a[:, hh * 128:(hh + 1) * 128],
                                              rhs=xsbbf5[:, hc * TF + hf * SW:hc * TF + hf * SW + SW],
                                              start=(hc == 0), stop=(hc == 31))
                            C.inc(eT, mm)
                    C.T(_)
                    f_prevT = C.val(eT)
                fT_end = C.val(eT)
                def _(v, mc=mc, tc2=tc2, fT_end=fT_end):
                    C.wait(v, (eT, fT_end))
                    for hf in range(NHALF):
                        pp = (pC, pD)[hf]
                        idx = (tc2 * CCH + mc) * NHALF + hf
                        tkn = C.inc(eV, v.scalar_tensor_tensor(
                            out=pool3[:, mc * TF + hf * SW:mc * TF + hf * SW + SW],
                            in0=pp[:, 0:SW], scalar=cvec[:, CV["b2"] + mc:CV["b2"] + mc + 1],
                            in1=pool1[:, mc * TF + hf * SW:mc * TF + hf * SW + SW],
                            op0=ALU.add, op1=ALU.add,
                            accum_out=stat[:, SC_C + idx:SC_C + idx + 1]))
                        scnt[0] = max(scnt[0], idx + 1)
                C.V(_)
                f_vg[mc] = C.tok(eV)
                def _(s, mc=mc, tc2=tc2, vt=f_vg[mc]):
                    C.wait(s, vt)
                    for hf in range(NHALF):
                        idx = (tc2 * CCH + mc) * NHALF + hf
                        if idx - 8 in _sq_junk_toks:
                            C.wait(s, _sq_junk_toks[idx - 8])
                        _sq_junk_toks[idx] = C.inc(eS, s.activation(
                            out=pool4[:, ((mc * NHALF + hf) % 8) * SW:((mc * NHALF + hf) % 8) * SW + SW],
                            in_=pool3[:, mc * TF + hf * SW:mc * TF + hf * SW + SW],
                            func=AF.Square,
                            accum_out=stat[:, SC_Q + idx:SC_Q + idx + 1]))
                C.S(_)
            csq_tok = C.tok(eS)
            cv_tok = C.tok(eV)

            box2 = []
            def _(y, tc2=tc2, ct=cv_tok, st=csq_tok):
                C.wait(y, ct, st)
                b = C.dma_batch(y)
                box2.append(C.qinc(b, y.dma_start(
                    out=spill_mc[:, :, tf0:tf0 + TF].rearrange("a p t -> p a t"),
                    in_=pool3[:, 0:CCH * TF].rearrange("p (a t) -> p a t", a=CCH))))
            C.Y(_)
            last = box2[0]

        NS = scnt[0]
        def _(t, lt=last, ct=C.tok(eS), cvt=C.tok(eV), NS=NS):
            C.wait(t, ct, cvt)
            t.matmul(out=pA[0:1, 0:NS], lhsT=cst[:, CS["ones_col"]:CS["ones_col"] + 1],
                     rhs=stat[:, SC_C:SC_C + NS], start=True, stop=True)
            mm = t.matmul(out=pA[0:1, 32:32 + NS], lhsT=cst[:, CS["ones_col"]:CS["ones_col"] + 1],
                          rhs=stat[:, SC_Q:SC_Q + NS], start=True, stop=True)
            C.inc(eT, mm)
        C.T(_)
        gs_T = C.tok(eT)
        def _(v, gt=gs_T, NS=NS):
            def ed(instr):
                C.wait(v, C.inc(eV, instr))
            C.wait(v, gt)
            ed(v.tensor_reduce(out=stat[0:1, 80:81], in_=pA[0:1, 0:NS],
                               axis=mybir.AxisListType.X, op=ALU.add))
            ed(v.tensor_reduce(out=stat[0:1, 81:82], in_=pA[0:1, 32:32 + NS],
                               axis=mybir.AxisListType.X, op=ALU.add))
            ed(v.tensor_scalar(out=stat[0:1, 82:83], in0=stat[0:1, 80:81],
                               scalar1=1.0 / (T * CDIM), scalar2=None, op0=ALU.mult))
            ed(v.tensor_tensor(out=stat[0:1, 83:84], in0=stat[0:1, 82:83],
                               in1=stat[0:1, 82:83], op=ALU.mult))
            ed(v.scalar_tensor_tensor(out=stat[0:1, 83:84], in0=stat[0:1, 81:82],
                                      scalar=1.0 / (T * CDIM), in1=stat[0:1, 83:84],
                                      op0=ALU.mult, op1=ALU.subtract))
        C.V(_)
        gv_tok = C.tok(eV)
        def _(s, gt=gv_tok):
            C.wait(s, gt)
            tkn = C.inc(eS, s.activation(out=stat[0:1, 84:85], in_=stat[0:1, 83:84], func=AF.Ln,
                                         bias=cst[0:1, CS["epsln"]:CS["epsln"] + 1]))
            C.wait(s, tkn)
            C.inc(eS, s.activation(out=stat[0:1, 84:85], in_=stat[0:1, 84:85], func=AF.Exp, scale=-0.5))
        C.S(_)
        gr_tok = C.tok(eS)
        def _(v, gt=gr_tok):
            C.wait(v, gt)
            C.inc(eV, v.scalar_tensor_tensor(out=stat[0:1, 85:86], in0=stat[0:1, 82:83],
                                             scalar=-1.0, in1=stat[0:1, 84:85],
                                             op0=ALU.mult, op1=ALU.mult))
        C.V(_)
        gn_tok = C.tok(eV)
        def _(t, gt=gn_tok):
            C.wait(t, gt)
            C.inc(eT, t.matmul(out=pB[:, 0:2], lhsT=cst[0:1, CS["ones_row"]:CS["ones_row"] + 128],
                               rhs=stat[0:1, 84:86], start=True, stop=True))
        C.T(_)
        gb_T = C.tok(eT)
        def _(s, gt=gb_T):
            C.wait(s, gt)
            C.inc(eS, s.copy(out=cvec[:, CV["gstat"]:CV["gstat"] + 2], in_=pB[:, 0:2]))
        C.S(_)
        gcol_tok = C.tok(eS)

        bx = []
        def _(y, lt=last):
            C.wait(y, lt)
            b = C.dma_batch(y)
            bx.append(C.qinc(b, y.dma_start(out=x_sb[:, 0:CCH * T].rearrange("p (a t) -> p a t", a=CCH),
                                            in_=spill_x.rearrange("a p t -> p a t"))))
        C.Y(_)
        xl_tok = bx[0]

        fin_tok = None
        for j in range(NFC):
            bx2 = []
            def _(y, j=j, ft=fin_tok):
                if ft is not None:
                    C.wait(y, ft)
                b = C.dma_batch(y)
                bx2.append(C.qinc(b, y.dma_start(
                    out=pool3[:, 0:CCH * TQ].rearrange("p (a t) -> p a t", a=CCH),
                    in_=spill_mc[:, :, j * TQ:(j + 1) * TQ].rearrange("a p t -> p a t"))))
            C.Y(_)
            c_tok = bx2[0]
            def _(v, j=j, ct=c_tok, xt=xl_tok, gt=gcol_tok):
                C.wait(v, ct, xt, gt)
                tk = None
                for cc in range(CCH):
                    xs = x_sb[:, cc * T + j * TQ: cc * T + (j + 1) * TQ]
                    tk = v.scalar_tensor_tensor(out=xs, in0=pool3[:, cc * TQ:(cc + 1) * TQ],
                                                scalar=cvec[:, CV["gstat"]:CV["gstat"] + 1],
                                                in1=xs, op0=ALU.mult, op1=ALU.add)
                C.wait(v, C.inc(eV, tk))
                for cc in range(CCH):
                    xs = x_sb[:, cc * T + j * TQ: cc * T + (j + 1) * TQ]
                    tk = v.tensor_scalar(out=xs, in0=xs,
                                         scalar1=cvec[:, CV["gstat"] + 1:CV["gstat"] + 2],
                                         scalar2=None, op0=ALU.add)
                C.inc(eV, tk)
            C.V(_)
            fin_tok = C.tok(eV)
        return [fin_tok]

    def load_x(seq, prev_toks):
        box = []
        def _(y):
            C.wait(y, *prev_toks)
            b = C.dma_batch(y)
            box.append(C.qinc(b, y.dma_start(out=x_sb[:, 0:CCH * T].rearrange("p (a t) -> p a t", a=CCH),
                                             in_=x_in[seq].rearrange("a p t -> p a t"))))
        C.Y(_)
        return [box[0]]

    def store_x(seq, prev_toks):
        box = []
        def _(y):
            C.wait(y, *prev_toks)
            b = C.dma_batch(y)
            box.append(C.qinc(b, y.dma_start(out=out[seq].rearrange("a p t -> p a t"),
                                             in_=x_sb[:, 0:CCH * T].rearrange("p (a t) -> p a t", a=CCH))))
        C.Y(_)
        return [box[0]]

    last = [C.tok(eV)]
    store_toks = []
    for seq in range(nseq):
        last = load_x(seq, last + [tok_theta, tok_A, tok_rbbc, tok_rw])
        if 9 in stages:
            last = [ln_stats(last)]
        if 7 in stages or 8 in stages:
            stats_tok8 = ln_stats(last)
            def _(t, st=stats_tok8):
                ln_bcast(t, 0, st)
            C.T(_)
            bc8 = C.tok(eT)
            def _(v, bc=bc8):
                C.wait(v, bc)
                tk = None
                if 8 in stages:
                    for cc in range(CCH):
                        xs = x_sb[:, cc * T: cc * T + TQ]
                        tk = v.tensor_tensor(out=pool1[:, cc * TQ:(cc + 1) * TQ], in0=xs, in1=pSC[:, 0:TQ], op=ALU.subtract)
                    C.wait(v, C.inc(eV, tk))
                    for cc in range(CCH):
                        xs = x_sb[:, cc * T: cc * T + TQ]
                        tk = v.tensor_tensor(out=xs, in0=pool1[:, cc * TQ:(cc + 1) * TQ], in1=pSC[:, TQ:2 * TQ], op=ALU.mult)
                    C.inc(eV, tk)
                else:
                    C.inc(eV, v.tensor_copy(out=pool1[:, 0:TQ], in_=pSC[:, 0:TQ]))
            C.V(_)
            last = [C.tok(eV)]
        if 1 in stages:
            last = rwkv(seq, last)
        if 2 in stages:
            last = attention(seq, last)
        if 3 in stages:
            last = rwkv(seq, last)
        if 4 in stages:
            last = moe(seq, last)
        if 5 in stages:
            last = mamba(seq, last)
        if 6 in stages:
            last = moe(seq, last)
        store_toks += store_x(seq, last)
        last = store_toks[-1:]

    def _(y):
        C.wait(y, *store_toks)
    C.Y(_)


_cache = {}


def _host_prep(inputs, T=TFULL):
    f32 = np.float32
    m = {}
    m["l0f_wr_t"] = np.ascontiguousarray(inputs["L0_ffn_W_r"].T)
    m["l0f_wk_t"] = np.ascontiguousarray(inputs["L0_ffn_W_k"].T)
    m["l0f_wv_t"] = np.ascontiguousarray(inputs["L0_ffn_W_v"].T)
    m["l0a_wq"] = np.ascontiguousarray(inputs["L0_attn_w_q"])
    m["l0a_wk"] = np.ascontiguousarray(inputs["L0_attn_w_k"])
    m["l0a_wv"] = np.ascontiguousarray(inputs["L0_attn_w_v"])
    m["l0a_wo"] = np.ascontiguousarray(inputs["L0_attn_w_o"])
    ip = inputs["L1_attn_in_proj_w"]
    ipt = ip.T
    ra = np.empty((CDIM, DCH, 256), dtype=f32)
    for dc in range(DCH):
        ra[:, dc, 0:128] = ipt[:, dc * 128:(dc + 1) * 128]
        ra[:, dc, 128:256] = ipt[:, DIN + dc * 128:DIN + (dc + 1) * 128]
    m["l1a_inproj_ra"] = np.ascontiguousarray(ra)
    m["l1a_xproj_t"] = np.ascontiguousarray(inputs["L1_attn_x_proj_w"].T)
    dtt = inputs["L1_attn_dt_proj_w"].T
    dtp = np.empty((128, 1024), dtype=f32)
    for h2 in range(2):
        for g in range(8):
            dtp[h2 * 64:(h2 + 1) * 64, g * 128:(g + 1) * 128] = dtt[:, (2 * g + h2) * 128:(2 * g + h2 + 1) * 128]
    m["l1a_dtproj_t"] = dtp
    m["l1a_outproj_t"] = np.ascontiguousarray(inputs["L1_attn_out_proj_w"].T)
    m["l1a_w1t"] = np.ascontiguousarray(inputs["L1_attn_ffn_w1"].T)
    m["l1a_w2t"] = np.ascontiguousarray(inputs["L1_attn_ffn_w2"].T)
    m["l1m_rwt"] = np.ascontiguousarray(inputs["L1_ffn_router_w"].T)
    m["l1m_ew"] = np.ascontiguousarray(inputs["L1_ffn_expert_weights"])
    def col(a, nch):
        return np.ascontiguousarray(a.reshape(nch, 128, -1).transpose(1, 0, 2).reshape(128, -1)).astype(f32)
    m["l0f_mur"] = col(inputs["L0_ffn_mu_r"], CCH)
    m["l0f_muk"] = col(inputs["L0_ffn_mu_k"], CCH)
    m["l0a_normw"] = col(inputs["L0_attn_norm_w"], CCH)
    m["l1a_convw"] = col(inputs["L1_attn_conv_w"].reshape(DIN, 4), DCH)
    m["l1a_convb"] = col(inputs["L1_attn_conv_b"], DCH)
    m["l1a_dtb"] = col(inputs["L1_attn_dt_proj_b"], DCH)
    m["l1a_alog"] = col(inputs["L1_attn_A_log"], DCH)
    m["l1a_dvec"] = col(inputs["L1_attn_D"], DCH)
    m["l1a_b1"] = col(inputs["L1_attn_ffn_b1"], 32)
    m["l1a_b2"] = col(inputs["L1_attn_ffn_b2"], CCH)
    m["l1m_rb"] = inputs["L1_ffn_router_b"].reshape(1, NEXP).astype(f32)
    m["pos_t"] = np.arange(T, dtype=np.float64).reshape(1, T).astype(f32)
    dh = np.arange(DH)
    dhexp = (-(dh // 2) / 32.0).astype(f32)
    m["dhexp"] = np.concatenate([dhexp, dhexp]).reshape(128, 1)
    sgn = np.where(np.arange(T) % 2 == 0, -1.0, 1.0).astype(f32).reshape(1, T)
    m["sgn_t"] = sgn
    import ml_dtypes
    m["ident_bf"] = np.eye(128, dtype=ml_dtypes.bfloat16)
    m["ident_f"] = np.eye(128, dtype=f32)
    mask = np.zeros((128, 128), dtype=np.float64)
    for a in range(128):
        mask[a, :a] = MASK_NEG
    m["maskT_bf"] = mask.astype(ml_dtypes.bfloat16)
    sel = np.zeros((16, 2048), dtype=f32)
    for n in range(16):
        sel[n, n * 128:(n + 1) * 128] = 1.0
    m["sel16"] = sel
    op = np.zeros((128, 8), dtype=ml_dtypes.bfloat16)
    op[0:64, 0:4] = 1
    op[64:128, 4:8] = 1
    m["onespair"] = op
    return m


def kernel(**inputs):
    T = TFULL
    x = np.asarray(inputs["L0_attn_x"])
    B = x.shape[0]
    shared = _host_prep(inputs, T)
    key = ("full", T)
    if key not in _cache:
        _cache[key] = build_program(T=T, nseq=NSEQ, stages=_STAGES)
    nc = _cache[key]
    in_maps = []
    for core in range(NCORES):
        im = dict(shared)
        xs = x[core * NSEQ:(core + 1) * NSEQ]
        xt = np.ascontiguousarray(xs.transpose(0, 2, 1))
        im["x"] = xt.reshape(NSEQ, CCH, 128, T)
        in_maps.append(im)
    res = run_bass_kernel_spmd(nc, in_maps, core_ids=list(range(NCORES)))
    outs = []
    for core in range(NCORES):
        o = res.results[core]["out"]
        o = o.reshape(NSEQ, CDIM, T).transpose(0, 2, 1)
        outs.append(o)
    return np.ascontiguousarray(np.concatenate(outs, axis=0)).astype(np.float32)


_STAGES = (1, 2, 3, 4, 5, 6)
_ATT_UPTO = 99


# version 08 (final)
# speedup vs baseline: 1.0199x; 1.0199x over previous
import math
import numpy as np
from contextlib import ExitStack

import concourse.bass as bass
import concourse.mybir as mybir
from concourse.bass_utils import run_bass_kernel_spmd

F32 = mybir.dt.float32
BF16 = mybir.dt.bfloat16
AF = mybir.ActivationFunctionType
ALU = mybir.AluOpType

NCORES = 8
NSEQ = 2
TFULL = 2048
CDIM = 1024
CCH = 8
NH = 16
KVH = 4
DH = 64
DIN = 2048
DCH = 16
DSTATE = 16
DTRANK = 64
NEXP = 8
DFF = 4096
EPS_LN = 1e-5
EPS_RMS = 1.1920929e-07
EPS_MOE = 1e-10
MASK_NEG = -1e30
MHAT_CLAMP = 30.0


class Ctx:
    NQP = 12

    def __init__(self, nc, blk, es):
        self.nc = nc
        self.blk = blk
        self.es = es
        self.sems = {}
        for name in ("eT", "eV", "eS", "eG", "qGa", "qGb", "qGc"):
            self.sems[name] = [es.enter_context(nc.semaphore(name)), 0]
        self.qpool = [es.enter_context(nc.semaphore(f"qP{i}")) for i in range(self.NQP)]
        self.qcnt = [0] * self.NQP
        self.qused = [False] * self.NQP
        self.qi = 0

    def dma_batch(self, y):
        i = self.qi % self.NQP
        self.qi += 1
        if self.qused[i]:
            y.wait_ge(self.qpool[i], self.qcnt[i])
        self.qused[i] = True
        return i

    def qinc(self, i, inst):
        self.qcnt[i] += 16
        inst.then_inc(self.qpool[i], 16)
        return ("Q", i, self.qcnt[i])

    def inc(self, name, inst, k=1):
        h = self.sems[name]
        inst.then_inc(h[0], k)
        h[1] += k
        return (name, h[1])

    def tok(self, name, delta=0):
        return (name, self.sems[name][1] + delta)

    def val(self, name):
        return self.sems[name][1]

    def wait(self, eng, *tokens):
        for tok in tokens:
            if tok[0] == "Q":
                _, i, v = tok
                if v > 0:
                    eng.wait_ge(self.qpool[i], v)
            else:
                name, v = tok
                if v > 0:
                    eng.wait_ge(self.sems[name][0], v)

    def T(self, fn): self.blk.tensor(fn)
    def V(self, fn): self.blk.vector(fn)
    def S(self, fn): self.blk.scalar(fn)
    def G(self, fn): self.blk.gpsimd(fn)
    def Y(self, fn): self.blk.sync(fn)


def build_program(T=TFULL, nseq=NSEQ, stages=(1, 2, 3, 4, 5, 6)):
    nc = bass.Bass("TRN2", target_bir_lowering=False)
    TC = T // 512 if T >= 512 else 1
    TQ = min(512, T)
    NFC = T // TQ
    NTT = T // 128

    dram = {}

    def din(name, shape, dt=F32):
        dram[name] = nc.dram_tensor(name, shape, dt, kind="ExternalInput").ap()
        return dram[name]

    def dint(name, shape, dt):
        dram[name] = nc.dram_tensor(name, shape, dt, kind="Internal").ap()
        return dram[name]

    x_in = din("x", [nseq, CCH, 128, T])
    out = nc.dram_tensor("out", [nseq, CCH, 128, T], F32, kind="ExternalOutput").ap()

    w_f32 = {}
    def wdecl(name, shape):
        w_f32[name] = din(name, shape)
        return w_f32[name]

    wdecl("l0f_wr_t", [CDIM, CDIM])
    wdecl("l0f_wk_t", [CDIM, CDIM])
    wdecl("l0f_wv_t", [CDIM, CDIM])
    wdecl("l0a_wq", [CDIM, CDIM])
    wdecl("l0a_wk", [CDIM, 256])
    wdecl("l0a_wv", [CDIM, 256])
    wdecl("l0a_wo", [CDIM, CDIM])
    wdecl("l1a_inproj_ra", [CDIM, DCH, 256])
    wdecl("l1a_xproj_t", [DIN, 96])
    wdecl("l1a_dtproj_t", [128, 1024])
    wdecl("l1a_outproj_t", [DIN, CDIM])
    wdecl("l1a_w1t", [CDIM, DFF])
    wdecl("l1a_w2t", [DFF, CDIM])
    wdecl("l1m_rwt", [CDIM, NEXP])
    wdecl("l1m_ew", [NEXP, CDIM, CDIM])
    din("l0f_mur", [128, CCH])
    din("l0f_muk", [128, CCH])
    din("l0a_normw", [128, CCH])
    din("l1a_convw", [128, 4 * DCH])
    din("l1a_convb", [128, DCH])
    din("l1a_dtb", [128, DCH])
    din("l1a_alog", [128, DSTATE * DCH])
    din("l1a_dvec", [128, DCH])
    din("l1a_b1", [128, 32])
    din("l1a_b2", [128, CCH])
    din("l1m_rb", [1, NEXP])
    din("pos_t", [1, T])
    din("dhexp", [128, 1])
    din("sgn_t", [1, T])
    din("ident_bf", [128, 128], BF16)
    din("ident_f", [128, 128])
    din("maskT_bf", [128, 128], BF16)
    din("sel16", [16, 2048])
    din("onespair", [128, 8], BF16)

    wbf = {}
    for name in w_f32:
        wbf[name] = dint("bf_" + name, list(w_f32[name].shape), BF16)

    spill_x = dint("spill_x", [CCH, 128, T], F32)
    spill_mc = dint("spill_mc", [CCH, 128, T], F32)

    es = ExitStack()

    def sb(name, shape, dt):
        return es.enter_context(nc.sbuf_tensor(name, shape, dt))

    def ps(name, shape, dt=F32):
        return es.enter_context(nc.psum_tensor(name, shape, dt))

    x_sb = sb("x_sb", [128, max(CCH * T, 32 * min(512, T))], F32)
    pool1 = sb("pool1", [128, 8192], F32)
    pool2 = sb("pool2", [128, 8192], F32)
    pool3 = sb("pool3", [128, 8192], F32)
    pool4 = sb("pool4", [128, 4096], F32)
    cvec = sb("cvec", [128, 2048], F32)
    stat = sb("stat", [128, 96], F32)
    rowsM = sb("rowsM", [65, T + 512], F32)
    cst = sb("cst", [128, 160], F32)
    ident_bf = sb("s_ident_bf", [128, 128], BF16)
    rw_sb = sb("s_rw", [128, 64], BF16)
    rw_lo = sb("s_rwlo", [128, 64], BF16)
    w5a = sb("s_w5a", [128, 2048], BF16)

    mask_bf = sb("s_maskT_bf", [128, 128], BF16)
    onespair = sb("s_onespair", [128, 8], BF16)
    sel16 = sb("s_sel16", [16, 2048], F32)

    pSC = ps("pSC", [128, 2048], F32)
    pA = ps("pA", [128, 512], F32)
    pB = ps("pB", [128, 512], F32)
    pC = ps("pC", [128, 512], F32)
    pD = ps("pD", [128, 512], F32)

    blk = es.enter_context(nc.Block())
    C = Ctx(nc, blk, es)

    with es:
        emit_all(C, nc, dram, wbf, x_in, out, spill_x, spill_mc,
                 x_sb, pool1, pool2, pool3, pool4,
                 cvec, stat, rowsM, cst, ident_bf, rw_sb, rw_lo, w5a, mask_bf, onespair, sel16,
                 pSC, pA, pB, pC, pD, T, nseq, stages)
    return nc


def emit_all(C, nc, dram, wbf, x_in, out, spill_x, spill_mc,
             x_sb, pool1, pool2, pool3, pool4,
             cvec, stat, rowsM, cst, ident_bf, rw_sb, rw_lo, w5a, mask_bf, onespair, sel16,
             pSC, pA, pB, pC, pD, T, nseq, stages):
    p1bf = pool1.bitcast(BF16)
    p2bf = pool2.bitcast(BF16)
    p3bf = pool3.bitcast(BF16)
    p4bf = pool4.bitcast(BF16)
    arA = pool1
    arB = pool2
    arC = pool3
    TQ = min(512, T)
    NFC = T // TQ

    CV = {}
    off = 0
    def cv(name, width):
        nonlocal off
        CV[name] = off
        off += width
        return CV[name]
    cv("mur", CCH)
    cv("muk", CCH)
    cv("normw", CCH)
    cv("convw", 4 * DCH)
    cv("convb", DCH)
    cv("dtb", DCH)
    cv("A", DSTATE * DCH)
    cv("D", DCH)
    cv("b1", 32)
    cv("b2", CCH)
    cv("mcols", NH)
    cv("theta", 1)
    cv("hcarry", 256)
    cv("convtail", 48)
    cv("gstat", 2)
    assert off <= 768

    CS = {}
    off2 = 0
    def cs(name, width):
        nonlocal off2
        CS[name] = off2
        off2 += width
        return CS[name]
    cs("ones_col", 1)
    cs("ones_row", 128)
    cs("zero_col", 1)
    cs("halfpi", 1)
    cs("ln1e4", 1)
    cs("tenk", 1)
    cs("rbbc", NEXP)
    cs("epsln", 1)
    cs("epsrms", 1)
    cs("scratch", 16)
    assert off2 <= 160

    eT = "eT"; eV = "eV"; eS = "eS"; eG = "eG"

    cast_tok = {}

    GROUP_A = ("l0f_wr_t", "l0f_wk_t", "l0f_wv_t", "l0a_wq", "l0a_wk", "l0a_wv", "l0a_wo")
    GROUP_B = ("l1a_inproj_ra", "l1a_xproj_t", "l1a_dtproj_t", "l1a_outproj_t",
               "l1a_w1t", "l1a_w2t", "l1m_rwt")

    def _(g):
        for name in GROUP_A:
            C.inc("qGa", g.dma_start(out=wbf[name], in_=dram[name]), 16)
        for name in GROUP_B:
            C.inc("qGb", g.dma_start(out=wbf[name], in_=dram[name]), 16)
        for e in range(NEXP):
            C.inc("qGc", g.dma_start(out=wbf["l1m_ew"][e], in_=dram["l1m_ew"][e]), 16)
    C.G(_)
    for name in GROUP_A:
        cast_tok["bf_" + name] = C.tok("qGa")
    for name in GROUP_B:
        cast_tok["bf_" + name] = C.tok("qGb")
    cast_tok["bf_l1m_ew"] = C.tok("qGc")

    def _(y):
        b = C.dma_batch(y)
        for dst, src in [
            (cvec[:, CV["mur"]:CV["mur"] + CCH], dram["l0f_mur"]),
            (cvec[:, CV["muk"]:CV["muk"] + CCH], dram["l0f_muk"]),
            (cvec[:, CV["normw"]:CV["normw"] + CCH], dram["l0a_normw"]),
            (cvec[:, CV["convw"]:CV["convw"] + 4 * DCH], dram["l1a_convw"]),
            (cvec[:, CV["convb"]:CV["convb"] + DCH], dram["l1a_convb"]),
            (cvec[:, CV["dtb"]:CV["dtb"] + DCH], dram["l1a_dtb"]),
            (cvec[:, CV["A"]:CV["A"] + DSTATE * DCH], dram["l1a_alog"]),
            (cvec[:, CV["D"]:CV["D"] + DCH], dram["l1a_dvec"]),
            (cvec[:, CV["b1"]:CV["b1"] + 32], dram["l1a_b1"]),
            (cvec[:, CV["b2"]:CV["b2"] + CCH], dram["l1a_b2"]),
            (stat[0:1, 0:NEXP], dram["l1m_rb"]),
            (cvec[:, CV["theta"]:CV["theta"] + 1], dram["dhexp"]),
            (ident_bf[:], dram["ident_bf"]),
            (mask_bf[:], dram["maskT_bf"]),
            (sel16[:], dram["sel16"]),
            (onespair[:], dram["onespair"]),
        ]:
            t = C.qinc(b, y.dma_start(out=dst, in_=src))
        cast_tok["consts"] = t
    C.Y(_)
    tok_consts = t_consts = cast_tok["consts"]

    rw_box = []
    def _(y):
        C.wait(y, cast_tok["bf_l1m_rwt"])
        b = C.dma_batch(y)
        C.qinc(b, y.dma_start(out=rw_sb[:, 0:64].rearrange("p (a d) -> p a d", a=CCH),
                              in_=dram["bf_l1m_rwt"].rearrange("(a p) d -> p a d", p=128)))
        rw_box.append(C.qinc(b, y.dma_start(out=pool2[:, 0:64].rearrange("p (a d) -> p a d", a=CCH),
                                            in_=dram["l1m_rwt"].rearrange("(a p) d -> p a d", p=128))))
    C.Y(_)
    def _(v, rt=rw_box[0]):
        C.wait(v, rt)
        C.inc(eV, v.tensor_tensor(out=rw_lo[:, 0:64], in0=pool2[:, 0:64], in1=rw_sb[:, 0:64],
                                  op=ALU.subtract))
    C.V(_)
    tok_rw = C.tok(eV)

    def _(v):
        C.wait(v, tok_consts)
        v.memset(cst[:, CS["ones_col"]:CS["ones_col"] + 1], 1.0)
        v.memset(cst[0:1, CS["ones_row"]:CS["ones_row"] + 128], 1.0)
        v.memset(cst[32:33, CS["ones_row"]:CS["ones_row"] + 128], 1.0)
        v.memset(cst[:, CS["zero_col"]:CS["zero_col"] + 1], 0.0)
        v.memset(cst[:, CS["halfpi"]:CS["halfpi"] + 1], math.pi / 2)
        v.memset(cst[:, CS["tenk"]:CS["tenk"] + 1], 10000.0)
        v.memset(cst[:, CS["epsln"]:CS["epsln"] + 1], EPS_LN)
        v.memset(cst[:, CS["epsrms"]:CS["epsrms"] + 1], EPS_RMS)
        t = C.inc(eV, v.engine_nop())
    C.V(_)
    tok_cst0 = C.tok(eV)

    def _(sc):
        C.wait(sc, tok_cst0)
        t = C.inc(eS, sc.activation(out=cst[:, CS["ln1e4"]:CS["ln1e4"] + 1],
                                    in_=cst[:, CS["tenk"]:CS["tenk"] + 1], func=AF.Ln))
        C.wait(sc, t)
        t = C.inc(eS, sc.activation(out=cvec[:, CV["theta"]:CV["theta"] + 1],
                                    in_=cvec[:, CV["theta"]:CV["theta"] + 1],
                                    func=AF.Exp, scale=cst[:, CS["ln1e4"]:CS["ln1e4"] + 1]))
    C.S(_)
    tok_theta = C.tok(eS)

    def _(s):
        C.wait(s, tok_consts)
        t = C.inc(eS, s.activation(out=cvec[:, CV["A"]:CV["A"] + DSTATE * DCH],
                                   in_=cvec[:, CV["A"]:CV["A"] + DSTATE * DCH], func=AF.Exp))
    C.S(_)
    def _(v):
        C.wait(v, C.tok(eS))
        t = C.inc(eV, v.tensor_scalar(out=cvec[:, CV["A"]:CV["A"] + DSTATE * DCH],
                                      in0=cvec[:, CV["A"]:CV["A"] + DSTATE * DCH],
                                      scalar1=-1.0, scalar2=None, op0=ALU.mult))
    C.V(_)
    tok_A = C.tok(eV)

    def _(t):
        C.wait(t, tok_consts)
        C.wait(t, tok_cst0)
        tk = C.inc(eT, t.matmul(out=pA[:, 0:NEXP], lhsT=cst[0:1, CS["ones_row"]:CS["ones_row"] + 128],
                                rhs=stat[0:1, 0:NEXP], start=True, stop=True))
    C.T(_)
    def _(s):
        C.wait(s, C.tok(eT))
        t = C.inc(eS, s.copy(out=cst[:, CS["rbbc"]:CS["rbbc"] + NEXP], in_=pA[:, 0:NEXP]))
    C.S(_)
    tok_rbbc = C.tok(eS)

    ones_bf = C.es.enter_context(nc.sbuf_tensor("ones_bf", [128, 2], BF16))

    def _(v):
        t = C.inc(eV, v.memset(ones_bf[:], 1.0))
    C.V(_)

    SQOFF = 8192

    def ln_stats(prev_x_tok):
        baseT0 = C.val(eT)
        def _(t):
            C.wait(t, *prev_x_tok)
            for j in range(NFC):
                for cc in range(CCH):
                    mm = t.matmul(out=pSC[0:1, j * TQ:(j + 1) * TQ],
                                  lhsT=cst[:, CS["ones_col"]:CS["ones_col"] + 1],
                                  rhs=x_sb[:, cc * T + j * TQ:cc * T + (j + 1) * TQ],
                                  start=(cc == 0), stop=(cc == CCH - 1))
            C.inc(eT, mm)
        C.T(_)
        baseS = C.val(eS)
        def _(s):
            C.wait(s, *prev_x_tok)
            for cc in range(CCH):
                if cc >= 2:
                    C.wait(s, (eT, baseT0 + 1 + cc - 1))
                C.inc(eS, s.activation(out=p3bf[:, SQOFF + (cc % 2) * T:SQOFF + (cc % 2) * T + T],
                                       in_=x_sb[:, cc * T:(cc + 1) * T], func=AF.Square))
        C.S(_)
        def _(t):
            for cc in range(CCH):
                C.wait(t, (eS, baseS + cc + 1))
                for j in range(NFC):
                    mm = t.matmul(out=pSC[32:33, j * TQ:(j + 1) * TQ],
                                  lhsT=ones_bf[:, 0:1],
                                  rhs=p3bf[:, SQOFF + (cc % 2) * T + j * TQ:SQOFF + (cc % 2) * T + (j + 1) * TQ],
                                  start=(cc == 0), stop=(cc == CCH - 1))
                C.inc(eT, mm)
        C.T(_)
        mm_tok = C.tok(eT)

        def _(v):
            C.wait(v, mm_tok)
            t = C.inc(eV, v.tensor_scalar(out=rowsM[32:33, 0:T], in0=pSC[0:1, 0:T],
                                          scalar1=1.0 / CDIM, scalar2=None, op0=ALU.mult))
            C.wait(v, t)
            t = C.inc(eV, v.tensor_tensor(out=rowsM[0:1, 0:T], in0=rowsM[32:33, 0:T],
                                          in1=rowsM[32:33, 0:T], op=ALU.mult))
            C.wait(v, t)
            st = v.scalar_tensor_tensor(out=pSC[64:65, 0:T], in0=pSC[32:33, 0:T],
                                        scalar=1.0 / CDIM, in1=rowsM[0:1, 0:T],
                                        op0=ALU.mult, op1=ALU.subtract)
            C.inc(eV, st)
        C.V(_)
        vt = C.tok(eV)

        def _(s):
            C.wait(s, vt)
            t = C.inc(eS, s.activation(out=rowsM[0:1, 0:T], in_=pSC[64:65, 0:T], func=AF.Ln,
                                       bias=cst[32:33, CS["epsln"]:CS["epsln"] + 1]))
            C.wait(s, t)
            C.inc(eS, s.activation(out=rowsM[0:1, 0:T], in_=rowsM[0:1, 0:T], func=AF.Exp, scale=-0.5))
        C.S(_)
        st_tok = C.tok(eS)

        def _(v):
            C.wait(v, st_tok)
            C.inc(eV, v.engine_nop())
        C.V(_)
        return C.tok(eV)

    def ln_bcast(t, j, stats_tok, want=True):
        C.wait(t, stats_tok)
        t.matmul(out=pSC[:, 0:TQ], lhsT=cst[32:33, CS["ones_row"]:CS["ones_row"] + 128],
                 rhs=rowsM[32:33, j * TQ:(j + 1) * TQ], start=True, stop=True)
        mm = t.matmul(out=pSC[:, TQ:2 * TQ], lhsT=cst[0:1, CS["ones_row"]:CS["ones_row"] + 128],
                      rhs=rowsM[0:1, j * TQ:(j + 1) * TQ], start=True, stop=True)
        return C.inc(eT, mm)

    def rwkv(seq, prev_toks):
        wtok_box = []
        def _(y):
            C.wait(y, cast_tok["bf_l0f_wr_t"], cast_tok["bf_l0f_wk_t"], cast_tok["bf_l0f_wv_t"])
            C.wait(y, *prev_toks)
            b = C.dma_batch(y)
            C.qinc(b, y.dma_start(out=p2bf[:, 0:8192].rearrange("p (a d) -> p a d", a=CCH),
                                  in_=dram["bf_l0f_wr_t"].rearrange("(a p) d -> p a d", p=128)))
            C.qinc(b, y.dma_start(out=p2bf[:, 8192:16384].rearrange("p (a d) -> p a d", a=CCH),
                                  in_=dram["bf_l0f_wk_t"].rearrange("(a p) d -> p a d", p=128)))
            wtok_box.append(C.qinc(b, y.dma_start(out=p3bf[:, 0:8192].rearrange("p (a d) -> p a d", a=CCH),
                                                  in_=dram["bf_l0f_wv_t"].rearrange("(a p) d -> p a d", p=128))))
        C.Y(_)
        w_tok = wtok_box[0]

        stats_tok = ln_stats(prev_toks)

        xn = pool1
        XROFF = 8192
        XKOFF = 12288
        r32 = pool4
        PCOL = CS["scratch"]

        prev_mix_tok = C.tok(eV)
        for j in range(NFC):
            def _(t, j=j, pm=prev_mix_tok):
                if j == 0:
                    C.wait(t, w_tok)
                else:
                    C.wait(t, pm)
                ln_bcast(t, j, stats_tok)
            C.T(_)
            bc_tok = C.tok(eT)

            def _(v, j=j, bc=bc_tok):
                C.wait(v, bc)
                DOFF = 4096
                tk = None
                for cc in range(CCH):
                    xo = cc * TQ
                    xs = x_sb[:, cc * T + j * TQ: cc * T + (j + 1) * TQ]
                    tk = v.tensor_tensor(out=xn[:, xo:xo + TQ], in0=xs, in1=pSC[:, 0:TQ], op=ALU.subtract)
                C.wait(v, C.inc(eV, tk))
                for cc in range(CCH):
                    xo = cc * TQ
                    tk = v.tensor_tensor(out=xn[:, xo:xo + TQ], in0=xn[:, xo:xo + TQ], in1=pSC[:, TQ:2 * TQ], op=ALU.mult)
                C.wait(v, C.inc(eV, tk))
                for cc in range(CCH):
                    xo = cc * TQ
                    tk = v.tensor_tensor(out=pool1[:, DOFF + xo + 1:DOFF + xo + TQ], in0=xn[:, xo + 1:xo + TQ],
                                    in1=xn[:, xo:xo + TQ - 1], op=ALU.subtract)
                for cc in range(CCH):
                    xo = cc * TQ
                    if j == 0:
                        tk = v.tensor_copy(out=pool1[:, DOFF + xo:DOFF + xo + 1], in_=xn[:, xo:xo + 1])
                    else:
                        tk = v.tensor_tensor(out=pool1[:, DOFF + xo:DOFF + xo + 1], in0=xn[:, xo:xo + 1],
                                        in1=cst[:, PCOL + cc:PCOL + cc + 1], op=ALU.subtract)
                C.wait(v, C.inc(eV, tk))
                for cc in range(CCH):
                    xo = cc * TQ
                    mur = cvec[:, CV["mur"] + cc:CV["mur"] + cc + 1]
                    v.scalar_tensor_tensor(out=p3bf[:, XROFF + xo + 1:XROFF + xo + TQ],
                                           in0=pool1[:, DOFF + xo + 1:DOFF + xo + TQ], scalar=mur,
                                           in1=xn[:, xo:xo + TQ - 1], op0=ALU.mult, op1=ALU.add)
                    if j == 0:
                        v.tensor_scalar(out=p3bf[:, XROFF + xo:XROFF + xo + 1],
                                        in0=pool1[:, DOFF + xo:DOFF + xo + 1], scalar1=mur,
                                        scalar2=None, op0=ALU.mult)
                    else:
                        v.scalar_tensor_tensor(out=p3bf[:, XROFF + xo:XROFF + xo + 1],
                                               in0=pool1[:, DOFF + xo:DOFF + xo + 1], scalar=mur,
                                               in1=cst[:, PCOL + cc:PCOL + cc + 1], op0=ALU.mult, op1=ALU.add)
                for cc in range(CCH):
                    xo = cc * TQ
                    muk = cvec[:, CV["muk"] + cc:CV["muk"] + cc + 1]
                    tk = v.scalar_tensor_tensor(out=p3bf[:, XKOFF + xo + 1:XKOFF + xo + TQ],
                                                in0=pool1[:, DOFF + xo + 1:DOFF + xo + TQ], scalar=muk,
                                                in1=xn[:, xo:xo + TQ - 1], op0=ALU.mult, op1=ALU.add)
                    if j == 0:
                        tk = v.tensor_scalar(out=p3bf[:, XKOFF + xo:XKOFF + xo + 1],
                                             in0=pool1[:, DOFF + xo:DOFF + xo + 1], scalar1=muk,
                                             scalar2=None, op0=ALU.mult)
                    else:
                        tk = v.scalar_tensor_tensor(out=p3bf[:, XKOFF + xo:XKOFF + xo + 1],
                                                    in0=pool1[:, DOFF + xo:DOFF + xo + 1], scalar=muk,
                                                    in1=cst[:, PCOL + cc:PCOL + cc + 1], op0=ALU.mult, op1=ALU.add)
                C.wait(v, C.inc(eV, tk))
                for cc in range(CCH):
                    xo = cc * TQ
                    mm = v.tensor_copy(out=cst[:, PCOL + cc:PCOL + cc + 1], in_=xn[:, xo + TQ - 1:xo + TQ])
                C.inc(eV, mm)
            C.V(_)
            mix_tok = C.tok(eV)

            sig_base = C.val(eS)
            baseT = C.val(eT)
            def _(t, j=j, mt=mix_tok, sig_base=sig_base):
                C.wait(t, mt)
                for dc in range(CCH):
                    if dc >= 2:
                        C.wait(t, (eS, sig_base + dc - 1))
                    pp = pA if dc % 2 == 0 else pB
                    for cc in range(CCH):
                        mm = t.matmul(out=pp[:, 0:TQ],
                                      lhsT=p2bf[:, cc * 1024 + dc * 128: cc * 1024 + (dc + 1) * 128],
                                      rhs=p3bf[:, XROFF + cc * TQ:XROFF + (cc + 1) * TQ],
                                      start=(cc == 0), stop=(cc == CCH - 1))
                    C.inc(eT, mm)
            C.T(_)
            def _(s, baseT=baseT):
                for dc in range(CCH):
                    C.wait(s, (eT, baseT + dc + 1))
                    pp = pA if dc % 2 == 0 else pB
                    C.inc(eS, s.activation(out=r32[:, dc * TQ:(dc + 1) * TQ], in_=pp[:, 0:TQ], func=AF.Sigmoid))
            C.S(_)

            k_base = C.val(eS)
            baseT2 = C.val(eT)
            def _(t, j=j, k_base=k_base):
                for dc in range(CCH):
                    if dc >= 2:
                        C.wait(t, (eS, k_base + 2 * dc - 3))
                    pp = pC if dc % 2 == 0 else pD
                    for cc in range(CCH):
                        mm = t.matmul(out=pp[:, 0:TQ],
                                      lhsT=p2bf[:, 8192 + cc * 1024 + dc * 128: 8192 + cc * 1024 + (dc + 1) * 128],
                                      rhs=p3bf[:, XKOFF + cc * TQ:XKOFF + (cc + 1) * TQ],
                                      start=(cc == 0), stop=(cc == CCH - 1))
                    C.inc(eT, mm)
            C.T(_)
            KROFF = 6144
            def _(s, baseT2=baseT2):
                for dc in range(CCH):
                    C.wait(s, (eT, baseT2 + dc + 1))
                    kr = pool1[:, KROFF + (dc % 2) * TQ:KROFF + (dc % 2) * TQ + TQ]
                    pp = pC if dc % 2 == 0 else pD
                    tk = C.inc(eS, s.activation(out=kr, in_=pp[:, 0:TQ], func=AF.Relu))
                    C.wait(s, tk)
                    C.inc(eS, s.activation(out=p1bf[:, 8192 + dc * TQ:8192 + (dc + 1) * TQ],
                                           in_=kr, func=AF.Square))
            C.S(_)
            kact_tok = C.tok(eS)

            v_base = C.val(eV)
            baseT3 = C.val(eT)
            def _(t, j=j, kt=kact_tok, v_base=v_base):
                C.wait(t, kt)
                for dc in range(CCH):
                    if dc >= 2:
                        C.wait(t, (eV, v_base + 2 * dc - 3))
                    pp = pA if dc % 2 == 0 else pB
                    for cc in range(CCH):
                        mm = t.matmul(out=pp[:, 0:TQ],
                                      lhsT=p3bf[:, cc * 1024 + dc * 128: cc * 1024 + (dc + 1) * 128],
                                      rhs=p1bf[:, 8192 + cc * TQ:8192 + (cc + 1) * TQ],
                                      start=(cc == 0), stop=(cc == CCH - 1))
                    C.inc(eT, mm)
            C.T(_)
            TMOFF = 7168
            def _(v, j=j, baseT3=baseT3):
                for dc in range(CCH):
                    C.wait(v, (eT, baseT3 + dc + 1))
                    pp = pA if dc % 2 == 0 else pB
                    tm = pool1[:, TMOFF + (dc % 2) * TQ:TMOFF + (dc % 2) * TQ + TQ]
                    tk = C.inc(eV, v.tensor_tensor(out=tm, in0=pp[:, 0:TQ],
                                                   in1=r32[:, dc * TQ:(dc + 1) * TQ], op=ALU.mult))
                    C.wait(v, tk)
                    xs = x_sb[:, dc * T + j * TQ: dc * T + (j + 1) * TQ]
                    C.inc(eV, v.scalar_tensor_tensor(out=xs, in0=tm, scalar=0.5, in1=xs,
                                                     op0=ALU.mult, op1=ALU.add))
            C.V(_)
            prev_mix_tok = C.tok(eV)
        return [C.tok(eV)]

    def build_rope(dst, dstoff, prev_tok):
        ldtok_box = []
        def _(y):
            C.wait(y, prev_tok)
            b = C.dma_batch(y)
            C.qinc(b, y.dma_start(out=pool3[0:1, 6144:6144 + T], in_=dram["pos_t"]))
            ldtok_box.append(C.qinc(b, y.dma_start(out=pool2[0:1, 0:T], in_=dram["sgn_t"])))
        C.Y(_)
        ld_tok = ldtok_box[0]

        def _(t):
            C.wait(t, ld_tok)
            for j in range(NFC):
                mm = t.matmul(out=pSC[:, j * TQ:(j + 1) * TQ],
                              lhsT=cst[0:1, CS["ones_row"]:CS["ones_row"] + 128],
                              rhs=pool3[0:1, 6144 + j * TQ:6144 + (j + 1) * TQ],
                              start=True, stop=True)
            C.inc(eT, mm)
        C.T(_)
        pb_tok = C.tok(eT)

        def _(v):
            C.wait(v, pb_tok)
            t = C.inc(eV, v.tensor_scalar(out=cst[:, CS["scratch"]:CS["scratch"] + 1],
                                          in0=cvec[:, CV["theta"]:CV["theta"] + 1],
                                          scalar1=1.0 / (2 * math.pi), scalar2=None, op0=ALU.mult))
            C.wait(v, t)
            v.tensor_scalar(out=pool3[:, 0:T], in0=pSC[:, 0:T], scalar1=cvec[:, CV["theta"]:CV["theta"] + 1],
                            scalar2=None, op0=ALU.mult)
            t = C.inc(eV, v.tensor_scalar(out=pool3[:, 2048:2048 + T], in0=pSC[:, 0:T],
                                          scalar1=cst[:, CS["scratch"]:CS["scratch"] + 1],
                                          scalar2=0.5, op0=ALU.mult, op1=ALU.add))
            C.wait(v, t)
            t = C.inc(eV, v.tensor_copy(out=pool3.bitcast(mybir.dt.int32)[:, 4096:4096 + T],
                                        in_=pool3[:, 2048:2048 + T]))
            C.wait(v, t)
            t = C.inc(eV, v.tensor_copy(out=pool3[:, 2048:2048 + T],
                                        in_=pool3.bitcast(mybir.dt.int32)[:, 4096:4096 + T]))
            C.wait(v, t)
            t = C.inc(eV, v.scalar_tensor_tensor(out=pool3[:, 0:T], in0=pool3[:, 2048:2048 + T],
                                                 scalar=-2 * math.pi, in1=pool3[:, 0:T],
                                                 op0=ALU.mult, op1=ALU.add))
            C.wait(v, t)
            t = C.inc(eV, v.tensor_scalar(out=pool3[:, 2048:2048 + T], in0=pool3[:, 0:T],
                                          scalar1=math.pi, scalar2=None, op0=ALU.is_gt))
            C.wait(v, t)
            t = C.inc(eV, v.scalar_tensor_tensor(out=pool3[:, 0:T], in0=pool3[:, 2048:2048 + T],
                                                 scalar=-2 * math.pi, in1=pool3[:, 0:T],
                                                 op0=ALU.mult, op1=ALU.add))
            C.wait(v, t)
            t = C.inc(eV, v.tensor_scalar(out=pool3[:, 2048:2048 + T], in0=pool3[:, 0:T],
                                          scalar1=-math.pi, scalar2=None, op0=ALU.is_lt))
            C.wait(v, t)
            t = C.inc(eV, v.scalar_tensor_tensor(out=pool3[:, 0:T], in0=pool3[:, 2048:2048 + T],
                                                 scalar=2 * math.pi, in1=pool3[:, 0:T],
                                                 op0=ALU.mult, op1=ALU.add))
            C.wait(v, t)
            t = C.inc(eV, v.tensor_scalar(out=pool3[:, 0:T], in0=pool3[:, 0:T], scalar1=math.pi,
                                          scalar2=-math.pi, op0=ALU.min, op1=ALU.max))
        C.V(_)
        ang_tok = C.tok(eV)

        def _(s):
            C.wait(s, ang_tok)
            s.activation(out=pool3[:, 2048:2048 + T], in_=pool3[:, 0:T], func=AF.Sin)
            t = C.inc(eS, s.activation(out=pool3[:, 4096:4096 + T], in_=pool3[:, 0:T], func=AF.Abs))
            C.wait(s, t)
            t = C.inc(eS, s.activation(out=pool3[:, 4096:4096 + T], in_=pool3[:, 4096:4096 + T], func=AF.Sin,
                                       scale=-1.0, bias=cst[:, CS["halfpi"]:CS["halfpi"] + 1]))
        C.S(_)
        sc_tok = C.tok(eS)

        def _(t):
            C.wait(t, sc_tok)
            for j in range(NFC):
                mm = t.matmul(out=pSC[:, j * TQ:(j + 1) * TQ],
                              lhsT=cst[0:1, CS["ones_row"]:CS["ones_row"] + 128],
                              rhs=pool2[0:1, j * TQ:(j + 1) * TQ],
                              start=True, stop=True)
            C.inc(eT, mm)
        C.T(_)
        sb_tok = C.tok(eT)

        def _(v):
            C.wait(v, sb_tok)
            t = C.inc(eV, v.tensor_tensor(out=dst[:, dstoff:dstoff + T], in0=pool3[:, 2048:2048 + T],
                                          in1=pSC[:, 0:T], op=ALU.mult))
            C.wait(v, t)
            t = C.inc(eV, v.tensor_tensor(out=dst[:, dstoff:dstoff + T], in0=dst[:, dstoff:dstoff + T],
                                          in1=pool3[:, 4096:4096 + T], op=ALU.add))
        C.V(_)
        return C.tok(eV)

    def attention(seq, prev_toks):
        NTT = T // 128
        wtok_box = []
        def _(y):
            C.wait(y, cast_tok["bf_l0a_wq"], cast_tok["bf_l0a_wk"], cast_tok["bf_l0a_wv"])
            C.wait(y, *prev_toks)
            b = C.dma_batch(y)
            C.qinc(b, y.dma_start(out=p4bf[:, 0:8192].rearrange("p (a d) -> p a d", a=CCH),
                                  in_=dram["bf_l0a_wq"].rearrange("(a p) d -> p a d", p=128)))
            C.qinc(b, y.dma_start(out=p2bf[:, 6144:8192].rearrange("p (a d) -> p a d", a=CCH),
                                  in_=dram["bf_l0a_wk"].rearrange("(a p) d -> p a d", p=128)))
            wtok_box.append(C.qinc(b, y.dma_start(out=p2bf[:, 8192:10240].rearrange("p (a d) -> p a d", a=CCH),
                                                  in_=dram["bf_l0a_wv"].rearrange("(a p) d -> p a d", p=128))))
        C.Y(_)
        w_tok = wtok_box[0]

        stats_tok = ln_stats(prev_toks)

        RFOFF = 4096
        rope_tok = build_rope(pool1, RFOFF, stats_tok)

        XGOFF = 0
        SQOFF2 = 4096
        VBOFF = 10240
        KOFF = 12288

        for j in range(NFC):
            def _(t, j=j):
                if j == 0:
                    C.wait(t, rope_tok)
                ln_bcast(t, j, stats_tok)
            C.T(_)
            bc_tok = C.tok(eT)

            def _(v, j=j, bc=bc_tok):
                C.wait(v, bc)
                tk = None
                for cc in range(CCH):
                    xs = x_sb[:, cc * T + j * TQ: cc * T + (j + 1) * TQ]
                    tk = v.tensor_tensor(out=pool1[:, cc * TQ:(cc + 1) * TQ], in0=xs, in1=pSC[:, 0:TQ], op=ALU.subtract)
                C.wait(v, C.inc(eV, tk))
                for cc in range(CCH):
                    tk = v.tensor_tensor(out=pool1[:, cc * TQ:(cc + 1) * TQ], in0=pool1[:, cc * TQ:(cc + 1) * TQ],
                                         in1=pSC[:, TQ:2 * TQ], op=ALU.mult)
                C.wait(v, C.inc(eV, tk))
                for cc in range(CCH):
                    xs = x_sb[:, cc * T + j * TQ: cc * T + (j + 1) * TQ]
                    tk = v.tensor_tensor(out=xs, in0=xs, in1=pool1[:, cc * TQ:(cc + 1) * TQ], op=ALU.add)
                C.inc(eV, tk)
            C.V(_)
            xn_tok = C.tok(eV)

            sq_base = C.val(eS)
            rmsT0 = C.val(eT)
            def _(s, j=j, xt=xn_tok, rmsT0=rmsT0):
                C.wait(s, xt)
                for cc in range(CCH):
                    if cc >= 2:
                        C.wait(s, (eT, rmsT0 + cc - 1))
                    C.inc(eS, s.activation(out=p2bf[:, SQOFF2 + (cc % 2) * TQ:SQOFF2 + (cc % 2) * TQ + TQ],
                                           in_=pool1[:, cc * TQ:(cc + 1) * TQ], func=AF.Square))
            C.S(_)
            def _(t, j=j, sq_base=sq_base):
                for cc in range(CCH):
                    C.wait(t, (eS, sq_base + cc + 1))
                    mm = t.matmul(out=pB[0:1, 0:TQ], lhsT=ones_bf[:, 0:1],
                                  rhs=p2bf[:, SQOFF2 + (cc % 2) * TQ:SQOFF2 + (cc % 2) * TQ + TQ],
                                  start=(cc == 0), stop=(cc == CCH - 1))
                    C.inc(eT, mm)
            C.T(_)
            rmsmm_tok = C.tok(eT)

            def _(s, j=j, mt=rmsmm_tok):
                C.wait(s, mt)
                t = C.inc(eS, s.activation(out=rowsM[0:1, T:T + TQ], in_=pB[0:1, 0:TQ], func=AF.Ln,
                                           scale=1.0 / CDIM, bias=cst[0:1, CS["epsrms"]:CS["epsrms"] + 1]))
                C.wait(s, t)
                C.inc(eS, s.activation(out=rowsM[0:1, T:T + TQ], in_=rowsM[0:1, T:T + TQ],
                                       func=AF.Exp, scale=-0.5))
            C.S(_)
            r2_tok = C.tok(eS)

            def _(t, j=j, rt=r2_tok):
                C.wait(t, rt)
                mm = t.matmul(out=pSC[:, 2 * TQ:3 * TQ], lhsT=cst[0:1, CS["ones_row"]:CS["ones_row"] + 128],
                              rhs=rowsM[0:1, T:T + TQ], start=True, stop=True)
                C.inc(eT, mm)
            C.T(_)
            r2bc_tok = C.tok(eT)

            def _(v, j=j, rt=r2bc_tok):
                C.wait(v, rt)
                tk = None
                for cc in range(CCH):
                    nw = cvec[:, CV["normw"] + cc:CV["normw"] + cc + 1]
                    tk = v.scalar_tensor_tensor(out=p2bf[:, XGOFF + cc * TQ:XGOFF + (cc + 1) * TQ],
                                                in0=pool1[:, cc * TQ:(cc + 1) * TQ], scalar=nw,
                                                in1=pSC[:, 2 * TQ:3 * TQ], op0=ALU.mult, op1=ALU.mult)
                C.inc(eV, tk)
            C.V(_)
            xg_tok = C.tok(eV)

            qe_base = C.val(eV)
            baseT = C.val(eT)
            def _(t, j=j, xg_tok=xg_tok, w_tok=w_tok, qe_base=qe_base):
                C.wait(t, xg_tok)
                if j == 0:
                    C.wait(t, w_tok)
                nmm = 0
                for dc in range(CCH):
                    if nmm >= 2:
                        C.wait(t, (eV, qe_base + nmm - 1))
                    pp = pA if nmm % 2 == 0 else pC
                    for cc in range(CCH):
                        mm = t.matmul(out=pp[:, 0:TQ],
                                      lhsT=p4bf[:, cc * 1024 + dc * 128: cc * 1024 + (dc + 1) * 128],
                                      rhs=p2bf[:, XGOFF + cc * TQ:XGOFF + (cc + 1) * TQ],
                                      start=(cc == 0), stop=(cc == CCH - 1))
                    C.inc(eT, mm)
                    nmm += 1
                for dc in range(2):
                    if nmm >= 2:
                        C.wait(t, (eV, qe_base + nmm - 1))
                    pp = pA if nmm % 2 == 0 else pC
                    for cc in range(CCH):
                        mm = t.matmul(out=pp[:, 0:TQ],
                                      lhsT=p2bf[:, 6144 + cc * 256 + dc * 128: 6144 + cc * 256 + (dc + 1) * 128],
                                      rhs=p2bf[:, XGOFF + cc * TQ:XGOFF + (cc + 1) * TQ],
                                      start=(cc == 0), stop=(cc == CCH - 1))
                    C.inc(eT, mm)
                    nmm += 1
                for dc in range(2):
                    if nmm >= 2:
                        C.wait(t, (eV, qe_base + nmm - 1))
                    pp = pA if nmm % 2 == 0 else pC
                    for cc in range(CCH):
                        mm = t.matmul(out=pp[:, 0:TQ],
                                      lhsT=p2bf[:, 8192 + cc * 256 + dc * 128: 8192 + cc * 256 + (dc + 1) * 128],
                                      rhs=p2bf[:, XGOFF + cc * TQ:XGOFF + (cc + 1) * TQ],
                                      start=(cc == 0), stop=(cc == CCH - 1))
                    C.inc(eT, mm)
                    nmm += 1
            C.T(_)

            def _(v, j=j, baseT=baseT):
                nmm = 0
                for dc in range(CCH):
                    C.wait(v, (eT, baseT + nmm + 1))
                    pp = pA if nmm % 2 == 0 else pC
                    C.inc(eV, v.tensor_tensor(out=p3bf[:, dc * T + j * TQ:dc * T + (j + 1) * TQ],
                                              in0=pp[:, 0:TQ],
                                              in1=pool1[:, RFOFF + j * TQ:RFOFF + (j + 1) * TQ],
                                              op=ALU.mult))
                    nmm += 1
                for dc in range(2):
                    C.wait(v, (eT, baseT + nmm + 1))
                    pp = pA if nmm % 2 == 0 else pC
                    C.inc(eV, v.tensor_tensor(out=p1bf[:, KOFF + dc * T + j * TQ:KOFF + dc * T + (j + 1) * TQ],
                                              in0=pp[:, 0:TQ],
                                              in1=pool1[:, RFOFF + j * TQ:RFOFF + (j + 1) * TQ],
                                              op=ALU.mult))
                    nmm += 1
                for dc in range(2):
                    C.wait(v, (eT, baseT + nmm + 1))
                    pp = pA if nmm % 2 == 0 else pC
                    C.inc(eV, v.tensor_copy(out=p2bf[:, VBOFF + dc * T + j * TQ:VBOFF + dc * T + (j + 1) * TQ],
                                            in_=pp[:, 0:TQ]))
                    nmm += 1
            C.V(_)
        proj_tok = C.tok(eV)
        if _ATT_UPTO == 1:
            return [proj_tok]

        KSW = 4608
        VT = 0
        wotok_box = []
        def _(y):
            C.wait(y, cast_tok["bf_l0a_wo"])
            C.wait(y, proj_tok)
            b = C.dma_batch(y)
            wotok_box.append(C.qinc(b, y.dma_start(out=p4bf[:, 0:8192].rearrange("p (a d) -> p a d", a=CCH),
                                                   in_=dram["bf_l0a_wo"].rearrange("(a p) d -> p a d", p=128))))
        C.Y(_)
        wo_tok = wotok_box[0]

        if _ATT_UPTO == 15:
            ksw_tok = vt_tok = None
        def _(v):
            C.wait(v, proj_tok)
            for c in range(2):
                v.tensor_copy(out=p2bf[0:64, KSW + c * T:KSW + (c + 1) * T],
                              in_=p1bf[64:128, KOFF + c * T:KOFF + (c + 1) * T])
                v.tensor_copy(out=p2bf[64:128, KSW + c * T:KSW + (c + 1) * T],
                              in_=p1bf[0:64, KOFF + c * T:KOFF + (c + 1) * T])
            for tt in range(NTT):
                for kv4 in range(4):
                    o = VT + tt * 260 + kv4 * 65 + 64
                    tk = v.memset(p2bf[:, o:o + 1], 1.0)
            C.inc(eV, tk)
        if _ATT_UPTO not in (15, 17):
            C.V(_)
            ksw_tok = C.tok(eV)
        elif _ATT_UPTO == 17:
            ksw_tok = proj_tok

        tb = C.val(eT)
        ev_base = C.val(eS)
        def _(t, ksw=ksw_tok, ev_base=ev_base):
            C.wait(t, ksw)
            n = 0
            for dc in range(2):
                for tt in range(NTT):
                    if n >= 2:
                        C.wait(t, (eS, ev_base + n - 1))
                    pp = pA if n % 2 == 0 else pC
                    C.inc(eT, t.matmul(out=pp[:, 0:128],
                                       lhsT=p2bf[:, VBOFF + dc * T + tt * 128:VBOFF + dc * T + (tt + 1) * 128],
                                       rhs=ident_bf[:], start=True, stop=True))
                    n += 1
        if _ATT_UPTO not in (15, 16):
            C.T(_)
        def _(s, tb=tb):
            n = 0
            for dc in range(2):
                for tt in range(NTT):
                    C.wait(s, (eT, tb + n + 1))
                    pp = pA if n % 2 == 0 else pC
                    s.activation(out=p2bf[:, VT + tt * 260 + (2 * dc) * 65:VT + tt * 260 + (2 * dc) * 65 + 64],
                                 in_=pp[:, 0:64], func=AF.Copy)
                    C.inc(eS, s.activation(out=p2bf[:, VT + tt * 260 + (2 * dc + 1) * 65:VT + tt * 260 + (2 * dc + 1) * 65 + 64],
                                           in_=pp[:, 64:128], func=AF.Copy))
                    n += 1
        if _ATT_UPTO not in (15, 16):
            C.S(_)
            vt_tok = C.tok(eS)
        elif _ATT_UPTO == 16:
            vt_tok = ksw_tok

        QMAX = 16
        KMAX = 33
        KM16 = 40
        MROW = 56
        sqs_base = C.val(eS)
        redv_base = C.val(eV)
        qmT = C.val(eT)
        ev_tok = {}
        def _(s, pt=proj_tok, qmT=qmT, redv_base=redv_base):
            C.wait(s, pt)
            n = 0
            for cc in range(CCH + 2):
                for fc in range(NFC):
                    if n >= 2:
                        C.wait(s, (eT, qmT + (n - 1) * 2))
                    src = (p3bf[:, cc * T + fc * TQ:cc * T + (fc + 1) * TQ] if cc < CCH else
                           p1bf[:, KOFF + (cc - CCH) * T + fc * TQ:KOFF + (cc - CCH) * T + (fc + 1) * TQ])
                    C.inc(eS, s.activation(out=p1bf[:, (n % 2) * TQ:(n % 2) * TQ + TQ], in_=src, func=AF.Square))
                    n += 1
                C.wait(s, (eT, qmT + 2 * NFC * (cc + 1)))
                if cc >= 1:
                    C.wait(s, (eV, redv_base + 2 * cc))
                s.copy(out=rowsM[0:1, 0:T], in_=pSC[0:1, 0:T])
                ev_tok[cc] = C.inc(eS, s.copy(out=rowsM[32:33, 0:T], in_=pSC[32:33, 0:T]), 1)
        C.S(_)
        def _(t, sqs_base=sqs_base):
            n = 0
            for cc in range(CCH + 2):
                for fc in range(NFC):
                    C.wait(t, (eS, sqs_base + n + 1 + cc))
                    if fc == 0 and cc >= 1:
                        C.wait(t, ev_tok[cc - 1])
                    t.matmul(out=pSC[0:1, fc * TQ:(fc + 1) * TQ], lhsT=onespair[:, 0:1],
                             rhs=p1bf[:, (n % 2) * TQ:(n % 2) * TQ + TQ], start=True, stop=True)
                    mm = t.matmul(out=pSC[32:33, fc * TQ:(fc + 1) * TQ], lhsT=onespair[:, 4:5],
                                  rhs=p1bf[:, (n % 2) * TQ:(n % 2) * TQ + TQ], start=True, stop=True)
                    C.inc(eT, mm, 2)
                    n += 1
        C.T(_)
        def _(v):
            for cc in range(CCH + 2):
                C.wait(v, ev_tok[cc])
                col = QMAX + 2 * cc if cc < CCH else KMAX + 2 * (cc - CCH)
                v.tensor_reduce(out=stat[0:1, col:col + 1], in_=rowsM[0:1, 0:T],
                                axis=mybir.AxisListType.X, op=ALU.max)
                C.inc(eV, v.tensor_reduce(out=stat[0:1, col + 1:col + 2], in_=rowsM[32:33, 0:T],
                                          axis=mybir.AxisListType.X, op=ALU.max), 2)
        C.V(_)
        red_tok = C.tok(eV)

        def _(v, rt=red_tok):
            C.wait(v, rt)
            for h in range(NH):
                t = v.tensor_copy(out=stat[0:1, KM16 + h:KM16 + h + 1],
                                  in_=stat[0:1, KMAX + h % 4:KMAX + h % 4 + 1])
            t = C.inc(eV, t)
            C.wait(v, t)
            t = C.inc(eV, v.tensor_tensor(out=stat[0:1, MROW:MROW + 16], in0=stat[0:1, QMAX:QMAX + 16],
                                          in1=stat[0:1, KM16:KM16 + 16], op=ALU.add))
            C.wait(v, t)
            t = C.inc(eV, v.tensor_scalar(out=stat[0:1, MROW:MROW + 16], in0=stat[0:1, MROW:MROW + 16],
                                          scalar1=1.0 / 16, scalar2=MHAT_CLAMP, op0=ALU.mult, op1=ALU.min))
            C.wait(v, t)
            t = C.inc(eV, v.tensor_scalar(out=stat[0:1, MROW:MROW + 16], in0=stat[0:1, MROW:MROW + 16],
                                          scalar1=-1.0, scalar2=None, op0=ALU.mult))
        C.V(_)
        mrow_tok = C.tok(eV)

        def _(t, mt=mrow_tok):
            C.wait(t, mt)
            C.inc(eT, t.matmul(out=pB[:, 0:16], lhsT=cst[0:1, CS["ones_row"]:CS["ones_row"] + 128],
                               rhs=stat[0:1, MROW:MROW + 16], start=True, stop=True))
        C.T(_)
        mc_T = C.tok(eT)
        def _(s, mt=mc_T):
            C.wait(s, mt)
            C.inc(eS, s.copy(out=cvec[:, CV["mcols"]:CV["mcols"] + NH], in_=pB[:, 0:16]))
        C.S(_)
        mcols_tok = C.tok(eS)
        if _ATT_UPTO in (2, 16, 17):
            return [mcols_tok, vt_tok, ksw_tok]
        if _ATT_UPTO == 15:
            return [mcols_tok]

        pOs = [pA, pB, pC, pD]
        PT0 = 0
        OS0 = 2048
        LR0 = 3072
        NG = (NTT + 3) // 4

        def glast(g):
            return min(4 * g + 3, NTT - 1)

        scores_idx = {}
        pv_idx = {}
        lbc_idx = {}
        n = 0
        for tk in range(NTT):
            n += 1; scores_idx[tk] = n
            n += 1; pv_idx[tk] = n
            g = tk // 4
            if tk == glast(g):
                n += 1; lbc_idx[g] = n
        exp_idx = {}
        evac_idx = {}
        m = 0
        for tk in range(NTT):
            m += 1; exp_idx[tk] = m
            g = tk // 4
            if tk == glast(g):
                evac_idx[g] = m + 3
                m += 3

        for h in range(NH):
            kv = h % 4
            qc = h // 2
            qrow = (h % 2) * 64
            aligned = (h % 2) == (kv % 2)
            if aligned:
                koff, krow = KOFF + (kv // 2) * T, (kv % 2) * 64
                ksrc = p1bf
            else:
                koff, krow = KSW + (kv // 2) * T, (h % 2) * 64
                ksrc = p2bf
            Sbase = C.val(eS)
            Tbase = C.val(eT)
            Vbase = C.val(eV)

            def _(t, h=h, koff=koff, krow=krow, qc=qc, qrow=qrow, ksrc=ksrc,
                  Sbase=Sbase, Tbase=Tbase, Vbase=Vbase):
                if h == 0:
                    C.wait(t, vt_tok, mcols_tok)
                for tk in range(NTT):
                    qj0 = (tk * 128) // TQ
                    if tk > 0:
                        C.wait(t, (eS, Sbase + exp_idx[tk - 1]))
                    for qj in range(qj0, NFC):
                        diag_in_chunk = (qj == qj0)
                        mm = t.matmul(out=pSC[:, qj * TQ:(qj + 1) * TQ],
                                      lhsT=ksrc[krow:krow + 64, koff + tk * 128:koff + (tk + 1) * 128],
                                      rhs=p3bf[qrow:qrow + 64, qc * T + qj * TQ:qc * T + (qj + 1) * TQ],
                                      start=True, stop=not diag_in_chunk)
                        if diag_in_chunk:
                            mm = t.matmul(out=pSC[:, tk * 128:(tk + 1) * 128],
                                          lhsT=ident_bf[:], rhs=mask_bf[:],
                                          start=False, stop=True)
                    C.inc(eT, mm)
                    C.wait(t, (eS, Sbase + exp_idx[tk]))
                    if tk == 0:
                        C.wait(t, (eV, Vbase))
                    first_bank = set()
                    for qt in range(tk, NTT):
                        g = qt // 4
                        st = (tk == 0) and (g not in first_bank)
                        first_bank.add(g)
                        sp = (tk == glast(g)) and (qt == glast(g))
                        mm = t.matmul(out=pOs[g][0:65, (qt % 4) * 128:(qt % 4) * 128 + 128],
                                      lhsT=p2bf[:, VT + tk * 260 + kv * 65:VT + tk * 260 + (kv + 1) * 65],
                                      rhs=p1bf[:, PT0 + (tk % 2) * 2048 + qt * 128:PT0 + (tk % 2) * 2048 + (qt + 1) * 128],
                                      start=st, stop=sp)
                    C.inc(eT, mm)
                    g = tk // 4
                    if tk == glast(g):
                        C.wait(t, (eS, Sbase + evac_idx[g]))
                        gw = min(4, NTT - 4 * g) * 128
                        mm = t.matmul(out=pOs[g][0:64, 0:gw], lhsT=cst[0:1, CS["ones_row"]:CS["ones_row"] + 64],
                                      rhs=pool1[0:1, LR0 + (g % 2) * TQ:LR0 + (g % 2) * TQ + gw],
                                      start=True, stop=True)
                        C.inc(eT, mm)
            C.T(_)

            def _(s, h=h, Sbase=Sbase, Tbase=Tbase, Vbase=Vbase):
                for tk in range(NTT):
                    qj0 = (tk * 128) // TQ
                    C.wait(s, (eT, Tbase + scores_idx[tk]))
                    C.inc(eS, s.activation(out=p1bf[:, PT0 + (tk % 2) * 2048 + qj0 * TQ:PT0 + (tk % 2) * 2048 + T],
                                           in_=pSC[:, qj0 * TQ:T], func=AF.Exp, scale=0.125,
                                           bias=cvec[:, CV["mcols"] + h:CV["mcols"] + h + 1]))
                    g = tk // 4
                    if tk == glast(g):
                        gw = min(4, NTT - 4 * g) * 128
                        C.wait(s, (eT, Tbase + pv_idx[tk]))
                        if g >= 2:
                            C.wait(s, (eV, Vbase + g - 1))
                        C.inc(eS, s.copy(out=pool1[:64, OS0 + (g % 2) * TQ:OS0 + (g % 2) * TQ + gw],
                                         in_=pOs[g][0:64, 0:gw]))
                        tkn = C.inc(eS, s.activation(out=pool1[0:1, LR0 + (g % 2) * TQ:LR0 + (g % 2) * TQ + gw],
                                                     in_=pOs[g][64:65, 0:gw], func=AF.Ln))
                        C.wait(s, tkn)
                        C.inc(eS, s.activation(out=pool1[0:1, LR0 + (g % 2) * TQ:LR0 + (g % 2) * TQ + gw],
                                               in_=pool1[0:1, LR0 + (g % 2) * TQ:LR0 + (g % 2) * TQ + gw],
                                               func=AF.Exp, scale=-1.0))
            C.S(_)

            def _(v, h=h, qc=qc, qrow=qrow, Sbase=Sbase, Tbase=Tbase):
                for g in range(NG):
                    gw = min(4, NTT - 4 * g) * 128
                    C.wait(v, (eT, Tbase + lbc_idx[g]))
                    C.inc(eV, v.tensor_tensor(out=p3bf[qrow:qrow + 64, qc * T + g * TQ:qc * T + g * TQ + gw],
                                              in0=pool1[0:64, OS0 + (g % 2) * TQ:OS0 + (g % 2) * TQ + gw],
                                              in1=pOs[g][0:64, 0:gw], op=ALU.mult))
            C.V(_)
        heads_tok = C.tok(eV)
        if _ATT_UPTO == 3:
            return [heads_tok, wo_tok]

        baseT4 = C.val(eT)
        v4_base = C.val(eV)
        def _(t, ht=heads_tok, wt=wo_tok, v4_base=v4_base):
            C.wait(t, ht, wt)
            n = 0
            for dc in range(CCH):
                for fc in range(NFC):
                    if n >= 2:
                        C.wait(t, (eV, v4_base + n - 1))
                    pp = pA if n % 2 == 0 else pB
                    for cc in range(CCH):
                        mm = t.matmul(out=pp[:, 0:TQ],
                                      lhsT=p4bf[:, cc * 1024 + dc * 128: cc * 1024 + (dc + 1) * 128],
                                      rhs=p3bf[:, cc * T + fc * TQ:cc * T + (fc + 1) * TQ],
                                      start=(cc == 0), stop=(cc == CCH - 1))
                    C.inc(eT, mm)
                    n += 1
        C.T(_)
        def _(v, baseT4=baseT4):
            n = 0
            for dc in range(CCH):
                for fc in range(NFC):
                    C.wait(v, (eT, baseT4 + n + 1))
                    pp = pA if n % 2 == 0 else pB
                    xs = x_sb[:, dc * T + fc * TQ: dc * T + (fc + 1) * TQ]
                    C.inc(eV, v.scalar_tensor_tensor(out=xs, in0=pp[:, 0:TQ], scalar=1.0, in1=xs,
                                                     op0=ALU.mult, op1=ALU.add))
                    n += 1
        C.V(_)
        return [C.tok(eV)]

    def moe(seq, prev_toks):
        NTT = T // 128
        IDF = 4096 + 8 * NTT + 128
        ident_f = pool1[:, IDF:IDF + 128]
        stats_tok = ln_stats(prev_toks)

        prev_b = C.tok(eV)
        for j in range(NFC):
            def _(t, j=j, pb=prev_b):
                C.wait(t, pb)
                ln_bcast(t, j, stats_tok)
            C.T(_)
            bc_tok = C.tok(eT)
            def _(v, j=j, bc=bc_tok):
                C.wait(v, bc)
                tk = None
                for cc in range(CCH):
                    xs = x_sb[:, cc * T + j * TQ: cc * T + (j + 1) * TQ]
                    tk = v.tensor_tensor(out=pool1[:, cc * TQ:(cc + 1) * TQ], in0=xs, in1=pSC[:, 0:TQ], op=ALU.subtract)
                C.wait(v, C.inc(eV, tk))
                for cc in range(CCH):
                    tk = v.tensor_tensor(out=pool1[:, cc * TQ:(cc + 1) * TQ],
                                         in0=pool1[:, cc * TQ:(cc + 1) * TQ],
                                         in1=pSC[:, TQ:2 * TQ], op=ALU.mult)
                C.inc(eV, tk)
            C.V(_)
            t2a = C.tok(eV)
            def _(s, j=j, ta=t2a):
                C.wait(s, ta)
                tk = None
                for cc in range(CCH):
                    tk = s.activation(out=p2bf[:, cc * T + j * TQ:cc * T + (j + 1) * TQ],
                                      in_=pool1[:, cc * TQ:(cc + 1) * TQ], func=AF.Copy)
                C.inc(eS, tk)
            C.S(_)
            t2b = C.tok(eS)
            def _(v, j=j, tb=t2b):
                C.wait(v, tb)
                tk = None
                for cc in range(CCH):
                    tk = v.tensor_tensor(out=p3bf[:, cc * T + j * TQ:cc * T + (j + 1) * TQ],
                                         in0=pool1[:, cc * TQ:(cc + 1) * TQ],
                                         in1=p2bf[:, cc * T + j * TQ:cc * T + (j + 1) * TQ],
                                         op=ALU.subtract)
                C.inc(eV, tk)
            C.V(_)
            prev_b = C.tok(eV)
        xn_tok = C.tok(eV)

        G05 = 4096
        SCR = 4096 + 8 * NTT
        L_, P_, I1_, PM_, I2_, G_ = SCR, SCR + 8, SCR + 16, SCR + 24, SCR + 32, SCR + 40
        (MX, SM, RS, M1, M2, S12, SD, RSD, SW, AC, TT_, F1, T2, RAC) = [SCR + 48 + k for k in range(14)]
        V_PER = 20
        S_PER = 7

        lgT = C.val(eT)
        vbase = C.val(eV)
        sbase = C.val(eS)

        def _(t, xt=xn_tok, vbase=vbase):
            C.wait(t, xt, tok_rw)
            for tt in range(NTT):
                pp = pA if tt % 2 == 0 else pB
                if tt >= 2:
                    C.wait(t, (eV, vbase + V_PER * (tt - 2) + 1))
                for cc in range(CCH):
                    hi = p2bf[:, cc * T + tt * 128:cc * T + (tt + 1) * 128]
                    lo = p3bf[:, cc * T + tt * 128:cc * T + (tt + 1) * 128]
                    t.matmul(out=pp[:, 0:NEXP], lhsT=hi, rhs=rw_sb[:, cc * 8:(cc + 1) * 8],
                             start=(cc == 0), stop=False)
                    t.matmul(out=pp[:, 0:NEXP], lhsT=hi, rhs=rw_lo[:, cc * 8:(cc + 1) * 8],
                             start=False, stop=False)
                    t.matmul(out=pp[:, 0:NEXP], lhsT=lo, rhs=rw_sb[:, cc * 8:(cc + 1) * 8],
                             start=False, stop=False)
                    mm = t.matmul(out=pp[:, 0:NEXP], lhsT=lo, rhs=rw_lo[:, cc * 8:(cc + 1) * 8],
                                  start=False, stop=(cc == CCH - 1))
                C.inc(eT, mm)
        C.T(_)

        def _(s, vbase=vbase):
            p1 = pool1
            for tt in range(NTT):
                C.wait(s, (eV, vbase + V_PER * tt + 3))
                tkn = C.inc(eS, s.activation(out=p1[:, P_:P_ + 8], in_=p1[:, L_:L_ + 8], func=AF.Exp,
                                             bias=p1[:, MX:MX + 1], accum_out=p1[:, SM:SM + 1]))
                C.wait(s, tkn)
                tkn = C.inc(eS, s.activation(out=p1[:, RS:RS + 1], in_=p1[:, SM:SM + 1], func=AF.Ln))
                C.wait(s, tkn)
                C.inc(eS, s.activation(out=p1[:, RS:RS + 1], in_=p1[:, RS:RS + 1], func=AF.Exp, scale=-1.0))
                C.wait(s, (eV, vbase + V_PER * tt + 10))
                tkn = C.inc(eS, s.activation(out=p1[:, RSD:RSD + 1], in_=p1[:, SD:SD + 1], func=AF.Ln))
                C.wait(s, tkn)
                C.inc(eS, s.activation(out=p1[:, RSD:RSD + 1], in_=p1[:, RSD:RSD + 1], func=AF.Exp, scale=-1.0))
                C.wait(s, (eV, vbase + V_PER * tt + 15))
                tkn = C.inc(eS, s.activation(out=p1[:, RAC:RAC + 1], in_=p1[:, AC:AC + 1], func=AF.Ln))
                C.wait(s, tkn)
                C.inc(eS, s.activation(out=p1[:, RAC:RAC + 1], in_=p1[:, RAC:RAC + 1], func=AF.Exp, scale=-1.0))
        C.S(_)

        def _(v, lgT=lgT, sbase=sbase):
            p1 = pool1
            for tt in range(NTT):
                def ed(instr):
                    C.wait(v, C.inc(eV, instr))
                C.wait(v, (eT, lgT + tt + 1))
                C.wait(v, (eS, sbase + S_PER * tt))
                pp = pA if tt % 2 == 0 else pB
                ed(v.tensor_tensor(out=p1[:, L_:L_ + 8], in0=pp[:, 0:NEXP],
                                   in1=cst[:, CS["rbbc"]:CS["rbbc"] + NEXP], op=ALU.add))
                ed(v.tensor_reduce(out=p1[:, MX:MX + 1], in_=p1[:, L_:L_ + 8],
                                   axis=mybir.AxisListType.X, op=ALU.max))
                ed(v.tensor_scalar(out=p1[:, MX:MX + 1], in0=p1[:, MX:MX + 1],
                                   scalar1=-1.0, scalar2=None, op0=ALU.mult))
                C.wait(v, (eS, sbase + S_PER * tt + 3))
                ed(v.tensor_scalar(out=p1[:, P_:P_ + 8], in0=p1[:, P_:P_ + 8],
                                   scalar1=p1[:, RS:RS + 1], scalar2=None, op0=ALU.mult))
                ed(v.tensor_reduce(out=p1[:, M1:M1 + 1], in_=p1[:, P_:P_ + 8],
                                   axis=mybir.AxisListType.X, op=ALU.max))
                ed(v.tensor_scalar(out=p1[:, I1_:I1_ + 8], in0=p1[:, P_:P_ + 8],
                                   scalar1=p1[:, M1:M1 + 1], scalar2=None, op0=ALU.is_ge))
                ed(v.scalar_tensor_tensor(out=p1[:, PM_:PM_ + 8], in0=p1[:, I1_:I1_ + 8],
                                          scalar=-1e30, in1=p1[:, P_:P_ + 8],
                                          op0=ALU.mult, op1=ALU.add))
                ed(v.tensor_reduce(out=p1[:, M2:M2 + 1], in_=p1[:, PM_:PM_ + 8],
                                   axis=mybir.AxisListType.X, op=ALU.max))
                ed(v.tensor_tensor(out=p1[:, S12:S12 + 1], in0=p1[:, M1:M1 + 1],
                                   in1=p1[:, M2:M2 + 1], op=ALU.add))
                ed(v.tensor_scalar(out=p1[:, SD:SD + 1], in0=p1[:, S12:S12 + 1],
                                   scalar1=EPS_MOE, scalar2=None, op0=ALU.add))
                ed(v.tensor_scalar(out=p1[:, I2_:I2_ + 8], in0=p1[:, PM_:PM_ + 8],
                                   scalar1=p1[:, M2:M2 + 1], scalar2=None, op0=ALU.is_ge))
                ed(v.tensor_tensor(out=p1[:, I1_:I1_ + 8], in0=p1[:, I1_:I1_ + 8],
                                   in1=p1[:, I2_:I2_ + 8], op=ALU.add))
                ed(v.tensor_tensor(out=p1[:, G_:G_ + 8], in0=p1[:, I1_:I1_ + 8],
                                   in1=p1[:, P_:P_ + 8], op=ALU.mult))
                C.wait(v, (eS, sbase + S_PER * tt + 5))
                ed(v.tensor_tensor(out=p1[:, SW:SW + 1], in0=p1[:, S12:S12 + 1],
                                   in1=p1[:, RSD:RSD + 1], op=ALU.mult))
                ed(v.tensor_scalar(out=p1[:, AC:AC + 1], in0=p1[:, SW:SW + 1],
                                   scalar1=EPS_MOE, scalar2=None, op0=ALU.max))
                ed(v.tensor_scalar(out=p1[:, TT_:TT_ + 1], in0=p1[:, SW:SW + 1],
                                   scalar1=EPS_MOE, scalar2=1.0, op0=ALU.mult, op1=ALU.add))
                C.wait(v, (eS, sbase + S_PER * tt + 7))
                ed(v.tensor_tensor(out=p1[:, F1:F1 + 1], in0=p1[:, TT_:TT_ + 1],
                                   in1=p1[:, RAC:RAC + 1], op=ALU.mult))
                ed(v.tensor_scalar(out=p1[:, F1:F1 + 1], in0=p1[:, F1:F1 + 1],
                                   scalar1=0.5, scalar2=None, op0=ALU.mult))
                ed(v.tensor_tensor(out=p1[:, T2:T2 + 1], in0=p1[:, RSD:RSD + 1],
                                   in1=p1[:, F1:F1 + 1], op=ALU.mult))
                ed(v.tensor_scalar(out=p1[:, G05 + tt * 8:G05 + (tt + 1) * 8], in0=p1[:, G_:G_ + 8],
                                   scalar1=p1[:, T2:T2 + 1], scalar2=None, op0=ALU.mult))
        C.V(_)
        gate_tok = C.tok(eV)

        last_tok = gate_tok
        for half in range(2):
            NH2 = NTT // 2
            tt0 = half * NH2
            mmT_guard = {}
            vcomb0 = None
            for e in range(NEXP):
                box = []
                def _(y, e=e, half=half, lt=last_tok):
                    C.wait(y, cast_tok["bf_l1m_ew"])
                    if e < 2:
                        C.wait(y, xn_tok)
                        C.wait(y, lt)
                    else:
                        C.wait(y, (eT, mmT_guard[e - 2]))
                    b = C.dma_batch(y)
                    if half == 0 and e == 0:
                        C.qinc(b, y.dma_start(out=ident_f, in_=dram["ident_f"]))
                    dst = p4bf[:, 0:8192] if e % 2 == 0 else p1bf[:, 0:8192]
                    box.append(C.qinc(b, y.dma_start(out=dst.rearrange("p (a d) -> p a d", a=CCH),
                                                     in_=dram["bf_l1m_ew"][e].rearrange("(a p) d -> p a d", p=128))))
                C.Y(_)
                wt = box[0]

                if vcomb0 is None:
                    vcomb0 = C.val(eV)
                baseT = C.val(eT)
                def _(t, e=e, half=half, wt=wt, vcomb0=vcomb0, tt0=tt0, NH2=NH2):
                    C.wait(t, wt)
                    wsrc = p4bf if e % 2 == 0 else p1bf
                    for i in range(NH2):
                        tt = tt0 + i
                        n = e * NH2 + i
                        if n >= 2:
                            C.wait(t, (eV, vcomb0 + n - 1))
                        pp0 = pA if n % 2 == 0 else pC
                        pp1 = pB if n % 2 == 0 else pD
                        for cc in range(CCH):
                            t.matmul(out=pp0[:, 0:512],
                                     lhsT=p2bf[:, cc * T + tt * 128:cc * T + (tt + 1) * 128],
                                     rhs=wsrc[:, cc * 1024:cc * 1024 + 512],
                                     start=(cc == 0), stop=(cc == CCH - 1))
                            mm = t.matmul(out=pp1[:, 0:512],
                                          lhsT=p2bf[:, cc * T + tt * 128:cc * T + (tt + 1) * 128],
                                          rhs=wsrc[:, cc * 1024 + 512:cc * 1024 + 1024],
                                          start=(cc == 0), stop=(cc == CCH - 1))
                        C.inc(eT, mm)
                C.T(_)
                mmT_guard[e] = C.val(eT)

                def _(v, e=e, baseT=baseT, tt0=tt0, NH2=NH2, vcomb0=vcomb0):
                    for i in range(NH2):
                        tt = tt0 + i
                        C.wait(v, (eT, baseT + i + 1))
                        if e > 0:
                            C.wait(v, (eV, vcomb0 + (e - 1) * NH2 + i + 1))
                        n = e * NH2 + i
                        pp0 = pA if n % 2 == 0 else pC
                        pp1 = pB if n % 2 == 0 else pD
                        gcol = pool1[:, G05 + tt * 8 + e:G05 + tt * 8 + e + 1]
                        do = i * 1024
                        if e == 0:
                            v.tensor_scalar(out=pool3[:, do:do + 512], in0=pp0[:, 0:512],
                                            scalar1=gcol, scalar2=None, op0=ALU.mult)
                            mm = v.tensor_scalar(out=pool3[:, do + 512:do + 1024], in0=pp1[:, 0:512],
                                                 scalar1=gcol, scalar2=None, op0=ALU.mult)
                        else:
                            v.scalar_tensor_tensor(out=pool3[:, do:do + 512], in0=pp0[:, 0:512],
                                                   scalar=gcol, in1=pool3[:, do:do + 512],
                                                   op0=ALU.mult, op1=ALU.add)
                            mm = v.scalar_tensor_tensor(out=pool3[:, do + 512:do + 1024], in0=pp1[:, 0:512],
                                                        scalar=gcol, in1=pool3[:, do + 512:do + 1024],
                                                        op0=ALU.mult, op1=ALU.add)
                        C.inc(eV, mm)
                C.V(_)
            comb_tok = C.tok(eV)

            trT = C.val(eT)
            trV = C.val(eV)
            def _(t, ct=comb_tok, trV=trV, NH2=NH2):
                C.wait(t, ct)
                n = 0
                for i in range(NH2):
                    for blk in range(CCH):
                        if n >= 2:
                            C.wait(t, (eV, trV + n - 1))
                        po = (n % 2) * 512
                        C.inc(eT, t.matmul(out=pSC[:, po:po + 128],
                                           lhsT=pool3[:, i * 1024 + blk * 128:i * 1024 + (blk + 1) * 128],
                                           rhs=ident_f[:], start=True, stop=True))
                        n += 1
            C.T(_)
            def _(v, trT=trT, tt0=tt0, NH2=NH2):
                n = 0
                for i in range(NH2):
                    tt = tt0 + i
                    for blk in range(CCH):
                        C.wait(v, (eT, trT + n + 1))
                        po = (n % 2) * 512
                        xs = x_sb[:, blk * T + tt * 128:blk * T + (tt + 1) * 128]
                        C.inc(eV, v.scalar_tensor_tensor(out=xs, in0=pSC[:, po:po + 128],
                                                         scalar=1.0, in1=xs, op0=ALU.mult, op1=ALU.add))
                        n += 1
            C.V(_)
            last_tok = C.tok(eV)
        return [last_tok]

    def mamba(seq, prev_toks):
        TQ5 = min(512, T)
        TC5 = T // TQ5
        cvbf = cvec.bitcast(BF16)
        XPOFF = 1536
        DTOFF = 3072
        HCAR = CV["hcarry"]
        xsbbf = x_sb.bitcast(BF16)
        DUO = 0
        YO = 16 * TQ5
        UBF = 0
        XMS = 0
        CVS = 2 * (TQ5 + 3)
        SRS = CVS + 2 * TQ5
        BCO = SRS + TQ5
        CCO = BCO + TQ5
        DT2 = 2 * (CCO + TQ5)
        MB = 0

        stats_tok = ln_stats(prev_toks)

        prev_b = C.tok(eV)
        for j in range(NFC):
            def _(t, j=j, pb=prev_b):
                C.wait(t, pb)
                ln_bcast(t, j, stats_tok)
            C.T(_)
            bc_tok = C.tok(eT)
            def _(v, j=j, bc=bc_tok):
                C.wait(v, bc)
                tk = None
                for cc in range(CCH):
                    xs = x_sb[:, cc * T + j * TQ: cc * T + (j + 1) * TQ]
                    tk = v.tensor_tensor(out=pool1[:, cc * TQ:(cc + 1) * TQ], in0=xs, in1=pSC[:, 0:TQ], op=ALU.subtract)
                C.wait(v, C.inc(eV, tk))
                for cc in range(CCH):
                    tk = v.tensor_tensor(out=p2bf[:, cc * T + j * TQ:cc * T + (j + 1) * TQ],
                                         in0=pool1[:, cc * TQ:(cc + 1) * TQ],
                                         in1=pSC[:, TQ:2 * TQ], op=ALU.mult)
                C.inc(eV, tk)
            C.V(_)
            prev_b = C.tok(eV)
        xn_tok = C.tok(eV)

        spill_box = []
        def _(y):
            C.wait(y, xn_tok)
            C.wait(y, cast_tok["bf_l1a_xproj_t"], cast_tok["bf_l1a_dtproj_t"])
            b = C.dma_batch(y)
            C.qinc(b, y.dma_start(out=spill_x.rearrange("a p t -> p a t"),
                                  in_=x_sb[:, 0:CCH * T].rearrange("p (a t) -> p a t", a=CCH)))
            C.qinc(b, y.dma_start(out=cvbf[:, XPOFF:XPOFF + 16 * 96].rearrange("p (a d) -> p a d", a=16),
                                  in_=dram["bf_l1a_xproj_t"].rearrange("(a p) d -> p a d", p=128)))
            spill_box.append(C.qinc(b, y.dma_start(
                out=cvbf[:, DTOFF:DTOFF + 1024],
                in_=dram["bf_l1a_dtproj_t"])))
        C.Y(_)
        spill_tok = spill_box[0]

        def _(v, st=spill_tok):
            C.wait(v, st)
            v.memset(cvec[:, HCAR:HCAR + 256], 0.0)
            C.inc(eV, v.memset(cvec[:, CV["convtail"]:CV["convtail"] + 48], 0.0))
        C.V(_)
        init_tok = C.tok(eV)

        phase_tok = init_tok
        a_vg = {}
        for tc in range(TC5):
            ts0 = tc * TQ5
            ld_toks = []
            for dc in range(DCH):
                box = []
                def _(y, dc=dc, tc=tc, pt=phase_tok):
                    if dc < 2:
                        C.wait(y, pt)
                        C.wait(y, cast_tok["bf_l1a_inproj_ra"])
                    else:
                        C.wait(y, (eT, a_guard[dc - 2]))
                    b = C.dma_batch(y)
                    box.append(C.qinc(b, y.dma_start(
                        out=w5a[:, (dc % 2) * 1024:(dc % 2) * 1024 + 1024].rearrange("p (a d) -> p a d", a=CCH),
                        in_=dram["bf_l1a_inproj_ra"][:, dc, 0:128].rearrange("(a p) d -> p a d", p=128))))
                C.Y(_)
                ld_toks.append(box[0])
                if dc == 0 and tc == 0:
                    a_guard = {}

                baseT = C.val(eT)
                def _(t, dc=dc, tc=tc, wt=ld_toks[dc], pt=phase_tok):
                    C.wait(t, wt)
                    if dc == 0:
                        C.wait(t, pt)
                    pp = pA if dc % 2 == 0 else pB
                    if dc >= 2:
                        C.wait(t, (eS, a_sguard[dc - 2]))
                    for cc in range(CCH):
                        mm = t.matmul(out=pp[:, 0:TQ5],
                                      lhsT=w5a[:, (dc % 2) * 1024 + cc * 128:(dc % 2) * 1024 + (cc + 1) * 128],
                                      rhs=p2bf[:, cc * T + ts0:cc * T + ts0 + TQ5],
                                      start=(cc == 0), stop=(cc == CCH - 1))
                    C.inc(eT, mm)
                C.T(_)
                a_guard[dc] = C.val(eT)
                if dc == 0:
                    a_sguard = {}

                sA = C.val(eS)
                vA = C.val(eV)
                def _(s, dc=dc, tc=tc, baseT=baseT, vA=vA):
                    C.wait(s, (eT, baseT + 1))
                    k = tc * 16 + dc
                    if k >= 2:
                        C.wait(s, a_vg[k - 2])
                    pp = pA if dc % 2 == 0 else pB
                    C.inc(eS, s.copy(out=pool4[:, XMS + (dc % 2) * (TQ5 + 3) + 3:XMS + (dc % 2) * (TQ5 + 3) + 3 + TQ5],
                                     in_=pp[:, 0:TQ5]))
                C.S(_)
                a_sguard[dc] = C.val(eS)

                def _(v, dc=dc, tc=tc, sA=sA):
                    C.wait(v, (eS, sA + 1))
                    xb = XMS + (dc % 2) * (TQ5 + 3)
                    cb = CVS + (dc % 2) * TQ5
                    tl = CV["convtail"] + dc * 3
                    def ed(instr):
                        C.wait(v, C.inc(eV, instr))
                    ed(v.tensor_copy(out=pool4[:, xb:xb + 3], in_=cvec[:, tl:tl + 3]))
                    ed(v.tensor_scalar(out=pool4[:, cb:cb + TQ5], in0=pool4[:, xb:xb + TQ5],
                                       scalar1=cvec[:, CV["convw"] + dc * 4:CV["convw"] + dc * 4 + 1],
                                       scalar2=None, op0=ALU.mult))
                    for jj in range(1, 4):
                        ed(v.scalar_tensor_tensor(out=pool4[:, cb:cb + TQ5], in0=pool4[:, xb + jj:xb + jj + TQ5],
                                                  scalar=cvec[:, CV["convw"] + dc * 4 + jj:CV["convw"] + dc * 4 + jj + 1],
                                                  in1=pool4[:, cb:cb + TQ5], op0=ALU.mult, op1=ALU.add))
                    ed(v.tensor_copy(out=cvec[:, tl:tl + 3], in_=pool4[:, xb + TQ5:xb + TQ5 + 3]))
                C.V(_)
                a_vg[tc * 16 + dc] = C.tok(eV)

                sA2 = C.val(eS)
                def _(s, dc=dc, vt=C.tok(eV)):
                    C.wait(s, vt)
                    tkn = C.inc(eS, s.activation(out=pool1[:, dc * TQ5:(dc + 1) * TQ5],
                                                 in_=pool4[:, CVS + (dc % 2) * TQ5:CVS + (dc % 2) * TQ5 + TQ5],
                                                 func=AF.Silu,
                                                 bias=cvec[:, CV["convb"] + dc:CV["convb"] + dc + 1]))
                    C.wait(s, tkn)
                    C.inc(eS, s.copy(out=xsbbf[:, UBF + dc * TQ5:UBF + (dc + 1) * TQ5],
                                     in_=pool1[:, dc * TQ5:(dc + 1) * TQ5]))
                C.S(_)
                def _(v, dc=dc, st=C.tok(eS)):
                    C.wait(v, st)
                    C.inc(eV, v.tensor_scalar(out=x_sb[:, YO + dc * TQ5:YO + (dc + 1) * TQ5],
                                              in0=pool1[:, dc * TQ5:(dc + 1) * TQ5],
                                              scalar1=cvec[:, CV["D"] + dc:CV["D"] + dc + 1],
                                              scalar2=None, op0=ALU.mult))
                C.V(_)
            A_tok = [C.tok(eV), C.tok(eS), C.tok(eT)]

            def _(t, tc=tc, at=A_tok):
                C.wait(t, *at)
                for cc2 in range(DCH):
                    t.matmul(out=pC[0:64, 0:TQ5],
                             lhsT=cvbf[:, XPOFF + cc2 * 96:XPOFF + cc2 * 96 + 64],
                             rhs=xsbbf[:, UBF + cc2 * TQ5:UBF + (cc2 + 1) * TQ5],
                             start=(cc2 == 0), stop=(cc2 == DCH - 1))
                for cc2 in range(DCH):
                    t.matmul(out=pD[0:16, 0:TQ5],
                             lhsT=cvbf[:, XPOFF + cc2 * 96 + 64:XPOFF + cc2 * 96 + 80],
                             rhs=xsbbf[:, UBF + cc2 * TQ5:UBF + (cc2 + 1) * TQ5],
                             start=(cc2 == 0), stop=(cc2 == DCH - 1))
                for cc2 in range(DCH):
                    mm = t.matmul(out=pSC[0:16, 0:TQ5],
                                  lhsT=cvbf[:, XPOFF + cc2 * 96 + 80:XPOFF + cc2 * 96 + 96],
                                  rhs=xsbbf[:, UBF + cc2 * TQ5:UBF + (cc2 + 1) * TQ5],
                                  start=(cc2 == 0), stop=(cc2 == DCH - 1))
                C.inc(eT, mm)
            C.T(_)
            B_T = C.tok(eT)

            def _(s, tc=tc, bt=B_T):
                C.wait(s, bt)
                s.copy(out=p4bf[0:64, DT2:DT2 + TQ5], in_=pC[0:64, 0:TQ5])
                s.copy(out=p4bf[64:128, DT2:DT2 + TQ5], in_=pC[0:64, 0:TQ5])
                s.copy(out=pool4[0:16, BCO:BCO + TQ5], in_=pD[0:16, 0:TQ5])
                C.inc(eS, s.copy(out=pool4[0:16, CCO:CCO + TQ5], in_=pSC[0:16, 0:TQ5]))
            C.S(_)
            B_tok = C.tok(eS)

            cT0 = C.val(eT)
            cS0 = C.val(eS)
            cV0 = C.val(eV)
            def _(t, tc=tc, bt=B_tok, cS0=cS0):
                C.wait(t, bt)
                for dc in range(DCH):
                    pp = pA if dc % 2 == 0 else pB
                    if dc >= 2:
                        C.wait(t, (eS, cS0 + (dc - 2) * 4 + 4))
                    mm = t.matmul(out=pp[:, 0:TQ5],
                                  lhsT=cvbf[(dc % 2) * 64:(dc % 2) * 64 + 64,
                                            DTOFF + (dc // 2) * 128:DTOFF + (dc // 2) * 128 + 128],
                                  rhs=p4bf[(dc % 2) * 64:(dc % 2) * 64 + 64, DT2:DT2 + TQ5],
                                  start=True, stop=True)
                    C.inc(eT, mm)
            C.T(_)

            def _(s, tc=tc, cT0=cT0, cV0=cV0):
                for dc in range(DCH):
                    C.wait(s, (eT, cT0 + dc + 1))
                    pp = pA if dc % 2 == 0 else pB
                    dtb = cvec[:, CV["dtb"] + dc:CV["dtb"] + dc + 1]
                    aslot = pool4[:, CVS + (dc % 2) * TQ5:CVS + (dc % 2) * TQ5 + TQ5]
                    if dc >= 2:
                        C.wait(s, (eV, cV0 + (dc - 2) * 2 + 2))
                    tkn = C.inc(eS, s.activation(out=aslot, in_=pp[:, 0:TQ5], func=AF.Abs, bias=dtb))
                    C.wait(s, tkn)
                    tkn = C.inc(eS, s.activation(out=aslot, in_=aslot, func=AF.Exp, scale=-1.0))
                    C.wait(s, tkn)
                    tkn = C.inc(eS, s.activation(out=aslot, in_=aslot, func=AF.Ln,
                                                 bias=cst[:, CS["ones_col"]:CS["ones_col"] + 1]))
                    C.wait(s, tkn)
                    C.inc(eS, s.activation(out=pool3[:, dc * TQ5:(dc + 1) * TQ5], in_=pp[:, 0:TQ5],
                                           func=AF.Relu, bias=dtb))
            C.S(_)

            def _(v, tc=tc, cS0=cS0):
                for dc in range(DCH):
                    C.wait(v, (eS, cS0 + dc * 4 + 4))
                    aslot = pool4[:, CVS + (dc % 2) * TQ5:CVS + (dc % 2) * TQ5 + TQ5]
                    tkn = C.inc(eV, v.tensor_tensor(out=pool3[:, dc * TQ5:(dc + 1) * TQ5],
                                                    in0=pool3[:, dc * TQ5:(dc + 1) * TQ5],
                                                    in1=aslot, op=ALU.add))
                    C.wait(v, tkn)
                    C.inc(eV, v.tensor_tensor(out=x_sb[:, DUO + dc * TQ5:DUO + (dc + 1) * TQ5],
                                              in0=pool3[:, dc * TQ5:(dc + 1) * TQ5],
                                              in1=pool1[:, dc * TQ5:(dc + 1) * TQ5], op=ALU.mult))
            C.V(_)
            C_tok = [C.tok(eV), C.tok(eS)]

            DA0, DB0, H0, HC0 = 0, 2 * TQ5, 4 * TQ5, 6 * TQ5
            dT0 = C.val(eT)
            dS0 = C.val(eS)
            dV0 = C.val(eV)
            dG0 = C.val(eG)

            def _(t, tc=tc, ct=C_tok, dV0=dV0):
                C.wait(t, *ct)
                for n in range(DSTATE):
                    ppB = pA if n % 2 == 0 else pB
                    ppC = pC if n % 2 == 0 else pD
                    if n >= 2:
                        C.wait(t, (eV, dV0 + (n - 2 + 1) * 64))
                    t.matmul(out=ppB[:, 0:TQ5], lhsT=sel16[:, n * 128:(n + 1) * 128],
                             rhs=pool4[0:16, BCO:BCO + TQ5], start=True, stop=True)
                    mm = t.matmul(out=ppC[:, 0:TQ5], lhsT=sel16[:, n * 128:(n + 1) * 128],
                                  rhs=pool4[0:16, CCO:CCO + TQ5], start=True, stop=True)
                    C.inc(eT, mm)
            C.T(_)

            def _(s, tc=tc, ct=C_tok, dV0=dV0):
                C.wait(s, *ct)
                for n in range(DSTATE):
                    for dc in range(DCH):
                        k = n * 16 + dc
                        if k >= 2:
                            C.wait(s, (eV, dV0 + (k - 2) * 4 + 2))
                        C.inc(eS, s.activation(out=pool1[:, DA0 + (k % 2) * TQ5:DA0 + (k % 2) * TQ5 + TQ5],
                                               in_=pool3[:, dc * TQ5:(dc + 1) * TQ5], func=AF.Exp,
                                               scale=cvec[:, CV["A"] + dc * 16 + n:CV["A"] + dc * 16 + n + 1]))
            C.S(_)

            def _(v, tc=tc, dT0=dT0, dS0=dS0, dG0=dG0):
                for n in range(DSTATE):
                    C.wait(v, (eT, dT0 + n + 1))
                    ppB = pA if n % 2 == 0 else pB
                    ppC = pC if n % 2 == 0 else pD
                    for dc in range(DCH):
                        k = n * 16 + dc
                        def ed(instr):
                            C.wait(v, C.inc(eV, instr))
                        C.wait(v, (eS, dS0 + k + 1))
                        if k >= 2:
                            C.wait(v, (eG, dG0 + k - 1))
                        ed(v.tensor_tensor(out=pool1[:, DB0 + (k % 2) * TQ5:DB0 + (k % 2) * TQ5 + TQ5],
                                           in0=x_sb[:, DUO + dc * TQ5:DUO + (dc + 1) * TQ5],
                                           in1=ppB[:, 0:TQ5], op=ALU.mult))
                        ed(v.tensor_tensor_scan(out=pool1[:, H0 + (k % 2) * TQ5:H0 + (k % 2) * TQ5 + TQ5],
                                                data0=pool1[:, DA0 + (k % 2) * TQ5:DA0 + (k % 2) * TQ5 + TQ5],
                                                data1=pool1[:, DB0 + (k % 2) * TQ5:DB0 + (k % 2) * TQ5 + TQ5],
                                                initial=cvec[:, HCAR + k:HCAR + k + 1],
                                                op0=ALU.mult, op1=ALU.add))
                        C.inc(eV, v.tensor_copy(out=cvec[:, HCAR + k:HCAR + k + 1],
                                                in_=pool1[:, H0 + (k % 2) * TQ5 + TQ5 - 1:H0 + (k % 2) * TQ5 + TQ5]))
                        C.inc(eV, v.tensor_tensor(out=pool1[:, HC0 + (k % 2) * TQ5:HC0 + (k % 2) * TQ5 + TQ5],
                                                  in0=pool1[:, H0 + (k % 2) * TQ5:H0 + (k % 2) * TQ5 + TQ5],
                                                  in1=ppC[:, 0:TQ5], op=ALU.mult))
            C.V(_)

            def _(g, tc=tc, dV0=dV0):
                for n in range(DSTATE):
                    for dc in range(DCH):
                        k = n * 16 + dc
                        C.wait(g, (eV, dV0 + k * 4 + 4))
                        C.inc(eG, g.tensor_tensor(out=x_sb[:, YO + dc * TQ5:YO + (dc + 1) * TQ5],
                                                  in0=x_sb[:, YO + dc * TQ5:YO + (dc + 1) * TQ5],
                                                  in1=pool1[:, HC0 + (k % 2) * TQ5:HC0 + (k % 2) * TQ5 + TQ5],
                                                  op=ALU.add))
            C.G(_)
            D_tok = [C.tok(eG), C.tok(eV), C.tok(eS), C.tok(eT)]

            e_ld = []
            for dc in range(DCH):
                box = []
                def _(y, dc=dc, tc=tc, dt=D_tok):
                    if dc < 2:
                        C.wait(y, *dt)
                    else:
                        C.wait(y, (eT, e_guard[dc - 2]))
                    b = C.dma_batch(y)
                    box.append(C.qinc(b, y.dma_start(
                        out=w5a[:, (dc % 2) * 1024:(dc % 2) * 1024 + 1024].rearrange("p (a d) -> p a d", a=CCH),
                        in_=dram["bf_l1a_inproj_ra"][:, dc, 128:256].rearrange("(a p) d -> p a d", p=128))))
                C.Y(_)
                e_ld.append(box[0])
                if dc == 0:
                    e_guard = {}

                eT0 = C.val(eT)
                def _(t, dc=dc, tc=tc, wt=e_ld[dc], dt=D_tok, eS0v=None):
                    C.wait(t, wt)
                    if dc == 0:
                        C.wait(t, *dt)
                    if dc >= 2:
                        C.wait(t, (eS, e_sguard[dc - 2]))
                    pp = pA if dc % 2 == 0 else pB
                    for cc in range(CCH):
                        mm = t.matmul(out=pp[:, 0:TQ5],
                                      lhsT=w5a[:, (dc % 2) * 1024 + cc * 128:(dc % 2) * 1024 + (cc + 1) * 128],
                                      rhs=p2bf[:, cc * T + ts0:cc * T + ts0 + TQ5],
                                      start=(cc == 0), stop=(cc == CCH - 1))
                    C.inc(eT, mm)
                C.T(_)
                e_guard[dc] = C.val(eT)
                if dc == 0:
                    e_sguard = {}

                eV0 = C.val(eV)
                def _(s, dc=dc, eT0=eT0, eV0=eV0):
                    C.wait(s, (eT, eT0 + 1))
                    if dc >= 1:
                        C.wait(s, (eV, e_vguard[dc - 1]))
                    pp = pA if dc % 2 == 0 else pB
                    C.inc(eS, s.activation(out=pool4[:, SRS:SRS + TQ5],
                                           in_=pp[:, 0:TQ5], func=AF.Silu))
                C.S(_)
                e_sguard[dc] = C.val(eS)
                if dc == 0:
                    e_vguard = {}
                sE = C.val(eS)
                def _(v, dc=dc, sE=sE):
                    C.wait(v, (eS, sE))
                    C.inc(eV, v.tensor_tensor(out=p3bf[:, dc * TQ5:(dc + 1) * TQ5],
                                              in0=x_sb[:, YO + dc * TQ5:YO + (dc + 1) * TQ5],
                                              in1=pool4[:, SRS:SRS + TQ5],
                                              op=ALU.mult))
                C.V(_)
                e_vguard[dc] = C.val(eV)
            E_tok = [C.tok(eV), C.tok(eS)]

            f_ld = []
            for mc in range(CCH):
                box = []
                def _(y, mc=mc, tc=tc, et=E_tok):
                    if mc == 0:
                        C.wait(y, *et)
                        C.wait(y, cast_tok["bf_l1a_outproj_t"])
                    else:
                        C.wait(y, (eT, f_guard[mc - 1]))
                    b = C.dma_batch(y)
                    box.append(C.qinc(b, y.dma_start(
                        out=w5a[:, 0:2048].rearrange("p (a d) -> p a d", a=DCH),
                        in_=dram["bf_l1a_outproj_t"][:, mc * 128:(mc + 1) * 128].rearrange("(a p) d -> p a d", p=128))))
                C.Y(_)
                f_ld.append(box[0])
                if mc == 0:
                    f_guard = {}
                fT0 = C.val(eT)
                def _(t, mc=mc, wt=f_ld[mc], et=E_tok, fS=None):
                    C.wait(t, wt)
                    if mc == 0:
                        C.wait(t, *et)
                    if mc >= 2:
                        C.wait(t, (eS, f_sguard[mc - 2]))
                    pp = pA if mc % 2 == 0 else pB
                    for dc in range(DCH):
                        mm = t.matmul(out=pp[:, 0:TQ5],
                                      lhsT=w5a[:, dc * 128:(dc + 1) * 128],
                                      rhs=p3bf[:, dc * TQ5:(dc + 1) * TQ5],
                                      start=(dc == 0), stop=(dc == DCH - 1))
                    C.inc(eT, mm)
                C.T(_)
                f_guard[mc] = C.val(eT)
                if mc == 0:
                    f_sguard = {}
                def _(s, mc=mc, fT0=fT0):
                    C.wait(s, (eT, fT0 + 1))
                    C.inc(eS, s.copy(out=pool4[:, MB + mc * TQ5:MB + (mc + 1) * TQ5], in_=pA[:, 0:TQ5] if mc % 2 == 0 else pB[:, 0:TQ5]))
                C.S(_)
                f_sguard[mc] = C.val(eS)
            F_tok = C.tok(eS)

            mbox = []
            def _(y, tc=tc, ft=F_tok):
                C.wait(y, ft)
                b = C.dma_batch(y)
                mbox.append(C.qinc(b, y.dma_start(
                    out=spill_mc[:, :, ts0:ts0 + TQ5].rearrange("a p t -> p a t"),
                    in_=pool4[:, MB:MB + CCH * TQ5].rearrange("p (a t) -> p a t", a=CCH))))
            C.Y(_)
            phase_tok = mbox[0]

        scan_done = phase_tok
        return mamba_ffn(seq, scan_done)

    def mamba_ffn(seq, scan_done):
        xsbbf5 = x_sb.bitcast(BF16)
        TF = min(1024, T)
        NTC2 = T // TF
        NHALF = max(1, TF // 512)
        SW = min(512, TF)
        SC_C = 16
        SC_Q = 48
        scnt = [0]
        _sq_junk_toks = {}

        last = scan_done
        for tc2 in range(NTC2):
            tf0 = tc2 * TF
            box = []
            def _(y, tc2=tc2, lt=last):
                C.wait(y, lt)
                b = C.dma_batch(y)
                box.append(C.qinc(b, y.dma_start(
                    out=pool1[:, 0:CCH * TF].rearrange("p (a t) -> p a t", a=CCH),
                    in_=spill_mc[:, :, tf0:tf0 + TF].rearrange("a p t -> p a t"))))
            C.Y(_)
            m_tok = box[0]

            def _(s, mt=m_tok):
                C.wait(s, mt)
                for cc in range(CCH):
                    t = C.inc(eS, s.copy(out=p2bf[:, cc * TF:(cc + 1) * TF],
                                         in_=pool1[:, cc * TF:(cc + 1) * TF]))
            C.S(_)
            mbf_tok = C.tok(eS)

            h1_ld = []
            for hc in range(32):
                bx = []
                def _(y, hc=hc, mt=mbf_tok):
                    if hc < 2:
                        C.wait(y, mt)
                        C.wait(y, cast_tok["bf_l1a_w1t"])
                    else:
                        C.wait(y, (eT, h1_guard[hc - 2]))
                    b = C.dma_batch(y)
                    bx.append(C.qinc(b, y.dma_start(
                        out=w5a[:, (hc % 2) * 1024:(hc % 2) * 1024 + 1024].rearrange("p (a d) -> p a d", a=CCH),
                        in_=dram["bf_l1a_w1t"][:, hc * 128:(hc + 1) * 128].rearrange("(a p) d -> p a d", p=128))))
                C.Y(_)
                h1_ld.append(bx[0])
                if hc == 0:
                    h1_guard = {}
                    h1_sg = {}
                hT0 = C.val(eT)
                def _(t, hc=hc, wt=h1_ld[hc], mt=mbf_tok):
                    C.wait(t, wt)
                    if hc == 0:
                        C.wait(t, mt)
                    if hc >= 2:
                        C.wait(t, (eS, h1_sg[hc - 2]))
                    for hf in range(NHALF):
                        pp = (pA, pB)[hf] if hc % 2 == 0 else (pC, pD)[hf]
                        for cc in range(CCH):
                            mm = t.matmul(out=pp[:, 0:SW],
                                          lhsT=w5a[:, (hc % 2) * 1024 + cc * 128:(hc % 2) * 1024 + (cc + 1) * 128],
                                          rhs=p2bf[:, cc * TF + hf * SW:cc * TF + hf * SW + SW],
                                          start=(cc == 0), stop=(cc == CCH - 1))
                        C.inc(eT, mm)
                C.T(_)
                h1_guard[hc] = C.val(eT)
                def _(s, hc=hc, hT0=hT0):
                    for hf in range(NHALF):
                        C.wait(s, (eT, hT0 + hf + 1))
                        pp = (pA, pB)[hf] if hc % 2 == 0 else (pC, pD)[hf]
                        C.inc(eS, s.activation(out=xsbbf5[:, hc * TF + hf * SW:hc * TF + hf * SW + SW],
                                               in_=pp[:, 0:SW], func=AF.Gelu,
                                               bias=cvec[:, CV["b1"] + hc:CV["b1"] + hc + 1]))
                C.S(_)
                h1_sg[hc] = C.val(eS)
            h1_tok = C.tok(eS)

            f_prevT = None
            f_vg = {}
            for mc in range(CCH):
                for sl in range(2):
                    bx = []
                    def _(y, mc=mc, sl=sl, ht=h1_tok, fp=f_prevT):
                        if mc == 0 and sl == 0:
                            C.wait(y, ht)
                            C.wait(y, cast_tok["bf_l1a_w2t"])
                        else:
                            C.wait(y, (eT, fp))
                        b = C.dma_batch(y)
                        bx.append(C.qinc(b, y.dma_start(
                            out=w5a[:, 0:2048].rearrange("p (a d) -> p a d", a=16),
                            in_=dram["bf_l1a_w2t"][sl * 2048:(sl + 1) * 2048, mc * 128:(mc + 1) * 128].rearrange("(a p) d -> p a d", p=128))))
                    C.Y(_)
                    def _(t, mc=mc, sl=sl, wt=bx[0]):
                        C.wait(t, wt)
                        if mc >= 1 and sl == 0:
                            C.wait(t, f_vg[mc - 1])
                        for hh in range(16):
                            hc = sl * 16 + hh
                            for hf in range(NHALF):
                                pp = (pC, pD)[hf]
                                mm = t.matmul(out=pp[:, 0:SW],
                                              lhsT=w5a[:, hh * 128:(hh + 1) * 128],
                                              rhs=xsbbf5[:, hc * TF + hf * SW:hc * TF + hf * SW + SW],
                                              start=(hc == 0), stop=(hc == 31))
                            C.inc(eT, mm)
                    C.T(_)
                    f_prevT = C.val(eT)
                fT_end = C.val(eT)
                def _(v, mc=mc, tc2=tc2, fT_end=fT_end):
                    C.wait(v, (eT, fT_end))
                    for hf in range(NHALF):
                        pp = (pC, pD)[hf]
                        idx = (tc2 * CCH + mc) * NHALF + hf
                        tkn = C.inc(eV, v.scalar_tensor_tensor(
                            out=pool3[:, mc * TF + hf * SW:mc * TF + hf * SW + SW],
                            in0=pp[:, 0:SW], scalar=cvec[:, CV["b2"] + mc:CV["b2"] + mc + 1],
                            in1=pool1[:, mc * TF + hf * SW:mc * TF + hf * SW + SW],
                            op0=ALU.add, op1=ALU.add,
                            accum_out=stat[:, SC_C + idx:SC_C + idx + 1]))
                        scnt[0] = max(scnt[0], idx + 1)
                C.V(_)
                f_vg[mc] = C.tok(eV)
                def _(s, mc=mc, tc2=tc2, vt=f_vg[mc]):
                    C.wait(s, vt)
                    for hf in range(NHALF):
                        idx = (tc2 * CCH + mc) * NHALF + hf
                        if idx - 8 in _sq_junk_toks:
                            C.wait(s, _sq_junk_toks[idx - 8])
                        _sq_junk_toks[idx] = C.inc(eS, s.activation(
                            out=pool4[:, ((mc * NHALF + hf) % 8) * SW:((mc * NHALF + hf) % 8) * SW + SW],
                            in_=pool3[:, mc * TF + hf * SW:mc * TF + hf * SW + SW],
                            func=AF.Square,
                            accum_out=stat[:, SC_Q + idx:SC_Q + idx + 1]))
                C.S(_)
            csq_tok = C.tok(eS)
            cv_tok = C.tok(eV)

            box2 = []
            def _(y, tc2=tc2, ct=cv_tok, st=csq_tok):
                C.wait(y, ct, st)
                b = C.dma_batch(y)
                box2.append(C.qinc(b, y.dma_start(
                    out=spill_mc[:, :, tf0:tf0 + TF].rearrange("a p t -> p a t"),
                    in_=pool3[:, 0:CCH * TF].rearrange("p (a t) -> p a t", a=CCH))))
            C.Y(_)
            last = box2[0]

        NS = scnt[0]
        def _(t, lt=last, ct=C.tok(eS), cvt=C.tok(eV), NS=NS):
            C.wait(t, ct, cvt)
            t.matmul(out=pA[0:1, 0:NS], lhsT=cst[:, CS["ones_col"]:CS["ones_col"] + 1],
                     rhs=stat[:, SC_C:SC_C + NS], start=True, stop=True)
            mm = t.matmul(out=pA[0:1, 32:32 + NS], lhsT=cst[:, CS["ones_col"]:CS["ones_col"] + 1],
                          rhs=stat[:, SC_Q:SC_Q + NS], start=True, stop=True)
            C.inc(eT, mm)
        C.T(_)
        gs_T = C.tok(eT)
        def _(v, gt=gs_T, NS=NS):
            def ed(instr):
                C.wait(v, C.inc(eV, instr))
            C.wait(v, gt)
            ed(v.tensor_reduce(out=stat[0:1, 80:81], in_=pA[0:1, 0:NS],
                               axis=mybir.AxisListType.X, op=ALU.add))
            ed(v.tensor_reduce(out=stat[0:1, 81:82], in_=pA[0:1, 32:32 + NS],
                               axis=mybir.AxisListType.X, op=ALU.add))
            ed(v.tensor_scalar(out=stat[0:1, 82:83], in0=stat[0:1, 80:81],
                               scalar1=1.0 / (T * CDIM), scalar2=None, op0=ALU.mult))
            ed(v.tensor_tensor(out=stat[0:1, 83:84], in0=stat[0:1, 82:83],
                               in1=stat[0:1, 82:83], op=ALU.mult))
            ed(v.scalar_tensor_tensor(out=stat[0:1, 83:84], in0=stat[0:1, 81:82],
                                      scalar=1.0 / (T * CDIM), in1=stat[0:1, 83:84],
                                      op0=ALU.mult, op1=ALU.subtract))
        C.V(_)
        gv_tok = C.tok(eV)
        def _(s, gt=gv_tok):
            C.wait(s, gt)
            tkn = C.inc(eS, s.activation(out=stat[0:1, 84:85], in_=stat[0:1, 83:84], func=AF.Ln,
                                         bias=cst[0:1, CS["epsln"]:CS["epsln"] + 1]))
            C.wait(s, tkn)
            C.inc(eS, s.activation(out=stat[0:1, 84:85], in_=stat[0:1, 84:85], func=AF.Exp, scale=-0.5))
        C.S(_)
        gr_tok = C.tok(eS)
        def _(v, gt=gr_tok):
            C.wait(v, gt)
            C.inc(eV, v.scalar_tensor_tensor(out=stat[0:1, 85:86], in0=stat[0:1, 82:83],
                                             scalar=-1.0, in1=stat[0:1, 84:85],
                                             op0=ALU.mult, op1=ALU.mult))
        C.V(_)
        gn_tok = C.tok(eV)
        def _(t, gt=gn_tok):
            C.wait(t, gt)
            C.inc(eT, t.matmul(out=pB[:, 0:2], lhsT=cst[0:1, CS["ones_row"]:CS["ones_row"] + 128],
                               rhs=stat[0:1, 84:86], start=True, stop=True))
        C.T(_)
        gb_T = C.tok(eT)
        def _(s, gt=gb_T):
            C.wait(s, gt)
            C.inc(eS, s.copy(out=cvec[:, CV["gstat"]:CV["gstat"] + 2], in_=pB[:, 0:2]))
        C.S(_)
        gcol_tok = C.tok(eS)

        bx = []
        def _(y, lt=last):
            C.wait(y, lt)
            b = C.dma_batch(y)
            bx.append(C.qinc(b, y.dma_start(out=x_sb[:, 0:CCH * T].rearrange("p (a t) -> p a t", a=CCH),
                                            in_=spill_x.rearrange("a p t -> p a t"))))
        C.Y(_)
        xl_tok = bx[0]

        fin_tok = None
        for j in range(NFC):
            bx2 = []
            def _(y, j=j, ft=fin_tok):
                if ft is not None:
                    C.wait(y, ft)
                b = C.dma_batch(y)
                bx2.append(C.qinc(b, y.dma_start(
                    out=pool3[:, 0:CCH * TQ].rearrange("p (a t) -> p a t", a=CCH),
                    in_=spill_mc[:, :, j * TQ:(j + 1) * TQ].rearrange("a p t -> p a t"))))
            C.Y(_)
            c_tok = bx2[0]
            def _(v, j=j, ct=c_tok, xt=xl_tok, gt=gcol_tok):
                C.wait(v, ct, xt, gt)
                tk = None
                for cc in range(CCH):
                    xs = x_sb[:, cc * T + j * TQ: cc * T + (j + 1) * TQ]
                    tk = v.scalar_tensor_tensor(out=xs, in0=pool3[:, cc * TQ:(cc + 1) * TQ],
                                                scalar=cvec[:, CV["gstat"]:CV["gstat"] + 1],
                                                in1=xs, op0=ALU.mult, op1=ALU.add)
                C.wait(v, C.inc(eV, tk))
                for cc in range(CCH):
                    xs = x_sb[:, cc * T + j * TQ: cc * T + (j + 1) * TQ]
                    tk = v.tensor_scalar(out=xs, in0=xs,
                                         scalar1=cvec[:, CV["gstat"] + 1:CV["gstat"] + 2],
                                         scalar2=None, op0=ALU.add)
                C.inc(eV, tk)
            C.V(_)
            fin_tok = C.tok(eV)
        return [fin_tok]

    def load_x(seq, prev_toks):
        box = []
        def _(y):
            C.wait(y, *prev_toks)
            b = C.dma_batch(y)
            box.append(C.qinc(b, y.dma_start(out=x_sb[:, 0:CCH * T].rearrange("p (a t) -> p a t", a=CCH),
                                             in_=x_in[seq].rearrange("a p t -> p a t"))))
        C.Y(_)
        return [box[0]]

    def store_x(seq, prev_toks):
        box = []
        def _(y):
            C.wait(y, *prev_toks)
            b = C.dma_batch(y)
            box.append(C.qinc(b, y.dma_start(out=out[seq].rearrange("a p t -> p a t"),
                                             in_=x_sb[:, 0:CCH * T].rearrange("p (a t) -> p a t", a=CCH))))
        C.Y(_)
        return [box[0]]

    last = [C.tok(eV)]
    store_toks = []
    for seq in range(nseq):
        last = load_x(seq, last + [tok_theta, tok_A, tok_rbbc, tok_rw])
        if 9 in stages:
            last = [ln_stats(last)]
        if 7 in stages or 8 in stages:
            stats_tok8 = ln_stats(last)
            def _(t, st=stats_tok8):
                ln_bcast(t, 0, st)
            C.T(_)
            bc8 = C.tok(eT)
            def _(v, bc=bc8):
                C.wait(v, bc)
                tk = None
                if 8 in stages:
                    for cc in range(CCH):
                        xs = x_sb[:, cc * T: cc * T + TQ]
                        tk = v.tensor_tensor(out=pool1[:, cc * TQ:(cc + 1) * TQ], in0=xs, in1=pSC[:, 0:TQ], op=ALU.subtract)
                    C.wait(v, C.inc(eV, tk))
                    for cc in range(CCH):
                        xs = x_sb[:, cc * T: cc * T + TQ]
                        tk = v.tensor_tensor(out=xs, in0=pool1[:, cc * TQ:(cc + 1) * TQ], in1=pSC[:, TQ:2 * TQ], op=ALU.mult)
                    C.inc(eV, tk)
                else:
                    C.inc(eV, v.tensor_copy(out=pool1[:, 0:TQ], in_=pSC[:, 0:TQ]))
            C.V(_)
            last = [C.tok(eV)]
        if 1 in stages:
            last = rwkv(seq, last)
        if 2 in stages:
            last = attention(seq, last)
        if 3 in stages:
            last = rwkv(seq, last)
        if 4 in stages:
            last = moe(seq, last)
        if 5 in stages:
            last = mamba(seq, last)
        if 6 in stages:
            last = moe(seq, last)
        store_toks += store_x(seq, last)
        last = store_toks[-1:]

    def _(y):
        C.wait(y, *store_toks)
    C.Y(_)


_cache = {}


def _host_prep(inputs, T=TFULL):
    f32 = np.float32
    m = {}
    m["l0f_wr_t"] = np.ascontiguousarray(inputs["L0_ffn_W_r"].T)
    m["l0f_wk_t"] = np.ascontiguousarray(inputs["L0_ffn_W_k"].T)
    m["l0f_wv_t"] = np.ascontiguousarray(inputs["L0_ffn_W_v"].T)
    m["l0a_wq"] = np.ascontiguousarray(inputs["L0_attn_w_q"])
    m["l0a_wk"] = np.ascontiguousarray(inputs["L0_attn_w_k"])
    m["l0a_wv"] = np.ascontiguousarray(inputs["L0_attn_w_v"])
    m["l0a_wo"] = np.ascontiguousarray(inputs["L0_attn_w_o"])
    ip = inputs["L1_attn_in_proj_w"]
    ipt = ip.T
    ra = np.empty((CDIM, DCH, 256), dtype=f32)
    for dc in range(DCH):
        ra[:, dc, 0:128] = ipt[:, dc * 128:(dc + 1) * 128]
        ra[:, dc, 128:256] = ipt[:, DIN + dc * 128:DIN + (dc + 1) * 128]
    m["l1a_inproj_ra"] = np.ascontiguousarray(ra)
    m["l1a_xproj_t"] = np.ascontiguousarray(inputs["L1_attn_x_proj_w"].T)
    dtt = inputs["L1_attn_dt_proj_w"].T
    dtp = np.empty((128, 1024), dtype=f32)
    for h2 in range(2):
        for g in range(8):
            dtp[h2 * 64:(h2 + 1) * 64, g * 128:(g + 1) * 128] = dtt[:, (2 * g + h2) * 128:(2 * g + h2 + 1) * 128]
    m["l1a_dtproj_t"] = dtp
    m["l1a_outproj_t"] = np.ascontiguousarray(inputs["L1_attn_out_proj_w"].T)
    m["l1a_w1t"] = np.ascontiguousarray(inputs["L1_attn_ffn_w1"].T)
    m["l1a_w2t"] = np.ascontiguousarray(inputs["L1_attn_ffn_w2"].T)
    m["l1m_rwt"] = np.ascontiguousarray(inputs["L1_ffn_router_w"].T)
    m["l1m_ew"] = np.ascontiguousarray(inputs["L1_ffn_expert_weights"])
    def col(a, nch):
        return np.ascontiguousarray(a.reshape(nch, 128, -1).transpose(1, 0, 2).reshape(128, -1)).astype(f32)
    m["l0f_mur"] = col(inputs["L0_ffn_mu_r"], CCH)
    m["l0f_muk"] = col(inputs["L0_ffn_mu_k"], CCH)
    m["l0a_normw"] = col(inputs["L0_attn_norm_w"], CCH)
    m["l1a_convw"] = col(inputs["L1_attn_conv_w"].reshape(DIN, 4), DCH)
    m["l1a_convb"] = col(inputs["L1_attn_conv_b"], DCH)
    m["l1a_dtb"] = col(inputs["L1_attn_dt_proj_b"], DCH)
    m["l1a_alog"] = col(inputs["L1_attn_A_log"], DCH)
    m["l1a_dvec"] = col(inputs["L1_attn_D"], DCH)
    m["l1a_b1"] = col(inputs["L1_attn_ffn_b1"], 32)
    m["l1a_b2"] = col(inputs["L1_attn_ffn_b2"], CCH)
    m["l1m_rb"] = inputs["L1_ffn_router_b"].reshape(1, NEXP).astype(f32)
    m["pos_t"] = np.arange(T, dtype=np.float64).reshape(1, T).astype(f32)
    dh = np.arange(DH)
    dhexp = (-(dh // 2) / 32.0).astype(f32)
    m["dhexp"] = np.concatenate([dhexp, dhexp]).reshape(128, 1)
    sgn = np.where(np.arange(T) % 2 == 0, -1.0, 1.0).astype(f32).reshape(1, T)
    m["sgn_t"] = sgn
    import ml_dtypes
    m["ident_bf"] = np.eye(128, dtype=ml_dtypes.bfloat16)
    m["ident_f"] = np.eye(128, dtype=f32)
    mask = np.zeros((128, 128), dtype=np.float64)
    for a in range(128):
        mask[a, :a] = MASK_NEG
    m["maskT_bf"] = mask.astype(ml_dtypes.bfloat16)
    sel = np.zeros((16, 2048), dtype=f32)
    for n in range(16):
        sel[n, n * 128:(n + 1) * 128] = 1.0
    m["sel16"] = sel
    op = np.zeros((128, 8), dtype=ml_dtypes.bfloat16)
    op[0:64, 0:4] = 1
    op[64:128, 4:8] = 1
    m["onespair"] = op
    return m


def kernel(**inputs):
    T = TFULL
    x = np.asarray(inputs["L0_attn_x"])
    B = x.shape[0]
    shared = _host_prep(inputs, T)
    key = ("full", T)
    if key not in _cache:
        _cache[key] = build_program(T=T, nseq=NSEQ, stages=_STAGES)
    nc = _cache[key]
    in_maps = []
    for core in range(NCORES):
        im = dict(shared)
        xs = x[core * NSEQ:(core + 1) * NSEQ]
        xt = np.ascontiguousarray(xs.transpose(0, 2, 1))
        im["x"] = xt.reshape(NSEQ, CCH, 128, T)
        in_maps.append(im)
    res = run_bass_kernel_spmd(nc, in_maps, core_ids=list(range(NCORES)))
    outs = []
    for core in range(NCORES):
        o = res.results[core]["out"]
        o = o.reshape(NSEQ, CDIM, T).transpose(0, 2, 1)
        outs.append(o)
    return np.ascontiguousarray(np.concatenate(outs, axis=0)).astype(np.float32)


_STAGES = (1, 2, 3, 4, 5, 6)
_ATT_UPTO = 99


# version 09
# speedup vs baseline: 1.0293x; 1.0092x over previous
import math
import numpy as np
from contextlib import ExitStack

import concourse.bass as bass
import concourse.mybir as mybir
from concourse.bass_utils import run_bass_kernel_spmd

F32 = mybir.dt.float32
BF16 = mybir.dt.bfloat16
AF = mybir.ActivationFunctionType
ALU = mybir.AluOpType

NCORES = 8
NSEQ = 2
TFULL = 2048
CDIM = 1024
CCH = 8
NH = 16
KVH = 4
DH = 64
DIN = 2048
DCH = 16
DSTATE = 16
DTRANK = 64
NEXP = 8
DFF = 4096
EPS_LN = 1e-5
EPS_RMS = 1.1920929e-07
EPS_MOE = 1e-10
MASK_NEG = -1e30
MHAT_CLAMP = 30.0


class Ctx:
    NQP = 12

    def __init__(self, nc, blk, es):
        self.nc = nc
        self.blk = blk
        self.es = es
        self.sems = {}
        for name in ("eT", "eV", "eS", "eG", "qGa", "qGb", "qGc"):
            self.sems[name] = [es.enter_context(nc.semaphore(name)), 0]
        self.qpool = [es.enter_context(nc.semaphore(f"qP{i}")) for i in range(self.NQP)]
        self.qcnt = [0] * self.NQP
        self.qused = [False] * self.NQP
        self.qi = 0

    def dma_batch(self, y):
        i = self.qi % self.NQP
        self.qi += 1
        if self.qused[i]:
            y.wait_ge(self.qpool[i], self.qcnt[i])
        self.qused[i] = True
        return i

    def qinc(self, i, inst):
        self.qcnt[i] += 16
        inst.then_inc(self.qpool[i], 16)
        return ("Q", i, self.qcnt[i])

    def inc(self, name, inst, k=1):
        h = self.sems[name]
        inst.then_inc(h[0], k)
        h[1] += k
        return (name, h[1])

    def tok(self, name, delta=0):
        return (name, self.sems[name][1] + delta)

    def val(self, name):
        return self.sems[name][1]

    def wait(self, eng, *tokens):
        for tok in tokens:
            if tok[0] == "Q":
                _, i, v = tok
                if v > 0:
                    eng.wait_ge(self.qpool[i], v)
            else:
                name, v = tok
                if v > 0:
                    eng.wait_ge(self.sems[name][0], v)

    def T(self, fn): self.blk.tensor(fn)
    def V(self, fn): self.blk.vector(fn)
    def S(self, fn): self.blk.scalar(fn)
    def G(self, fn): self.blk.gpsimd(fn)
    def Y(self, fn): self.blk.sync(fn)


def build_program(T=TFULL, nseq=NSEQ, stages=(1, 2, 3, 4, 5, 6)):
    nc = bass.Bass("TRN2", target_bir_lowering=False)
    TC = T // 512 if T >= 512 else 1
    TQ = min(512, T)
    NFC = T // TQ
    NTT = T // 128

    dram = {}

    def din(name, shape, dt=F32):
        dram[name] = nc.dram_tensor(name, shape, dt, kind="ExternalInput").ap()
        return dram[name]

    def dint(name, shape, dt):
        dram[name] = nc.dram_tensor(name, shape, dt, kind="Internal").ap()
        return dram[name]

    x_in = din("x", [nseq, CCH, 128, T])
    out = nc.dram_tensor("out", [nseq, CCH, 128, T], F32, kind="ExternalOutput").ap()

    w_f32 = {}
    def wdecl(name, shape):
        w_f32[name] = din(name, shape)
        return w_f32[name]

    wdecl("l0f_wr_t", [CDIM, CDIM])
    wdecl("l0f_wk_t", [CDIM, CDIM])
    wdecl("l0f_wv_t", [CDIM, CDIM])
    wdecl("l0a_wq", [CDIM, CDIM])
    wdecl("l0a_wk", [CDIM, 256])
    wdecl("l0a_wv", [CDIM, 256])
    wdecl("l0a_wo", [CDIM, CDIM])
    wdecl("l1a_inproj_ra", [CDIM, DCH, 256])
    wdecl("l1a_xproj_t", [DIN, 96])
    wdecl("l1a_dtproj_t", [128, 1024])
    wdecl("l1a_outproj_t", [DIN, CDIM])
    wdecl("l1a_w1t", [CDIM, DFF])
    wdecl("l1a_w2t", [DFF, CDIM])
    wdecl("l1m_rwt", [CDIM, NEXP])
    wdecl("l1m_ew", [NEXP, CDIM, CDIM])
    din("l0f_mur", [128, CCH])
    din("l0f_muk", [128, CCH])
    din("l0a_normw", [128, CCH])
    din("l1a_convw", [128, 4 * DCH])
    din("l1a_convb", [128, DCH])
    din("l1a_dtb", [128, DCH])
    din("l1a_alog", [128, DSTATE * DCH])
    din("l1a_dvec", [128, DCH])
    din("l1a_b1", [128, 32])
    din("l1a_b2", [128, CCH])
    din("l1m_rb", [1, NEXP])
    din("pos_t", [1, T])
    din("dhexp", [128, 1])
    din("sgn_t", [1, T])
    din("ident_bf", [128, 128], BF16)
    din("ident_f", [128, 128])
    din("maskT_bf", [128, 128], BF16)
    din("sel16", [16, 2048])
    din("onespair", [128, 8], BF16)

    wbf = {}
    for name in w_f32:
        wbf[name] = dint("bf_" + name, list(w_f32[name].shape), BF16)

    spill_x = dint("spill_x", [CCH, 128, T], F32)
    spill_mc = dint("spill_mc", [CCH, 128, T], F32)

    es = ExitStack()

    def sb(name, shape, dt):
        return es.enter_context(nc.sbuf_tensor(name, shape, dt))

    def ps(name, shape, dt=F32):
        return es.enter_context(nc.psum_tensor(name, shape, dt))

    x_sb = sb("x_sb", [128, max(CCH * T, 32 * min(512, T))], F32)
    pool1 = sb("pool1", [128, 8192], F32)
    pool2 = sb("pool2", [128, 8192], F32)
    pool3 = sb("pool3", [128, 8192], F32)
    pool4 = sb("pool4", [128, 4096], F32)
    cvec = sb("cvec", [128, 2048], F32)
    stat = sb("stat", [128, 96], F32)
    rowsM = sb("rowsM", [65, T + 512], F32)
    cst = sb("cst", [128, 160], F32)
    ident_bf = sb("s_ident_bf", [128, 128], BF16)
    rw_sb = sb("s_rw", [128, 64], BF16)
    rw_lo = sb("s_rwlo", [128, 64], BF16)
    w5a = sb("s_w5a", [128, 2048], BF16)

    mask_bf = sb("s_maskT_bf", [128, 128], BF16)
    onespair = sb("s_onespair", [128, 8], BF16)
    sel16 = sb("s_sel16", [16, 2048], F32)

    pSC = ps("pSC", [128, 2048], F32)
    pA = ps("pA", [128, 512], F32)
    pB = ps("pB", [128, 512], F32)
    pC = ps("pC", [128, 512], F32)
    pD = ps("pD", [128, 512], F32)

    blk = es.enter_context(nc.Block())
    C = Ctx(nc, blk, es)

    with es:
        emit_all(C, nc, dram, wbf, x_in, out, spill_x, spill_mc,
                 x_sb, pool1, pool2, pool3, pool4,
                 cvec, stat, rowsM, cst, ident_bf, rw_sb, rw_lo, w5a, mask_bf, onespair, sel16,
                 pSC, pA, pB, pC, pD, T, nseq, stages)
    return nc


def emit_all(C, nc, dram, wbf, x_in, out, spill_x, spill_mc,
             x_sb, pool1, pool2, pool3, pool4,
             cvec, stat, rowsM, cst, ident_bf, rw_sb, rw_lo, w5a, mask_bf, onespair, sel16,
             pSC, pA, pB, pC, pD, T, nseq, stages):
    p1bf = pool1.bitcast(BF16)
    p2bf = pool2.bitcast(BF16)
    p3bf = pool3.bitcast(BF16)
    p4bf = pool4.bitcast(BF16)
    arA = pool1
    arB = pool2
    arC = pool3
    TQ = min(512, T)
    NFC = T // TQ

    CV = {}
    off = 0
    def cv(name, width):
        nonlocal off
        CV[name] = off
        off += width
        return CV[name]
    cv("mur", CCH)
    cv("muk", CCH)
    cv("normw", CCH)
    cv("convw", 4 * DCH)
    cv("convb", DCH)
    cv("dtb", DCH)
    cv("A", DSTATE * DCH)
    cv("D", DCH)
    cv("b1", 32)
    cv("b2", CCH)
    cv("mcols", NH)
    cv("theta", 1)
    cv("hcarry", 256)
    cv("convtail", 48)
    cv("gstat", 2)
    assert off <= 768

    CS = {}
    off2 = 0
    def cs(name, width):
        nonlocal off2
        CS[name] = off2
        off2 += width
        return CS[name]
    cs("ones_col", 1)
    cs("ones_row", 128)
    cs("zero_col", 1)
    cs("halfpi", 1)
    cs("ln1e4", 1)
    cs("tenk", 1)
    cs("rbbc", NEXP)
    cs("epsln", 1)
    cs("epsrms", 1)
    cs("scratch", 16)
    assert off2 <= 160

    eT = "eT"; eV = "eV"; eS = "eS"; eG = "eG"

    cast_tok = {}

    GROUP_A = ("l0f_wr_t", "l0f_wk_t", "l0f_wv_t", "l0a_wq", "l0a_wk", "l0a_wv", "l0a_wo")
    GROUP_B = ("l1a_inproj_ra", "l1a_xproj_t", "l1a_dtproj_t", "l1a_outproj_t",
               "l1a_w1t", "l1a_w2t", "l1m_rwt")

    def _(g):
        for name in GROUP_A:
            C.inc("qGa", g.dma_start(out=wbf[name], in_=dram[name]), 16)
        for name in GROUP_B:
            C.inc("qGb", g.dma_start(out=wbf[name], in_=dram[name]), 16)
        for e in range(NEXP):
            C.inc("qGc", g.dma_start(out=wbf["l1m_ew"][e], in_=dram["l1m_ew"][e]), 16)
    C.G(_)
    for name in GROUP_A:
        cast_tok["bf_" + name] = C.tok("qGa")
    for name in GROUP_B:
        cast_tok["bf_" + name] = C.tok("qGb")
    cast_tok["bf_l1m_ew"] = C.tok("qGc")

    def _(y):
        b = C.dma_batch(y)
        for dst, src in [
            (cvec[:, CV["mur"]:CV["mur"] + CCH], dram["l0f_mur"]),
            (cvec[:, CV["muk"]:CV["muk"] + CCH], dram["l0f_muk"]),
            (cvec[:, CV["normw"]:CV["normw"] + CCH], dram["l0a_normw"]),
            (cvec[:, CV["convw"]:CV["convw"] + 4 * DCH], dram["l1a_convw"]),
            (cvec[:, CV["convb"]:CV["convb"] + DCH], dram["l1a_convb"]),
            (cvec[:, CV["dtb"]:CV["dtb"] + DCH], dram["l1a_dtb"]),
            (cvec[:, CV["A"]:CV["A"] + DSTATE * DCH], dram["l1a_alog"]),
            (cvec[:, CV["D"]:CV["D"] + DCH], dram["l1a_dvec"]),
            (cvec[:, CV["b1"]:CV["b1"] + 32], dram["l1a_b1"]),
            (cvec[:, CV["b2"]:CV["b2"] + CCH], dram["l1a_b2"]),
            (stat[0:1, 0:NEXP], dram["l1m_rb"]),
            (cvec[:, CV["theta"]:CV["theta"] + 1], dram["dhexp"]),
            (ident_bf[:], dram["ident_bf"]),
            (mask_bf[:], dram["maskT_bf"]),
            (sel16[:], dram["sel16"]),
            (onespair[:], dram["onespair"]),
        ]:
            t = C.qinc(b, y.dma_start(out=dst, in_=src))
        cast_tok["consts"] = t
    C.Y(_)
    tok_consts = t_consts = cast_tok["consts"]

    rw_box = []
    def _(y):
        C.wait(y, cast_tok["bf_l1m_rwt"])
        b = C.dma_batch(y)
        C.qinc(b, y.dma_start(out=rw_sb[:, 0:64].rearrange("p (a d) -> p a d", a=CCH),
                              in_=dram["bf_l1m_rwt"].rearrange("(a p) d -> p a d", p=128)))
        rw_box.append(C.qinc(b, y.dma_start(out=pool2[:, 0:64].rearrange("p (a d) -> p a d", a=CCH),
                                            in_=dram["l1m_rwt"].rearrange("(a p) d -> p a d", p=128))))
    C.Y(_)
    def _(v, rt=rw_box[0]):
        C.wait(v, rt)
        C.inc(eV, v.tensor_tensor(out=rw_lo[:, 0:64], in0=pool2[:, 0:64], in1=rw_sb[:, 0:64],
                                  op=ALU.subtract))
    C.V(_)
    tok_rw = C.tok(eV)

    def _(v):
        C.wait(v, tok_consts)
        v.memset(cst[:, CS["ones_col"]:CS["ones_col"] + 1], 1.0)
        v.memset(cst[0:1, CS["ones_row"]:CS["ones_row"] + 128], 1.0)
        v.memset(cst[32:33, CS["ones_row"]:CS["ones_row"] + 128], 1.0)
        v.memset(cst[:, CS["zero_col"]:CS["zero_col"] + 1], 0.0)
        v.memset(cst[:, CS["halfpi"]:CS["halfpi"] + 1], math.pi / 2)
        v.memset(cst[:, CS["tenk"]:CS["tenk"] + 1], 10000.0)
        v.memset(cst[:, CS["epsln"]:CS["epsln"] + 1], EPS_LN)
        v.memset(cst[:, CS["epsrms"]:CS["epsrms"] + 1], EPS_RMS)
        t = C.inc(eV, v.engine_nop())
    C.V(_)
    tok_cst0 = C.tok(eV)

    def _(sc):
        C.wait(sc, tok_cst0)
        t = C.inc(eS, sc.activation(out=cst[:, CS["ln1e4"]:CS["ln1e4"] + 1],
                                    in_=cst[:, CS["tenk"]:CS["tenk"] + 1], func=AF.Ln))
        C.wait(sc, t)
        t = C.inc(eS, sc.activation(out=cvec[:, CV["theta"]:CV["theta"] + 1],
                                    in_=cvec[:, CV["theta"]:CV["theta"] + 1],
                                    func=AF.Exp, scale=cst[:, CS["ln1e4"]:CS["ln1e4"] + 1]))
    C.S(_)
    tok_theta = C.tok(eS)

    def _(s):
        C.wait(s, tok_consts)
        t = C.inc(eS, s.activation(out=cvec[:, CV["A"]:CV["A"] + DSTATE * DCH],
                                   in_=cvec[:, CV["A"]:CV["A"] + DSTATE * DCH], func=AF.Exp))
    C.S(_)
    def _(v):
        C.wait(v, C.tok(eS))
        t = C.inc(eV, v.tensor_scalar(out=cvec[:, CV["A"]:CV["A"] + DSTATE * DCH],
                                      in0=cvec[:, CV["A"]:CV["A"] + DSTATE * DCH],
                                      scalar1=-1.0, scalar2=None, op0=ALU.mult))
    C.V(_)
    tok_A = C.tok(eV)

    def _(t):
        C.wait(t, tok_consts)
        C.wait(t, tok_cst0)
        tk = C.inc(eT, t.matmul(out=pA[:, 0:NEXP], lhsT=cst[0:1, CS["ones_row"]:CS["ones_row"] + 128],
                                rhs=stat[0:1, 0:NEXP], start=True, stop=True))
    C.T(_)
    def _(s):
        C.wait(s, C.tok(eT))
        t = C.inc(eS, s.copy(out=cst[:, CS["rbbc"]:CS["rbbc"] + NEXP], in_=pA[:, 0:NEXP]))
    C.S(_)
    tok_rbbc = C.tok(eS)

    ones_bf = C.es.enter_context(nc.sbuf_tensor("ones_bf", [128, 2], BF16))

    def _(v):
        t = C.inc(eV, v.memset(ones_bf[:], 1.0))
    C.V(_)

    SQOFF = 8192

    def ln_stats(prev_x_tok):
        per_chunk = (len(prev_x_tok) == CCH)
        baseT0 = C.val(eT)
        def _(t):
            if not per_chunk:
                C.wait(t, *prev_x_tok)
            for cc in range(CCH):
                if per_chunk:
                    C.wait(t, prev_x_tok[cc])
                for j in range(NFC):
                    mm = t.matmul(out=pSC[0:1, j * TQ:(j + 1) * TQ],
                                  lhsT=cst[:, CS["ones_col"]:CS["ones_col"] + 1],
                                  rhs=x_sb[:, cc * T + j * TQ:cc * T + (j + 1) * TQ],
                                  start=(cc == 0), stop=(cc == CCH - 1))
            C.inc(eT, mm)
        C.T(_)
        baseS = C.val(eS)
        def _(s):
            if not per_chunk:
                C.wait(s, *prev_x_tok)
            for cc in range(CCH):
                if per_chunk:
                    C.wait(s, prev_x_tok[cc])
                if cc >= 2:
                    C.wait(s, (eT, baseT0 + 1 + cc - 1))
                C.inc(eS, s.activation(out=p3bf[:, SQOFF + (cc % 2) * T:SQOFF + (cc % 2) * T + T],
                                       in_=x_sb[:, cc * T:(cc + 1) * T], func=AF.Square))
        C.S(_)
        def _(t):
            for cc in range(CCH):
                C.wait(t, (eS, baseS + cc + 1))
                for j in range(NFC):
                    mm = t.matmul(out=pSC[32:33, j * TQ:(j + 1) * TQ],
                                  lhsT=ones_bf[:, 0:1],
                                  rhs=p3bf[:, SQOFF + (cc % 2) * T + j * TQ:SQOFF + (cc % 2) * T + (j + 1) * TQ],
                                  start=(cc == 0), stop=(cc == CCH - 1))
                C.inc(eT, mm)
        C.T(_)
        mm_tok = C.tok(eT)

        def _(v):
            C.wait(v, mm_tok)
            t = C.inc(eV, v.tensor_scalar(out=rowsM[32:33, 0:T], in0=pSC[0:1, 0:T],
                                          scalar1=1.0 / CDIM, scalar2=None, op0=ALU.mult))
            C.wait(v, t)
            t = C.inc(eV, v.tensor_tensor(out=rowsM[0:1, 0:T], in0=rowsM[32:33, 0:T],
                                          in1=rowsM[32:33, 0:T], op=ALU.mult))
            C.wait(v, t)
            st = v.scalar_tensor_tensor(out=pSC[64:65, 0:T], in0=pSC[32:33, 0:T],
                                        scalar=1.0 / CDIM, in1=rowsM[0:1, 0:T],
                                        op0=ALU.mult, op1=ALU.subtract)
            C.inc(eV, st)
        C.V(_)
        vt = C.tok(eV)

        def _(s):
            C.wait(s, vt)
            t = C.inc(eS, s.activation(out=rowsM[0:1, 0:T], in_=pSC[64:65, 0:T], func=AF.Ln,
                                       bias=cst[32:33, CS["epsln"]:CS["epsln"] + 1]))
            C.wait(s, t)
            C.inc(eS, s.activation(out=rowsM[0:1, 0:T], in_=rowsM[0:1, 0:T], func=AF.Exp, scale=-0.5))
        C.S(_)
        st_tok = C.tok(eS)

        def _(v):
            C.wait(v, st_tok)
            C.inc(eV, v.engine_nop())
        C.V(_)
        return C.tok(eV)

    def ln_bcast(t, j, stats_tok, want=True):
        C.wait(t, stats_tok)
        t.matmul(out=pSC[:, 0:TQ], lhsT=cst[32:33, CS["ones_row"]:CS["ones_row"] + 128],
                 rhs=rowsM[32:33, j * TQ:(j + 1) * TQ], start=True, stop=True)
        mm = t.matmul(out=pSC[:, TQ:2 * TQ], lhsT=cst[0:1, CS["ones_row"]:CS["ones_row"] + 128],
                      rhs=rowsM[0:1, j * TQ:(j + 1) * TQ], start=True, stop=True)
        return C.inc(eT, mm)

    def rwkv(seq, prev_toks):
        wtok_box = []
        def _(y):
            C.wait(y, cast_tok["bf_l0f_wr_t"], cast_tok["bf_l0f_wk_t"], cast_tok["bf_l0f_wv_t"])
            C.wait(y, *prev_toks)
            b = C.dma_batch(y)
            C.qinc(b, y.dma_start(out=p2bf[:, 0:8192].rearrange("p (a d) -> p a d", a=CCH),
                                  in_=dram["bf_l0f_wr_t"].rearrange("(a p) d -> p a d", p=128)))
            C.qinc(b, y.dma_start(out=p2bf[:, 8192:16384].rearrange("p (a d) -> p a d", a=CCH),
                                  in_=dram["bf_l0f_wk_t"].rearrange("(a p) d -> p a d", p=128)))
            wtok_box.append(C.qinc(b, y.dma_start(out=p3bf[:, 0:8192].rearrange("p (a d) -> p a d", a=CCH),
                                                  in_=dram["bf_l0f_wv_t"].rearrange("(a p) d -> p a d", p=128))))
        C.Y(_)
        w_tok = wtok_box[0]

        stats_tok = ln_stats(prev_toks)

        xn = pool1
        XROFF = 8192
        XKOFF = 12288
        r32 = pool4
        PCOL = CS["scratch"]

        prev_mix_tok = C.tok(eV)
        for j in range(NFC):
            def _(t, j=j, pm=prev_mix_tok):
                if j == 0:
                    C.wait(t, w_tok)
                else:
                    C.wait(t, pm)
                ln_bcast(t, j, stats_tok)
            C.T(_)
            bc_tok = C.tok(eT)

            def _(v, j=j, bc=bc_tok):
                C.wait(v, bc)
                DOFF = 4096
                tk = None
                for cc in range(CCH):
                    xo = cc * TQ
                    xs = x_sb[:, cc * T + j * TQ: cc * T + (j + 1) * TQ]
                    tk = v.tensor_tensor(out=xn[:, xo:xo + TQ], in0=xs, in1=pSC[:, 0:TQ], op=ALU.subtract)
                C.wait(v, C.inc(eV, tk))
                for cc in range(CCH):
                    xo = cc * TQ
                    tk = v.tensor_tensor(out=xn[:, xo:xo + TQ], in0=xn[:, xo:xo + TQ], in1=pSC[:, TQ:2 * TQ], op=ALU.mult)
                C.wait(v, C.inc(eV, tk))
                for cc in range(CCH):
                    xo = cc * TQ
                    tk = v.tensor_tensor(out=pool1[:, DOFF + xo + 1:DOFF + xo + TQ], in0=xn[:, xo + 1:xo + TQ],
                                    in1=xn[:, xo:xo + TQ - 1], op=ALU.subtract)
                for cc in range(CCH):
                    xo = cc * TQ
                    if j == 0:
                        tk = v.tensor_copy(out=pool1[:, DOFF + xo:DOFF + xo + 1], in_=xn[:, xo:xo + 1])
                    else:
                        tk = v.tensor_tensor(out=pool1[:, DOFF + xo:DOFF + xo + 1], in0=xn[:, xo:xo + 1],
                                        in1=cst[:, PCOL + cc:PCOL + cc + 1], op=ALU.subtract)
                C.wait(v, C.inc(eV, tk))
                for cc in range(CCH):
                    xo = cc * TQ
                    mur = cvec[:, CV["mur"] + cc:CV["mur"] + cc + 1]
                    v.scalar_tensor_tensor(out=p3bf[:, XROFF + xo + 1:XROFF + xo + TQ],
                                           in0=pool1[:, DOFF + xo + 1:DOFF + xo + TQ], scalar=mur,
                                           in1=xn[:, xo:xo + TQ - 1], op0=ALU.mult, op1=ALU.add)
                    if j == 0:
                        v.tensor_scalar(out=p3bf[:, XROFF + xo:XROFF + xo + 1],
                                        in0=pool1[:, DOFF + xo:DOFF + xo + 1], scalar1=mur,
                                        scalar2=None, op0=ALU.mult)
                    else:
                        v.scalar_tensor_tensor(out=p3bf[:, XROFF + xo:XROFF + xo + 1],
                                               in0=pool1[:, DOFF + xo:DOFF + xo + 1], scalar=mur,
                                               in1=cst[:, PCOL + cc:PCOL + cc + 1], op0=ALU.mult, op1=ALU.add)
                for cc in range(CCH):
                    xo = cc * TQ
                    muk = cvec[:, CV["muk"] + cc:CV["muk"] + cc + 1]
                    tk = v.scalar_tensor_tensor(out=p3bf[:, XKOFF + xo + 1:XKOFF + xo + TQ],
                                                in0=pool1[:, DOFF + xo + 1:DOFF + xo + TQ], scalar=muk,
                                                in1=xn[:, xo:xo + TQ - 1], op0=ALU.mult, op1=ALU.add)
                    if j == 0:
                        tk = v.tensor_scalar(out=p3bf[:, XKOFF + xo:XKOFF + xo + 1],
                                             in0=pool1[:, DOFF + xo:DOFF + xo + 1], scalar1=muk,
                                             scalar2=None, op0=ALU.mult)
                    else:
                        tk = v.scalar_tensor_tensor(out=p3bf[:, XKOFF + xo:XKOFF + xo + 1],
                                                    in0=pool1[:, DOFF + xo:DOFF + xo + 1], scalar=muk,
                                                    in1=cst[:, PCOL + cc:PCOL + cc + 1], op0=ALU.mult, op1=ALU.add)
                C.wait(v, C.inc(eV, tk))
                for cc in range(CCH):
                    xo = cc * TQ
                    mm = v.tensor_copy(out=cst[:, PCOL + cc:PCOL + cc + 1], in_=xn[:, xo + TQ - 1:xo + TQ])
                C.inc(eV, mm)
            C.V(_)
            mix_tok = C.tok(eV)

            sig_base = C.val(eS)
            baseT = C.val(eT)
            def _(t, j=j, mt=mix_tok, sig_base=sig_base):
                C.wait(t, mt)
                for dc in range(CCH):
                    if dc >= 2:
                        C.wait(t, (eS, sig_base + dc - 1))
                    pp = pA if dc % 2 == 0 else pB
                    for cc in range(CCH):
                        mm = t.matmul(out=pp[:, 0:TQ],
                                      lhsT=p2bf[:, cc * 1024 + dc * 128: cc * 1024 + (dc + 1) * 128],
                                      rhs=p3bf[:, XROFF + cc * TQ:XROFF + (cc + 1) * TQ],
                                      start=(cc == 0), stop=(cc == CCH - 1))
                    C.inc(eT, mm)
            C.T(_)
            def _(s, baseT=baseT):
                for dc in range(CCH):
                    C.wait(s, (eT, baseT + dc + 1))
                    pp = pA if dc % 2 == 0 else pB
                    C.inc(eS, s.activation(out=r32[:, dc * TQ:(dc + 1) * TQ], in_=pp[:, 0:TQ], func=AF.Sigmoid))
            C.S(_)

            k_base = C.val(eS)
            baseT2 = C.val(eT)
            def _(t, j=j, k_base=k_base):
                for dc in range(CCH):
                    if dc >= 2:
                        C.wait(t, (eS, k_base + 2 * dc - 3))
                    pp = pC if dc % 2 == 0 else pD
                    for cc in range(CCH):
                        mm = t.matmul(out=pp[:, 0:TQ],
                                      lhsT=p2bf[:, 8192 + cc * 1024 + dc * 128: 8192 + cc * 1024 + (dc + 1) * 128],
                                      rhs=p3bf[:, XKOFF + cc * TQ:XKOFF + (cc + 1) * TQ],
                                      start=(cc == 0), stop=(cc == CCH - 1))
                    C.inc(eT, mm)
            C.T(_)
            KROFF = 6144
            def _(s, baseT2=baseT2):
                for dc in range(CCH):
                    C.wait(s, (eT, baseT2 + dc + 1))
                    kr = pool1[:, KROFF + (dc % 2) * TQ:KROFF + (dc % 2) * TQ + TQ]
                    pp = pC if dc % 2 == 0 else pD
                    tk = C.inc(eS, s.activation(out=kr, in_=pp[:, 0:TQ], func=AF.Relu))
                    C.wait(s, tk)
                    C.inc(eS, s.activation(out=p1bf[:, 8192 + dc * TQ:8192 + (dc + 1) * TQ],
                                           in_=kr, func=AF.Square))
            C.S(_)
            kact_tok = C.tok(eS)

            v_base = C.val(eV)
            baseT3 = C.val(eT)
            def _(t, j=j, kt=kact_tok, v_base=v_base):
                C.wait(t, kt)
                for dc in range(CCH):
                    if dc >= 2:
                        C.wait(t, (eV, v_base + 2 * dc - 3))
                    pp = pA if dc % 2 == 0 else pB
                    for cc in range(CCH):
                        mm = t.matmul(out=pp[:, 0:TQ],
                                      lhsT=p3bf[:, cc * 1024 + dc * 128: cc * 1024 + (dc + 1) * 128],
                                      rhs=p1bf[:, 8192 + cc * TQ:8192 + (cc + 1) * TQ],
                                      start=(cc == 0), stop=(cc == CCH - 1))
                    C.inc(eT, mm)
            C.T(_)
            TMOFF = 7168
            def _(v, j=j, baseT3=baseT3):
                for dc in range(CCH):
                    C.wait(v, (eT, baseT3 + dc + 1))
                    pp = pA if dc % 2 == 0 else pB
                    tm = pool1[:, TMOFF + (dc % 2) * TQ:TMOFF + (dc % 2) * TQ + TQ]
                    tk = C.inc(eV, v.tensor_tensor(out=tm, in0=pp[:, 0:TQ],
                                                   in1=r32[:, dc * TQ:(dc + 1) * TQ], op=ALU.mult))
                    C.wait(v, tk)
                    xs = x_sb[:, dc * T + j * TQ: dc * T + (j + 1) * TQ]
                    C.inc(eV, v.scalar_tensor_tensor(out=xs, in0=tm, scalar=0.5, in1=xs,
                                                     op0=ALU.mult, op1=ALU.add))
            C.V(_)
            prev_mix_tok = C.tok(eV)
        return [C.tok(eV)]

    def build_rope(dst, dstoff, prev_tok):
        ldtok_box = []
        def _(y):
            C.wait(y, prev_tok)
            b = C.dma_batch(y)
            C.qinc(b, y.dma_start(out=pool3[0:1, 6144:6144 + T], in_=dram["pos_t"]))
            ldtok_box.append(C.qinc(b, y.dma_start(out=pool2[0:1, 0:T], in_=dram["sgn_t"])))
        C.Y(_)
        ld_tok = ldtok_box[0]

        def _(t):
            C.wait(t, ld_tok)
            for j in range(NFC):
                mm = t.matmul(out=pSC[:, j * TQ:(j + 1) * TQ],
                              lhsT=cst[0:1, CS["ones_row"]:CS["ones_row"] + 128],
                              rhs=pool3[0:1, 6144 + j * TQ:6144 + (j + 1) * TQ],
                              start=True, stop=True)
            C.inc(eT, mm)
        C.T(_)
        pb_tok = C.tok(eT)

        def _(v):
            C.wait(v, pb_tok)
            t = C.inc(eV, v.tensor_scalar(out=cst[:, CS["scratch"]:CS["scratch"] + 1],
                                          in0=cvec[:, CV["theta"]:CV["theta"] + 1],
                                          scalar1=1.0 / (2 * math.pi), scalar2=None, op0=ALU.mult))
            C.wait(v, t)
            v.tensor_scalar(out=pool3[:, 0:T], in0=pSC[:, 0:T], scalar1=cvec[:, CV["theta"]:CV["theta"] + 1],
                            scalar2=None, op0=ALU.mult)
            t = C.inc(eV, v.tensor_scalar(out=pool3[:, 2048:2048 + T], in0=pSC[:, 0:T],
                                          scalar1=cst[:, CS["scratch"]:CS["scratch"] + 1],
                                          scalar2=0.5, op0=ALU.mult, op1=ALU.add))
            C.wait(v, t)
            t = C.inc(eV, v.tensor_copy(out=pool3.bitcast(mybir.dt.int32)[:, 4096:4096 + T],
                                        in_=pool3[:, 2048:2048 + T]))
            C.wait(v, t)
            t = C.inc(eV, v.tensor_copy(out=pool3[:, 2048:2048 + T],
                                        in_=pool3.bitcast(mybir.dt.int32)[:, 4096:4096 + T]))
            C.wait(v, t)
            t = C.inc(eV, v.scalar_tensor_tensor(out=pool3[:, 0:T], in0=pool3[:, 2048:2048 + T],
                                                 scalar=-2 * math.pi, in1=pool3[:, 0:T],
                                                 op0=ALU.mult, op1=ALU.add))
            C.wait(v, t)
            t = C.inc(eV, v.tensor_scalar(out=pool3[:, 2048:2048 + T], in0=pool3[:, 0:T],
                                          scalar1=math.pi, scalar2=None, op0=ALU.is_gt))
            C.wait(v, t)
            t = C.inc(eV, v.scalar_tensor_tensor(out=pool3[:, 0:T], in0=pool3[:, 2048:2048 + T],
                                                 scalar=-2 * math.pi, in1=pool3[:, 0:T],
                                                 op0=ALU.mult, op1=ALU.add))
            C.wait(v, t)
            t = C.inc(eV, v.tensor_scalar(out=pool3[:, 2048:2048 + T], in0=pool3[:, 0:T],
                                          scalar1=-math.pi, scalar2=None, op0=ALU.is_lt))
            C.wait(v, t)
            t = C.inc(eV, v.scalar_tensor_tensor(out=pool3[:, 0:T], in0=pool3[:, 2048:2048 + T],
                                                 scalar=2 * math.pi, in1=pool3[:, 0:T],
                                                 op0=ALU.mult, op1=ALU.add))
            C.wait(v, t)
            t = C.inc(eV, v.tensor_scalar(out=pool3[:, 0:T], in0=pool3[:, 0:T], scalar1=math.pi,
                                          scalar2=-math.pi, op0=ALU.min, op1=ALU.max))
        C.V(_)
        ang_tok = C.tok(eV)

        def _(s):
            C.wait(s, ang_tok)
            s.activation(out=pool3[:, 2048:2048 + T], in_=pool3[:, 0:T], func=AF.Sin)
            t = C.inc(eS, s.activation(out=pool3[:, 4096:4096 + T], in_=pool3[:, 0:T], func=AF.Abs))
            C.wait(s, t)
            t = C.inc(eS, s.activation(out=pool3[:, 4096:4096 + T], in_=pool3[:, 4096:4096 + T], func=AF.Sin,
                                       scale=-1.0, bias=cst[:, CS["halfpi"]:CS["halfpi"] + 1]))
        C.S(_)
        sc_tok = C.tok(eS)

        def _(t):
            C.wait(t, sc_tok)
            for j in range(NFC):
                mm = t.matmul(out=pSC[:, j * TQ:(j + 1) * TQ],
                              lhsT=cst[0:1, CS["ones_row"]:CS["ones_row"] + 128],
                              rhs=pool2[0:1, j * TQ:(j + 1) * TQ],
                              start=True, stop=True)
            C.inc(eT, mm)
        C.T(_)
        sb_tok = C.tok(eT)

        def _(v):
            C.wait(v, sb_tok)
            t = C.inc(eV, v.tensor_tensor(out=dst[:, dstoff:dstoff + T], in0=pool3[:, 2048:2048 + T],
                                          in1=pSC[:, 0:T], op=ALU.mult))
            C.wait(v, t)
            t = C.inc(eV, v.tensor_tensor(out=dst[:, dstoff:dstoff + T], in0=dst[:, dstoff:dstoff + T],
                                          in1=pool3[:, 4096:4096 + T], op=ALU.add))
        C.V(_)
        return C.tok(eV)

    def attention(seq, prev_toks):
        NTT = T // 128
        wtok_box = []
        def _(y):
            C.wait(y, cast_tok["bf_l0a_wq"], cast_tok["bf_l0a_wk"], cast_tok["bf_l0a_wv"])
            C.wait(y, *prev_toks)
            b = C.dma_batch(y)
            C.qinc(b, y.dma_start(out=p4bf[:, 0:8192].rearrange("p (a d) -> p a d", a=CCH),
                                  in_=dram["bf_l0a_wq"].rearrange("(a p) d -> p a d", p=128)))
            C.qinc(b, y.dma_start(out=p2bf[:, 6144:8192].rearrange("p (a d) -> p a d", a=CCH),
                                  in_=dram["bf_l0a_wk"].rearrange("(a p) d -> p a d", p=128)))
            wtok_box.append(C.qinc(b, y.dma_start(out=p2bf[:, 8192:10240].rearrange("p (a d) -> p a d", a=CCH),
                                                  in_=dram["bf_l0a_wv"].rearrange("(a p) d -> p a d", p=128))))
        C.Y(_)
        w_tok = wtok_box[0]

        stats_tok = ln_stats(prev_toks)

        RFOFF = 4096
        rope_tok = build_rope(pool1, RFOFF, stats_tok)

        XGOFF = 0
        SQOFF2 = 4096
        VBOFF = 10240
        KOFF = 12288

        for j in range(NFC):
            def _(t, j=j):
                if j == 0:
                    C.wait(t, rope_tok)
                ln_bcast(t, j, stats_tok)
            C.T(_)
            bc_tok = C.tok(eT)

            def _(v, j=j, bc=bc_tok):
                C.wait(v, bc)
                tk = None
                for cc in range(CCH):
                    xs = x_sb[:, cc * T + j * TQ: cc * T + (j + 1) * TQ]
                    tk = v.tensor_tensor(out=pool1[:, cc * TQ:(cc + 1) * TQ], in0=xs, in1=pSC[:, 0:TQ], op=ALU.subtract)
                C.wait(v, C.inc(eV, tk))
                for cc in range(CCH):
                    tk = v.tensor_tensor(out=pool1[:, cc * TQ:(cc + 1) * TQ], in0=pool1[:, cc * TQ:(cc + 1) * TQ],
                                         in1=pSC[:, TQ:2 * TQ], op=ALU.mult)
                C.wait(v, C.inc(eV, tk))
                for cc in range(CCH):
                    xs = x_sb[:, cc * T + j * TQ: cc * T + (j + 1) * TQ]
                    tk = v.tensor_tensor(out=xs, in0=xs, in1=pool1[:, cc * TQ:(cc + 1) * TQ], op=ALU.add)
                C.inc(eV, tk)
            C.V(_)
            xn_tok = C.tok(eV)

            sq_base = C.val(eS)
            rmsT0 = C.val(eT)
            def _(s, j=j, xt=xn_tok, rmsT0=rmsT0):
                C.wait(s, xt)
                for cc in range(CCH):
                    if cc >= 2:
                        C.wait(s, (eT, rmsT0 + cc - 1))
                    C.inc(eS, s.activation(out=p2bf[:, SQOFF2 + (cc % 2) * TQ:SQOFF2 + (cc % 2) * TQ + TQ],
                                           in_=pool1[:, cc * TQ:(cc + 1) * TQ], func=AF.Square))
            C.S(_)
            def _(t, j=j, sq_base=sq_base):
                for cc in range(CCH):
                    C.wait(t, (eS, sq_base + cc + 1))
                    mm = t.matmul(out=pB[0:1, 0:TQ], lhsT=ones_bf[:, 0:1],
                                  rhs=p2bf[:, SQOFF2 + (cc % 2) * TQ:SQOFF2 + (cc % 2) * TQ + TQ],
                                  start=(cc == 0), stop=(cc == CCH - 1))
                    C.inc(eT, mm)
            C.T(_)
            rmsmm_tok = C.tok(eT)

            def _(s, j=j, mt=rmsmm_tok):
                C.wait(s, mt)
                t = C.inc(eS, s.activation(out=rowsM[0:1, T:T + TQ], in_=pB[0:1, 0:TQ], func=AF.Ln,
                                           scale=1.0 / CDIM, bias=cst[0:1, CS["epsrms"]:CS["epsrms"] + 1]))
                C.wait(s, t)
                C.inc(eS, s.activation(out=rowsM[0:1, T:T + TQ], in_=rowsM[0:1, T:T + TQ],
                                       func=AF.Exp, scale=-0.5))
            C.S(_)
            r2_tok = C.tok(eS)

            def _(t, j=j, rt=r2_tok):
                C.wait(t, rt)
                mm = t.matmul(out=pSC[:, 2 * TQ:3 * TQ], lhsT=cst[0:1, CS["ones_row"]:CS["ones_row"] + 128],
                              rhs=rowsM[0:1, T:T + TQ], start=True, stop=True)
                C.inc(eT, mm)
            C.T(_)
            r2bc_tok = C.tok(eT)

            def _(v, j=j, rt=r2bc_tok):
                C.wait(v, rt)
                tk = None
                for cc in range(CCH):
                    nw = cvec[:, CV["normw"] + cc:CV["normw"] + cc + 1]
                    tk = v.scalar_tensor_tensor(out=p2bf[:, XGOFF + cc * TQ:XGOFF + (cc + 1) * TQ],
                                                in0=pool1[:, cc * TQ:(cc + 1) * TQ], scalar=nw,
                                                in1=pSC[:, 2 * TQ:3 * TQ], op0=ALU.mult, op1=ALU.mult)
                C.inc(eV, tk)
            C.V(_)
            xg_tok = C.tok(eV)

            qe_base = C.val(eV)
            baseT = C.val(eT)
            def _(t, j=j, xg_tok=xg_tok, w_tok=w_tok, qe_base=qe_base):
                C.wait(t, xg_tok)
                if j == 0:
                    C.wait(t, w_tok)
                nmm = 0
                for dc in range(CCH):
                    if nmm >= 2:
                        C.wait(t, (eV, qe_base + nmm - 1))
                    pp = pA if nmm % 2 == 0 else pC
                    for cc in range(CCH):
                        mm = t.matmul(out=pp[:, 0:TQ],
                                      lhsT=p4bf[:, cc * 1024 + dc * 128: cc * 1024 + (dc + 1) * 128],
                                      rhs=p2bf[:, XGOFF + cc * TQ:XGOFF + (cc + 1) * TQ],
                                      start=(cc == 0), stop=(cc == CCH - 1))
                    C.inc(eT, mm)
                    nmm += 1
                for dc in range(2):
                    if nmm >= 2:
                        C.wait(t, (eV, qe_base + nmm - 1))
                    pp = pA if nmm % 2 == 0 else pC
                    for cc in range(CCH):
                        mm = t.matmul(out=pp[:, 0:TQ],
                                      lhsT=p2bf[:, 6144 + cc * 256 + dc * 128: 6144 + cc * 256 + (dc + 1) * 128],
                                      rhs=p2bf[:, XGOFF + cc * TQ:XGOFF + (cc + 1) * TQ],
                                      start=(cc == 0), stop=(cc == CCH - 1))
                    C.inc(eT, mm)
                    nmm += 1
                for dc in range(2):
                    if nmm >= 2:
                        C.wait(t, (eV, qe_base + nmm - 1))
                    pp = pA if nmm % 2 == 0 else pC
                    for cc in range(CCH):
                        mm = t.matmul(out=pp[:, 0:TQ],
                                      lhsT=p2bf[:, 8192 + cc * 256 + dc * 128: 8192 + cc * 256 + (dc + 1) * 128],
                                      rhs=p2bf[:, XGOFF + cc * TQ:XGOFF + (cc + 1) * TQ],
                                      start=(cc == 0), stop=(cc == CCH - 1))
                    C.inc(eT, mm)
                    nmm += 1
            C.T(_)

            def _(v, j=j, baseT=baseT):
                nmm = 0
                for dc in range(CCH):
                    C.wait(v, (eT, baseT + nmm + 1))
                    pp = pA if nmm % 2 == 0 else pC
                    C.inc(eV, v.tensor_tensor(out=p3bf[:, dc * T + j * TQ:dc * T + (j + 1) * TQ],
                                              in0=pp[:, 0:TQ],
                                              in1=pool1[:, RFOFF + j * TQ:RFOFF + (j + 1) * TQ],
                                              op=ALU.mult))
                    nmm += 1
                for dc in range(2):
                    C.wait(v, (eT, baseT + nmm + 1))
                    pp = pA if nmm % 2 == 0 else pC
                    C.inc(eV, v.tensor_tensor(out=p1bf[:, KOFF + dc * T + j * TQ:KOFF + dc * T + (j + 1) * TQ],
                                              in0=pp[:, 0:TQ],
                                              in1=pool1[:, RFOFF + j * TQ:RFOFF + (j + 1) * TQ],
                                              op=ALU.mult))
                    nmm += 1
                for dc in range(2):
                    C.wait(v, (eT, baseT + nmm + 1))
                    pp = pA if nmm % 2 == 0 else pC
                    C.inc(eV, v.tensor_copy(out=p2bf[:, VBOFF + dc * T + j * TQ:VBOFF + dc * T + (j + 1) * TQ],
                                            in_=pp[:, 0:TQ]))
                    nmm += 1
            C.V(_)
        proj_tok = C.tok(eV)
        if _ATT_UPTO == 1:
            return [proj_tok]

        KSW = 4608
        VT = 0
        wotok_box = []
        def _(y):
            C.wait(y, cast_tok["bf_l0a_wo"])
            C.wait(y, proj_tok)
            b = C.dma_batch(y)
            wotok_box.append(C.qinc(b, y.dma_start(out=p4bf[:, 0:8192].rearrange("p (a d) -> p a d", a=CCH),
                                                   in_=dram["bf_l0a_wo"].rearrange("(a p) d -> p a d", p=128))))
        C.Y(_)
        wo_tok = wotok_box[0]

        if _ATT_UPTO == 15:
            ksw_tok = vt_tok = None
        def _(v):
            C.wait(v, proj_tok)
            for c in range(2):
                v.tensor_copy(out=p2bf[0:64, KSW + c * T:KSW + (c + 1) * T],
                              in_=p1bf[64:128, KOFF + c * T:KOFF + (c + 1) * T])
                v.tensor_copy(out=p2bf[64:128, KSW + c * T:KSW + (c + 1) * T],
                              in_=p1bf[0:64, KOFF + c * T:KOFF + (c + 1) * T])
            for tt in range(NTT):
                for kv4 in range(4):
                    o = VT + tt * 260 + kv4 * 65 + 64
                    tk = v.memset(p2bf[:, o:o + 1], 1.0)
            C.inc(eV, tk)
        if _ATT_UPTO not in (15, 17):
            C.V(_)
            ksw_tok = C.tok(eV)
        elif _ATT_UPTO == 17:
            ksw_tok = proj_tok

        tb = C.val(eT)
        ev_base = C.val(eS)
        def _(t, ksw=ksw_tok, ev_base=ev_base):
            C.wait(t, ksw)
            n = 0
            for dc in range(2):
                for tt in range(NTT):
                    if n >= 2:
                        C.wait(t, (eS, ev_base + n - 1))
                    pp = pA if n % 2 == 0 else pC
                    C.inc(eT, t.matmul(out=pp[:, 0:128],
                                       lhsT=p2bf[:, VBOFF + dc * T + tt * 128:VBOFF + dc * T + (tt + 1) * 128],
                                       rhs=ident_bf[:], start=True, stop=True))
                    n += 1
        if _ATT_UPTO not in (15, 16):
            C.T(_)
        def _(s, tb=tb):
            n = 0
            for dc in range(2):
                for tt in range(NTT):
                    C.wait(s, (eT, tb + n + 1))
                    pp = pA if n % 2 == 0 else pC
                    s.activation(out=p2bf[:, VT + tt * 260 + (2 * dc) * 65:VT + tt * 260 + (2 * dc) * 65 + 64],
                                 in_=pp[:, 0:64], func=AF.Copy)
                    C.inc(eS, s.activation(out=p2bf[:, VT + tt * 260 + (2 * dc + 1) * 65:VT + tt * 260 + (2 * dc + 1) * 65 + 64],
                                           in_=pp[:, 64:128], func=AF.Copy))
                    n += 1
        if _ATT_UPTO not in (15, 16):
            C.S(_)
            vt_tok = C.tok(eS)
        elif _ATT_UPTO == 16:
            vt_tok = ksw_tok

        QMAX = 16
        KMAX = 33
        KM16 = 40
        MROW = 56
        sqs_base = C.val(eS)
        redv_base = C.val(eV)
        qmT = C.val(eT)
        ev_tok = {}
        def _(s, pt=proj_tok, qmT=qmT, redv_base=redv_base):
            C.wait(s, pt)
            n = 0
            for cc in range(CCH + 2):
                for fc in range(NFC):
                    if n >= 2:
                        C.wait(s, (eT, qmT + (n - 1) * 2))
                    src = (p3bf[:, cc * T + fc * TQ:cc * T + (fc + 1) * TQ] if cc < CCH else
                           p1bf[:, KOFF + (cc - CCH) * T + fc * TQ:KOFF + (cc - CCH) * T + (fc + 1) * TQ])
                    C.inc(eS, s.activation(out=p1bf[:, (n % 2) * TQ:(n % 2) * TQ + TQ], in_=src, func=AF.Square))
                    n += 1
                C.wait(s, (eT, qmT + 2 * NFC * (cc + 1)))
                if cc >= 1:
                    C.wait(s, (eV, redv_base + 2 * cc))
                s.copy(out=rowsM[0:1, 0:T], in_=pSC[0:1, 0:T])
                ev_tok[cc] = C.inc(eS, s.copy(out=rowsM[32:33, 0:T], in_=pSC[32:33, 0:T]), 1)
        C.S(_)
        def _(t, sqs_base=sqs_base):
            n = 0
            for cc in range(CCH + 2):
                for fc in range(NFC):
                    C.wait(t, (eS, sqs_base + n + 1 + cc))
                    if fc == 0 and cc >= 1:
                        C.wait(t, ev_tok[cc - 1])
                    t.matmul(out=pSC[0:1, fc * TQ:(fc + 1) * TQ], lhsT=onespair[:, 0:1],
                             rhs=p1bf[:, (n % 2) * TQ:(n % 2) * TQ + TQ], start=True, stop=True)
                    mm = t.matmul(out=pSC[32:33, fc * TQ:(fc + 1) * TQ], lhsT=onespair[:, 4:5],
                                  rhs=p1bf[:, (n % 2) * TQ:(n % 2) * TQ + TQ], start=True, stop=True)
                    C.inc(eT, mm, 2)
                    n += 1
        C.T(_)
        def _(v):
            for cc in range(CCH + 2):
                C.wait(v, ev_tok[cc])
                col = QMAX + 2 * cc if cc < CCH else KMAX + 2 * (cc - CCH)
                v.tensor_reduce(out=stat[0:1, col:col + 1], in_=rowsM[0:1, 0:T],
                                axis=mybir.AxisListType.X, op=ALU.max)
                C.inc(eV, v.tensor_reduce(out=stat[0:1, col + 1:col + 2], in_=rowsM[32:33, 0:T],
                                          axis=mybir.AxisListType.X, op=ALU.max), 2)
        C.V(_)
        red_tok = C.tok(eV)

        def _(v, rt=red_tok):
            C.wait(v, rt)
            for h in range(NH):
                t = v.tensor_copy(out=stat[0:1, KM16 + h:KM16 + h + 1],
                                  in_=stat[0:1, KMAX + h % 4:KMAX + h % 4 + 1])
            t = C.inc(eV, t)
            C.wait(v, t)
            t = C.inc(eV, v.tensor_tensor(out=stat[0:1, MROW:MROW + 16], in0=stat[0:1, QMAX:QMAX + 16],
                                          in1=stat[0:1, KM16:KM16 + 16], op=ALU.add))
            C.wait(v, t)
            t = C.inc(eV, v.tensor_scalar(out=stat[0:1, MROW:MROW + 16], in0=stat[0:1, MROW:MROW + 16],
                                          scalar1=1.0 / 16, scalar2=MHAT_CLAMP, op0=ALU.mult, op1=ALU.min))
            C.wait(v, t)
            t = C.inc(eV, v.tensor_scalar(out=stat[0:1, MROW:MROW + 16], in0=stat[0:1, MROW:MROW + 16],
                                          scalar1=-1.0, scalar2=None, op0=ALU.mult))
        C.V(_)
        mrow_tok = C.tok(eV)

        def _(t, mt=mrow_tok):
            C.wait(t, mt)
            C.inc(eT, t.matmul(out=pB[:, 0:16], lhsT=cst[0:1, CS["ones_row"]:CS["ones_row"] + 128],
                               rhs=stat[0:1, MROW:MROW + 16], start=True, stop=True))
        C.T(_)
        mc_T = C.tok(eT)
        def _(s, mt=mc_T):
            C.wait(s, mt)
            C.inc(eS, s.copy(out=cvec[:, CV["mcols"]:CV["mcols"] + NH], in_=pB[:, 0:16]))
        C.S(_)
        mcols_tok = C.tok(eS)
        if _ATT_UPTO in (2, 16, 17):
            return [mcols_tok, vt_tok, ksw_tok]
        if _ATT_UPTO == 15:
            return [mcols_tok]

        pOs = [pA, pB, pC, pD]
        PT0 = 0
        OS0 = 2048
        LR0 = 3072
        NG = (NTT + 3) // 4

        def glast(g):
            return min(4 * g + 3, NTT - 1)

        scores_idx = {}
        pv_idx = {}
        lbc_idx = {}
        n = 0
        for tk in range(NTT):
            n += 1; scores_idx[tk] = n
            n += 1; pv_idx[tk] = n
            g = tk // 4
            if tk == glast(g):
                n += 1; lbc_idx[g] = n
        exp_idx = {}
        evac_idx = {}
        m = 0
        for tk in range(NTT):
            m += 1; exp_idx[tk] = m
            g = tk // 4
            if tk == glast(g):
                evac_idx[g] = m + 3
                m += 3

        for h in range(NH):
            kv = h % 4
            qc = h // 2
            qrow = (h % 2) * 64
            aligned = (h % 2) == (kv % 2)
            if aligned:
                koff, krow = KOFF + (kv // 2) * T, (kv % 2) * 64
                ksrc = p1bf
            else:
                koff, krow = KSW + (kv // 2) * T, (h % 2) * 64
                ksrc = p2bf
            Sbase = C.val(eS)
            Tbase = C.val(eT)
            Vbase = C.val(eV)

            def _(t, h=h, koff=koff, krow=krow, qc=qc, qrow=qrow, ksrc=ksrc,
                  Sbase=Sbase, Tbase=Tbase, Vbase=Vbase):
                if h == 0:
                    C.wait(t, vt_tok, mcols_tok)
                for tk in range(NTT):
                    qj0 = (tk * 128) // TQ
                    if tk > 0:
                        C.wait(t, (eS, Sbase + exp_idx[tk - 1]))
                    for qj in range(qj0, NFC):
                        diag_in_chunk = (qj == qj0)
                        mm = t.matmul(out=pSC[:, qj * TQ:(qj + 1) * TQ],
                                      lhsT=ksrc[krow:krow + 64, koff + tk * 128:koff + (tk + 1) * 128],
                                      rhs=p3bf[qrow:qrow + 64, qc * T + qj * TQ:qc * T + (qj + 1) * TQ],
                                      start=True, stop=not diag_in_chunk)
                        if diag_in_chunk:
                            mm = t.matmul(out=pSC[:, tk * 128:(tk + 1) * 128],
                                          lhsT=ident_bf[:], rhs=mask_bf[:],
                                          start=False, stop=True)
                    C.inc(eT, mm)
                    C.wait(t, (eS, Sbase + exp_idx[tk]))
                    if tk == 0:
                        C.wait(t, (eV, Vbase))
                    first_bank = set()
                    for qt in range(tk, NTT):
                        g = qt // 4
                        st = (tk == 0) and (g not in first_bank)
                        first_bank.add(g)
                        sp = (tk == glast(g)) and (qt == glast(g))
                        mm = t.matmul(out=pOs[g][0:65, (qt % 4) * 128:(qt % 4) * 128 + 128],
                                      lhsT=p2bf[:, VT + tk * 260 + kv * 65:VT + tk * 260 + (kv + 1) * 65],
                                      rhs=p1bf[:, PT0 + (tk % 2) * 2048 + qt * 128:PT0 + (tk % 2) * 2048 + (qt + 1) * 128],
                                      start=st, stop=sp)
                    C.inc(eT, mm)
                    g = tk // 4
                    if tk == glast(g):
                        C.wait(t, (eS, Sbase + evac_idx[g]))
                        gw = min(4, NTT - 4 * g) * 128
                        mm = t.matmul(out=pOs[g][0:64, 0:gw], lhsT=cst[0:1, CS["ones_row"]:CS["ones_row"] + 64],
                                      rhs=pool1[0:1, LR0 + (g % 2) * TQ:LR0 + (g % 2) * TQ + gw],
                                      start=True, stop=True)
                        C.inc(eT, mm)
            C.T(_)

            def _(s, h=h, Sbase=Sbase, Tbase=Tbase, Vbase=Vbase):
                for tk in range(NTT):
                    qj0 = (tk * 128) // TQ
                    C.wait(s, (eT, Tbase + scores_idx[tk]))
                    C.inc(eS, s.activation(out=p1bf[:, PT0 + (tk % 2) * 2048 + qj0 * TQ:PT0 + (tk % 2) * 2048 + T],
                                           in_=pSC[:, qj0 * TQ:T], func=AF.Exp, scale=0.125,
                                           bias=cvec[:, CV["mcols"] + h:CV["mcols"] + h + 1]))
                    g = tk // 4
                    if tk == glast(g):
                        gw = min(4, NTT - 4 * g) * 128
                        C.wait(s, (eT, Tbase + pv_idx[tk]))
                        if g >= 2:
                            C.wait(s, (eV, Vbase + g - 1))
                        C.inc(eS, s.copy(out=pool1[:64, OS0 + (g % 2) * TQ:OS0 + (g % 2) * TQ + gw],
                                         in_=pOs[g][0:64, 0:gw]))
                        tkn = C.inc(eS, s.activation(out=pool1[0:1, LR0 + (g % 2) * TQ:LR0 + (g % 2) * TQ + gw],
                                                     in_=pOs[g][64:65, 0:gw], func=AF.Ln))
                        C.wait(s, tkn)
                        C.inc(eS, s.activation(out=pool1[0:1, LR0 + (g % 2) * TQ:LR0 + (g % 2) * TQ + gw],
                                               in_=pool1[0:1, LR0 + (g % 2) * TQ:LR0 + (g % 2) * TQ + gw],
                                               func=AF.Exp, scale=-1.0))
            C.S(_)

            def _(v, h=h, qc=qc, qrow=qrow, Sbase=Sbase, Tbase=Tbase):
                for g in range(NG):
                    gw = min(4, NTT - 4 * g) * 128
                    C.wait(v, (eT, Tbase + lbc_idx[g]))
                    C.inc(eV, v.tensor_tensor(out=p3bf[qrow:qrow + 64, qc * T + g * TQ:qc * T + g * TQ + gw],
                                              in0=pool1[0:64, OS0 + (g % 2) * TQ:OS0 + (g % 2) * TQ + gw],
                                              in1=pOs[g][0:64, 0:gw], op=ALU.mult))
            C.V(_)
        heads_tok = C.tok(eV)
        if _ATT_UPTO == 3:
            return [heads_tok, wo_tok]

        baseT4 = C.val(eT)
        v4_base = C.val(eV)
        def _(t, ht=heads_tok, wt=wo_tok, v4_base=v4_base):
            C.wait(t, ht, wt)
            n = 0
            for dc in range(CCH):
                for fc in range(NFC):
                    if n >= 2:
                        C.wait(t, (eV, v4_base + n - 1))
                    pp = pA if n % 2 == 0 else pB
                    for cc in range(CCH):
                        mm = t.matmul(out=pp[:, 0:TQ],
                                      lhsT=p4bf[:, cc * 1024 + dc * 128: cc * 1024 + (dc + 1) * 128],
                                      rhs=p3bf[:, cc * T + fc * TQ:cc * T + (fc + 1) * TQ],
                                      start=(cc == 0), stop=(cc == CCH - 1))
                    C.inc(eT, mm)
                    n += 1
        C.T(_)
        def _(v, baseT4=baseT4):
            n = 0
            for dc in range(CCH):
                for fc in range(NFC):
                    C.wait(v, (eT, baseT4 + n + 1))
                    pp = pA if n % 2 == 0 else pB
                    xs = x_sb[:, dc * T + fc * TQ: dc * T + (fc + 1) * TQ]
                    C.inc(eV, v.scalar_tensor_tensor(out=xs, in0=pp[:, 0:TQ], scalar=1.0, in1=xs,
                                                     op0=ALU.mult, op1=ALU.add))
                    n += 1
        C.V(_)
        return [C.tok(eV)]

    def moe(seq, prev_toks):
        NTT = T // 128
        IDF = 4096 + 8 * NTT + 128
        ident_f = pool1[:, IDF:IDF + 128]
        stats_tok = ln_stats(prev_toks)

        prev_b = C.tok(eV)
        for j in range(NFC):
            def _(t, j=j, pb=prev_b):
                C.wait(t, pb)
                ln_bcast(t, j, stats_tok)
            C.T(_)
            bc_tok = C.tok(eT)
            def _(v, j=j, bc=bc_tok):
                C.wait(v, bc)
                tk = None
                for cc in range(CCH):
                    xs = x_sb[:, cc * T + j * TQ: cc * T + (j + 1) * TQ]
                    tk = v.tensor_tensor(out=pool1[:, cc * TQ:(cc + 1) * TQ], in0=xs, in1=pSC[:, 0:TQ], op=ALU.subtract)
                C.wait(v, C.inc(eV, tk))
                for cc in range(CCH):
                    tk = v.tensor_tensor(out=pool1[:, cc * TQ:(cc + 1) * TQ],
                                         in0=pool1[:, cc * TQ:(cc + 1) * TQ],
                                         in1=pSC[:, TQ:2 * TQ], op=ALU.mult)
                C.inc(eV, tk)
            C.V(_)
            t2a = C.tok(eV)
            def _(s, j=j, ta=t2a):
                C.wait(s, ta)
                tk = None
                for cc in range(CCH):
                    tk = s.activation(out=p2bf[:, cc * T + j * TQ:cc * T + (j + 1) * TQ],
                                      in_=pool1[:, cc * TQ:(cc + 1) * TQ], func=AF.Copy)
                C.inc(eS, tk)
            C.S(_)
            t2b = C.tok(eS)
            def _(v, j=j, tb=t2b):
                C.wait(v, tb)
                tk = None
                for cc in range(CCH):
                    tk = v.tensor_tensor(out=p3bf[:, cc * T + j * TQ:cc * T + (j + 1) * TQ],
                                         in0=pool1[:, cc * TQ:(cc + 1) * TQ],
                                         in1=p2bf[:, cc * T + j * TQ:cc * T + (j + 1) * TQ],
                                         op=ALU.subtract)
                C.inc(eV, tk)
            C.V(_)
            prev_b = C.tok(eV)
        xn_tok = C.tok(eV)

        G05 = 4096
        SCR = 4096 + 8 * NTT
        L_, P_, I1_, PM_, I2_, G_ = SCR, SCR + 8, SCR + 16, SCR + 24, SCR + 32, SCR + 40
        (MX, SM, RS, M1, M2, S12, SD, RSD, SW, AC, TT_, F1, T2, RAC) = [SCR + 48 + k for k in range(14)]
        V_PER = 20
        S_PER = 7

        lgT = C.val(eT)
        vbase = C.val(eV)
        sbase = C.val(eS)

        def _(t, xt=xn_tok, vbase=vbase):
            C.wait(t, xt, tok_rw)
            for tt in range(NTT):
                pp = pA if tt % 2 == 0 else pB
                if tt >= 2:
                    C.wait(t, (eV, vbase + V_PER * (tt - 2) + 1))
                for cc in range(CCH):
                    hi = p2bf[:, cc * T + tt * 128:cc * T + (tt + 1) * 128]
                    lo = p3bf[:, cc * T + tt * 128:cc * T + (tt + 1) * 128]
                    t.matmul(out=pp[:, 0:NEXP], lhsT=hi, rhs=rw_sb[:, cc * 8:(cc + 1) * 8],
                             start=(cc == 0), stop=False)
                    t.matmul(out=pp[:, 0:NEXP], lhsT=hi, rhs=rw_lo[:, cc * 8:(cc + 1) * 8],
                             start=False, stop=False)
                    t.matmul(out=pp[:, 0:NEXP], lhsT=lo, rhs=rw_sb[:, cc * 8:(cc + 1) * 8],
                             start=False, stop=False)
                    mm = t.matmul(out=pp[:, 0:NEXP], lhsT=lo, rhs=rw_lo[:, cc * 8:(cc + 1) * 8],
                                  start=False, stop=(cc == CCH - 1))
                C.inc(eT, mm)
        C.T(_)

        def _(s, vbase=vbase):
            p1 = pool1
            for tt in range(NTT):
                C.wait(s, (eV, vbase + V_PER * tt + 3))
                tkn = C.inc(eS, s.activation(out=p1[:, P_:P_ + 8], in_=p1[:, L_:L_ + 8], func=AF.Exp,
                                             bias=p1[:, MX:MX + 1], accum_out=p1[:, SM:SM + 1]))
                C.wait(s, tkn)
                tkn = C.inc(eS, s.activation(out=p1[:, RS:RS + 1], in_=p1[:, SM:SM + 1], func=AF.Ln))
                C.wait(s, tkn)
                C.inc(eS, s.activation(out=p1[:, RS:RS + 1], in_=p1[:, RS:RS + 1], func=AF.Exp, scale=-1.0))
                C.wait(s, (eV, vbase + V_PER * tt + 10))
                tkn = C.inc(eS, s.activation(out=p1[:, RSD:RSD + 1], in_=p1[:, SD:SD + 1], func=AF.Ln))
                C.wait(s, tkn)
                C.inc(eS, s.activation(out=p1[:, RSD:RSD + 1], in_=p1[:, RSD:RSD + 1], func=AF.Exp, scale=-1.0))
                C.wait(s, (eV, vbase + V_PER * tt + 15))
                tkn = C.inc(eS, s.activation(out=p1[:, RAC:RAC + 1], in_=p1[:, AC:AC + 1], func=AF.Ln))
                C.wait(s, tkn)
                C.inc(eS, s.activation(out=p1[:, RAC:RAC + 1], in_=p1[:, RAC:RAC + 1], func=AF.Exp, scale=-1.0))
        C.S(_)

        def _(v, lgT=lgT, sbase=sbase):
            p1 = pool1
            for tt in range(NTT):
                def ed(instr):
                    C.wait(v, C.inc(eV, instr))
                C.wait(v, (eT, lgT + tt + 1))
                C.wait(v, (eS, sbase + S_PER * tt))
                pp = pA if tt % 2 == 0 else pB
                ed(v.tensor_tensor(out=p1[:, L_:L_ + 8], in0=pp[:, 0:NEXP],
                                   in1=cst[:, CS["rbbc"]:CS["rbbc"] + NEXP], op=ALU.add))
                ed(v.tensor_reduce(out=p1[:, MX:MX + 1], in_=p1[:, L_:L_ + 8],
                                   axis=mybir.AxisListType.X, op=ALU.max))
                ed(v.tensor_scalar(out=p1[:, MX:MX + 1], in0=p1[:, MX:MX + 1],
                                   scalar1=-1.0, scalar2=None, op0=ALU.mult))
                C.wait(v, (eS, sbase + S_PER * tt + 3))
                ed(v.tensor_scalar(out=p1[:, P_:P_ + 8], in0=p1[:, P_:P_ + 8],
                                   scalar1=p1[:, RS:RS + 1], scalar2=None, op0=ALU.mult))
                ed(v.tensor_reduce(out=p1[:, M1:M1 + 1], in_=p1[:, P_:P_ + 8],
                                   axis=mybir.AxisListType.X, op=ALU.max))
                ed(v.tensor_scalar(out=p1[:, I1_:I1_ + 8], in0=p1[:, P_:P_ + 8],
                                   scalar1=p1[:, M1:M1 + 1], scalar2=None, op0=ALU.is_ge))
                ed(v.scalar_tensor_tensor(out=p1[:, PM_:PM_ + 8], in0=p1[:, I1_:I1_ + 8],
                                          scalar=-1e30, in1=p1[:, P_:P_ + 8],
                                          op0=ALU.mult, op1=ALU.add))
                ed(v.tensor_reduce(out=p1[:, M2:M2 + 1], in_=p1[:, PM_:PM_ + 8],
                                   axis=mybir.AxisListType.X, op=ALU.max))
                ed(v.tensor_tensor(out=p1[:, S12:S12 + 1], in0=p1[:, M1:M1 + 1],
                                   in1=p1[:, M2:M2 + 1], op=ALU.add))
                ed(v.tensor_scalar(out=p1[:, SD:SD + 1], in0=p1[:, S12:S12 + 1],
                                   scalar1=EPS_MOE, scalar2=None, op0=ALU.add))
                ed(v.tensor_scalar(out=p1[:, I2_:I2_ + 8], in0=p1[:, PM_:PM_ + 8],
                                   scalar1=p1[:, M2:M2 + 1], scalar2=None, op0=ALU.is_ge))
                ed(v.tensor_tensor(out=p1[:, I1_:I1_ + 8], in0=p1[:, I1_:I1_ + 8],
                                   in1=p1[:, I2_:I2_ + 8], op=ALU.add))
                ed(v.tensor_tensor(out=p1[:, G_:G_ + 8], in0=p1[:, I1_:I1_ + 8],
                                   in1=p1[:, P_:P_ + 8], op=ALU.mult))
                C.wait(v, (eS, sbase + S_PER * tt + 5))
                ed(v.tensor_tensor(out=p1[:, SW:SW + 1], in0=p1[:, S12:S12 + 1],
                                   in1=p1[:, RSD:RSD + 1], op=ALU.mult))
                ed(v.tensor_scalar(out=p1[:, AC:AC + 1], in0=p1[:, SW:SW + 1],
                                   scalar1=EPS_MOE, scalar2=None, op0=ALU.max))
                ed(v.tensor_scalar(out=p1[:, TT_:TT_ + 1], in0=p1[:, SW:SW + 1],
                                   scalar1=EPS_MOE, scalar2=1.0, op0=ALU.mult, op1=ALU.add))
                C.wait(v, (eS, sbase + S_PER * tt + 7))
                ed(v.tensor_tensor(out=p1[:, F1:F1 + 1], in0=p1[:, TT_:TT_ + 1],
                                   in1=p1[:, RAC:RAC + 1], op=ALU.mult))
                ed(v.tensor_scalar(out=p1[:, F1:F1 + 1], in0=p1[:, F1:F1 + 1],
                                   scalar1=0.5, scalar2=None, op0=ALU.mult))
                ed(v.tensor_tensor(out=p1[:, T2:T2 + 1], in0=p1[:, RSD:RSD + 1],
                                   in1=p1[:, F1:F1 + 1], op=ALU.mult))
                ed(v.tensor_scalar(out=p1[:, G05 + tt * 8:G05 + (tt + 1) * 8], in0=p1[:, G_:G_ + 8],
                                   scalar1=p1[:, T2:T2 + 1], scalar2=None, op0=ALU.mult))
        C.V(_)
        gate_tok = C.tok(eV)

        last_tok = gate_tok
        for half in range(2):
            NH2 = NTT // 2
            tt0 = half * NH2
            mmT_guard = {}
            vcomb0 = None
            for e in range(NEXP):
                box = []
                def _(y, e=e, half=half, lt=last_tok):
                    C.wait(y, cast_tok["bf_l1m_ew"])
                    if e < 2:
                        C.wait(y, xn_tok)
                        C.wait(y, lt)
                    else:
                        C.wait(y, (eT, mmT_guard[e - 2]))
                    b = C.dma_batch(y)
                    if half == 0 and e == 0:
                        C.qinc(b, y.dma_start(out=ident_f, in_=dram["ident_f"]))
                    dst = p4bf[:, 0:8192] if e % 2 == 0 else p1bf[:, 0:8192]
                    box.append(C.qinc(b, y.dma_start(out=dst.rearrange("p (a d) -> p a d", a=CCH),
                                                     in_=dram["bf_l1m_ew"][e].rearrange("(a p) d -> p a d", p=128))))
                C.Y(_)
                wt = box[0]

                if vcomb0 is None:
                    vcomb0 = C.val(eV)
                baseT = C.val(eT)
                def _(t, e=e, half=half, wt=wt, vcomb0=vcomb0, tt0=tt0, NH2=NH2):
                    C.wait(t, wt)
                    wsrc = p4bf if e % 2 == 0 else p1bf
                    for i in range(NH2):
                        tt = tt0 + i
                        n = e * NH2 + i
                        if n >= 2:
                            C.wait(t, (eV, vcomb0 + n - 1))
                        pp0 = pA if n % 2 == 0 else pC
                        pp1 = pB if n % 2 == 0 else pD
                        for cc in range(CCH):
                            t.matmul(out=pp0[:, 0:512],
                                     lhsT=p2bf[:, cc * T + tt * 128:cc * T + (tt + 1) * 128],
                                     rhs=wsrc[:, cc * 1024:cc * 1024 + 512],
                                     start=(cc == 0), stop=(cc == CCH - 1))
                            mm = t.matmul(out=pp1[:, 0:512],
                                          lhsT=p2bf[:, cc * T + tt * 128:cc * T + (tt + 1) * 128],
                                          rhs=wsrc[:, cc * 1024 + 512:cc * 1024 + 1024],
                                          start=(cc == 0), stop=(cc == CCH - 1))
                        C.inc(eT, mm)
                C.T(_)
                mmT_guard[e] = C.val(eT)

                def _(v, e=e, baseT=baseT, tt0=tt0, NH2=NH2, vcomb0=vcomb0):
                    for i in range(NH2):
                        tt = tt0 + i
                        C.wait(v, (eT, baseT + i + 1))
                        if e > 0:
                            C.wait(v, (eV, vcomb0 + (e - 1) * NH2 + i + 1))
                        n = e * NH2 + i
                        pp0 = pA if n % 2 == 0 else pC
                        pp1 = pB if n % 2 == 0 else pD
                        gcol = pool1[:, G05 + tt * 8 + e:G05 + tt * 8 + e + 1]
                        do = i * 1024
                        if e == 0:
                            v.tensor_scalar(out=pool3[:, do:do + 512], in0=pp0[:, 0:512],
                                            scalar1=gcol, scalar2=None, op0=ALU.mult)
                            mm = v.tensor_scalar(out=pool3[:, do + 512:do + 1024], in0=pp1[:, 0:512],
                                                 scalar1=gcol, scalar2=None, op0=ALU.mult)
                        else:
                            v.scalar_tensor_tensor(out=pool3[:, do:do + 512], in0=pp0[:, 0:512],
                                                   scalar=gcol, in1=pool3[:, do:do + 512],
                                                   op0=ALU.mult, op1=ALU.add)
                            mm = v.scalar_tensor_tensor(out=pool3[:, do + 512:do + 1024], in0=pp1[:, 0:512],
                                                        scalar=gcol, in1=pool3[:, do + 512:do + 1024],
                                                        op0=ALU.mult, op1=ALU.add)
                        C.inc(eV, mm)
                C.V(_)
            comb_tok = C.tok(eV)

            trT = C.val(eT)
            trV = C.val(eV)
            def _(t, ct=comb_tok, trV=trV, NH2=NH2):
                C.wait(t, ct)
                n = 0
                for i in range(NH2):
                    for blk in range(CCH):
                        if n >= 2:
                            C.wait(t, (eV, trV + n - 1))
                        po = (n % 2) * 512
                        C.inc(eT, t.matmul(out=pSC[:, po:po + 128],
                                           lhsT=pool3[:, i * 1024 + blk * 128:i * 1024 + (blk + 1) * 128],
                                           rhs=ident_f[:], start=True, stop=True))
                        n += 1
            C.T(_)
            def _(v, trT=trT, tt0=tt0, NH2=NH2):
                n = 0
                for i in range(NH2):
                    tt = tt0 + i
                    for blk in range(CCH):
                        C.wait(v, (eT, trT + n + 1))
                        po = (n % 2) * 512
                        xs = x_sb[:, blk * T + tt * 128:blk * T + (tt + 1) * 128]
                        C.inc(eV, v.scalar_tensor_tensor(out=xs, in0=pSC[:, po:po + 128],
                                                         scalar=1.0, in1=xs, op0=ALU.mult, op1=ALU.add))
                        n += 1
            C.V(_)
            last_tok = C.tok(eV)
        return [last_tok]

    def mamba(seq, prev_toks):
        TQ5 = min(512, T)
        TC5 = T // TQ5
        cvbf = cvec.bitcast(BF16)
        XPOFF = 1536
        DTOFF = 3072
        HCAR = CV["hcarry"]
        xsbbf = x_sb.bitcast(BF16)
        DUO = 0
        YO = 16 * TQ5
        UBF = 0
        XMS = 0
        CVS = 2 * (TQ5 + 3)
        SRS = CVS + 2 * TQ5
        BCO = SRS + TQ5
        CCO = BCO + TQ5
        DT2 = 2 * (CCO + TQ5)
        MB = 0

        stats_tok = ln_stats(prev_toks)

        prev_b = C.tok(eV)
        for j in range(NFC):
            def _(t, j=j, pb=prev_b):
                C.wait(t, pb)
                ln_bcast(t, j, stats_tok)
            C.T(_)
            bc_tok = C.tok(eT)
            def _(v, j=j, bc=bc_tok):
                C.wait(v, bc)
                tk = None
                for cc in range(CCH):
                    xs = x_sb[:, cc * T + j * TQ: cc * T + (j + 1) * TQ]
                    tk = v.tensor_tensor(out=pool1[:, cc * TQ:(cc + 1) * TQ], in0=xs, in1=pSC[:, 0:TQ], op=ALU.subtract)
                C.wait(v, C.inc(eV, tk))
                for cc in range(CCH):
                    tk = v.tensor_tensor(out=p2bf[:, cc * T + j * TQ:cc * T + (j + 1) * TQ],
                                         in0=pool1[:, cc * TQ:(cc + 1) * TQ],
                                         in1=pSC[:, TQ:2 * TQ], op=ALU.mult)
                C.inc(eV, tk)
            C.V(_)
            prev_b = C.tok(eV)
        xn_tok = C.tok(eV)

        spill_box = []
        def _(y):
            C.wait(y, xn_tok)
            C.wait(y, cast_tok["bf_l1a_xproj_t"], cast_tok["bf_l1a_dtproj_t"])
            b = C.dma_batch(y)
            C.qinc(b, y.dma_start(out=spill_x.rearrange("a p t -> p a t"),
                                  in_=x_sb[:, 0:CCH * T].rearrange("p (a t) -> p a t", a=CCH)))
            C.qinc(b, y.dma_start(out=cvbf[:, XPOFF:XPOFF + 16 * 96].rearrange("p (a d) -> p a d", a=16),
                                  in_=dram["bf_l1a_xproj_t"].rearrange("(a p) d -> p a d", p=128)))
            spill_box.append(C.qinc(b, y.dma_start(
                out=cvbf[:, DTOFF:DTOFF + 1024],
                in_=dram["bf_l1a_dtproj_t"])))
        C.Y(_)
        spill_tok = spill_box[0]

        def _(v, st=spill_tok):
            C.wait(v, st)
            v.memset(cvec[:, HCAR:HCAR + 256], 0.0)
            C.inc(eV, v.memset(cvec[:, CV["convtail"]:CV["convtail"] + 48], 0.0))
        C.V(_)
        init_tok = C.tok(eV)

        phase_tok = init_tok
        a_vg = {}
        for tc in range(TC5):
            ts0 = tc * TQ5
            ld_toks = []
            for dc in range(DCH):
                box = []
                def _(y, dc=dc, tc=tc, pt=phase_tok):
                    if dc < 2:
                        C.wait(y, pt)
                        C.wait(y, cast_tok["bf_l1a_inproj_ra"])
                    else:
                        C.wait(y, (eT, a_guard[dc - 2]))
                    b = C.dma_batch(y)
                    box.append(C.qinc(b, y.dma_start(
                        out=w5a[:, (dc % 2) * 1024:(dc % 2) * 1024 + 1024].rearrange("p (a d) -> p a d", a=CCH),
                        in_=dram["bf_l1a_inproj_ra"][:, dc, 0:128].rearrange("(a p) d -> p a d", p=128))))
                C.Y(_)
                ld_toks.append(box[0])
                if dc == 0 and tc == 0:
                    a_guard = {}

                baseT = C.val(eT)
                def _(t, dc=dc, tc=tc, wt=ld_toks[dc], pt=phase_tok):
                    C.wait(t, wt)
                    if dc == 0:
                        C.wait(t, pt)
                    pp = pA if dc % 2 == 0 else pB
                    if dc >= 2:
                        C.wait(t, (eS, a_sguard[dc - 2]))
                    for cc in range(CCH):
                        mm = t.matmul(out=pp[:, 0:TQ5],
                                      lhsT=w5a[:, (dc % 2) * 1024 + cc * 128:(dc % 2) * 1024 + (cc + 1) * 128],
                                      rhs=p2bf[:, cc * T + ts0:cc * T + ts0 + TQ5],
                                      start=(cc == 0), stop=(cc == CCH - 1))
                    C.inc(eT, mm)
                C.T(_)
                a_guard[dc] = C.val(eT)
                if dc == 0:
                    a_sguard = {}

                sA = C.val(eS)
                vA = C.val(eV)
                def _(s, dc=dc, tc=tc, baseT=baseT, vA=vA):
                    C.wait(s, (eT, baseT + 1))
                    k = tc * 16 + dc
                    if k >= 2:
                        C.wait(s, a_vg[k - 2])
                    pp = pA if dc % 2 == 0 else pB
                    C.inc(eS, s.copy(out=pool4[:, XMS + (dc % 2) * (TQ5 + 3) + 3:XMS + (dc % 2) * (TQ5 + 3) + 3 + TQ5],
                                     in_=pp[:, 0:TQ5]))
                C.S(_)
                a_sguard[dc] = C.val(eS)

                def _(v, dc=dc, tc=tc, sA=sA):
                    C.wait(v, (eS, sA + 1))
                    xb = XMS + (dc % 2) * (TQ5 + 3)
                    cb = CVS + (dc % 2) * TQ5
                    tl = CV["convtail"] + dc * 3
                    def ed(instr):
                        C.wait(v, C.inc(eV, instr))
                    ed(v.tensor_copy(out=pool4[:, xb:xb + 3], in_=cvec[:, tl:tl + 3]))
                    ed(v.tensor_scalar(out=pool4[:, cb:cb + TQ5], in0=pool4[:, xb:xb + TQ5],
                                       scalar1=cvec[:, CV["convw"] + dc * 4:CV["convw"] + dc * 4 + 1],
                                       scalar2=None, op0=ALU.mult))
                    for jj in range(1, 4):
                        ed(v.scalar_tensor_tensor(out=pool4[:, cb:cb + TQ5], in0=pool4[:, xb + jj:xb + jj + TQ5],
                                                  scalar=cvec[:, CV["convw"] + dc * 4 + jj:CV["convw"] + dc * 4 + jj + 1],
                                                  in1=pool4[:, cb:cb + TQ5], op0=ALU.mult, op1=ALU.add))
                    ed(v.tensor_copy(out=cvec[:, tl:tl + 3], in_=pool4[:, xb + TQ5:xb + TQ5 + 3]))
                C.V(_)
                a_vg[tc * 16 + dc] = C.tok(eV)

                sA2 = C.val(eS)
                def _(s, dc=dc, vt=C.tok(eV)):
                    C.wait(s, vt)
                    tkn = C.inc(eS, s.activation(out=pool1[:, dc * TQ5:(dc + 1) * TQ5],
                                                 in_=pool4[:, CVS + (dc % 2) * TQ5:CVS + (dc % 2) * TQ5 + TQ5],
                                                 func=AF.Silu,
                                                 bias=cvec[:, CV["convb"] + dc:CV["convb"] + dc + 1]))
                    C.wait(s, tkn)
                    C.inc(eS, s.copy(out=xsbbf[:, UBF + dc * TQ5:UBF + (dc + 1) * TQ5],
                                     in_=pool1[:, dc * TQ5:(dc + 1) * TQ5]))
                C.S(_)
                def _(v, dc=dc, st=C.tok(eS)):
                    C.wait(v, st)
                    C.inc(eV, v.tensor_scalar(out=x_sb[:, YO + dc * TQ5:YO + (dc + 1) * TQ5],
                                              in0=pool1[:, dc * TQ5:(dc + 1) * TQ5],
                                              scalar1=cvec[:, CV["D"] + dc:CV["D"] + dc + 1],
                                              scalar2=None, op0=ALU.mult))
                C.V(_)
            A_tok = [C.tok(eV), C.tok(eS), C.tok(eT)]

            def _(t, tc=tc, at=A_tok):
                C.wait(t, *at)
                for cc2 in range(DCH):
                    t.matmul(out=pC[0:64, 0:TQ5],
                             lhsT=cvbf[:, XPOFF + cc2 * 96:XPOFF + cc2 * 96 + 64],
                             rhs=xsbbf[:, UBF + cc2 * TQ5:UBF + (cc2 + 1) * TQ5],
                             start=(cc2 == 0), stop=(cc2 == DCH - 1))
                for cc2 in range(DCH):
                    t.matmul(out=pD[0:16, 0:TQ5],
                             lhsT=cvbf[:, XPOFF + cc2 * 96 + 64:XPOFF + cc2 * 96 + 80],
                             rhs=xsbbf[:, UBF + cc2 * TQ5:UBF + (cc2 + 1) * TQ5],
                             start=(cc2 == 0), stop=(cc2 == DCH - 1))
                for cc2 in range(DCH):
                    mm = t.matmul(out=pSC[0:16, 0:TQ5],
                                  lhsT=cvbf[:, XPOFF + cc2 * 96 + 80:XPOFF + cc2 * 96 + 96],
                                  rhs=xsbbf[:, UBF + cc2 * TQ5:UBF + (cc2 + 1) * TQ5],
                                  start=(cc2 == 0), stop=(cc2 == DCH - 1))
                C.inc(eT, mm)
            C.T(_)
            B_T = C.tok(eT)

            def _(s, tc=tc, bt=B_T):
                C.wait(s, bt)
                s.copy(out=p4bf[0:64, DT2:DT2 + TQ5], in_=pC[0:64, 0:TQ5])
                s.copy(out=p4bf[64:128, DT2:DT2 + TQ5], in_=pC[0:64, 0:TQ5])
                s.copy(out=pool4[0:16, BCO:BCO + TQ5], in_=pD[0:16, 0:TQ5])
                C.inc(eS, s.copy(out=pool4[0:16, CCO:CCO + TQ5], in_=pSC[0:16, 0:TQ5]))
            C.S(_)
            B_tok = C.tok(eS)

            cT0 = C.val(eT)
            cS0 = C.val(eS)
            cV0 = C.val(eV)
            def _(t, tc=tc, bt=B_tok, cS0=cS0):
                C.wait(t, bt)
                for dc in range(DCH):
                    pp = pA if dc % 2 == 0 else pB
                    if dc >= 2:
                        C.wait(t, (eS, cS0 + (dc - 2) * 4 + 4))
                    mm = t.matmul(out=pp[:, 0:TQ5],
                                  lhsT=cvbf[(dc % 2) * 64:(dc % 2) * 64 + 64,
                                            DTOFF + (dc // 2) * 128:DTOFF + (dc // 2) * 128 + 128],
                                  rhs=p4bf[(dc % 2) * 64:(dc % 2) * 64 + 64, DT2:DT2 + TQ5],
                                  start=True, stop=True)
                    C.inc(eT, mm)
            C.T(_)

            def _(s, tc=tc, cT0=cT0, cV0=cV0):
                for dc in range(DCH):
                    C.wait(s, (eT, cT0 + dc + 1))
                    pp = pA if dc % 2 == 0 else pB
                    dtb = cvec[:, CV["dtb"] + dc:CV["dtb"] + dc + 1]
                    aslot = pool4[:, CVS + (dc % 2) * TQ5:CVS + (dc % 2) * TQ5 + TQ5]
                    if dc >= 2:
                        C.wait(s, (eV, cV0 + (dc - 2) * 2 + 2))
                    tkn = C.inc(eS, s.activation(out=aslot, in_=pp[:, 0:TQ5], func=AF.Abs, bias=dtb))
                    C.wait(s, tkn)
                    tkn = C.inc(eS, s.activation(out=aslot, in_=aslot, func=AF.Exp, scale=-1.0))
                    C.wait(s, tkn)
                    tkn = C.inc(eS, s.activation(out=aslot, in_=aslot, func=AF.Ln,
                                                 bias=cst[:, CS["ones_col"]:CS["ones_col"] + 1]))
                    C.wait(s, tkn)
                    C.inc(eS, s.activation(out=pool3[:, dc * TQ5:(dc + 1) * TQ5], in_=pp[:, 0:TQ5],
                                           func=AF.Relu, bias=dtb))
            C.S(_)

            def _(v, tc=tc, cS0=cS0):
                for dc in range(DCH):
                    C.wait(v, (eS, cS0 + dc * 4 + 4))
                    aslot = pool4[:, CVS + (dc % 2) * TQ5:CVS + (dc % 2) * TQ5 + TQ5]
                    tkn = C.inc(eV, v.tensor_tensor(out=pool3[:, dc * TQ5:(dc + 1) * TQ5],
                                                    in0=pool3[:, dc * TQ5:(dc + 1) * TQ5],
                                                    in1=aslot, op=ALU.add))
                    C.wait(v, tkn)
                    C.inc(eV, v.tensor_tensor(out=x_sb[:, DUO + dc * TQ5:DUO + (dc + 1) * TQ5],
                                              in0=pool3[:, dc * TQ5:(dc + 1) * TQ5],
                                              in1=pool1[:, dc * TQ5:(dc + 1) * TQ5], op=ALU.mult))
            C.V(_)
            C_tok = [C.tok(eV), C.tok(eS)]

            DA0, DB0, H0, HC0 = 0, 2 * TQ5, 4 * TQ5, 6 * TQ5
            dT0 = C.val(eT)
            dS0 = C.val(eS)
            dV0 = C.val(eV)
            dG0 = C.val(eG)

            def _(t, tc=tc, ct=C_tok, dV0=dV0):
                C.wait(t, *ct)
                for n in range(DSTATE):
                    ppB = pA if n % 2 == 0 else pB
                    ppC = pC if n % 2 == 0 else pD
                    if n >= 2:
                        C.wait(t, (eV, dV0 + (n - 2 + 1) * 64))
                    t.matmul(out=ppB[:, 0:TQ5], lhsT=sel16[:, n * 128:(n + 1) * 128],
                             rhs=pool4[0:16, BCO:BCO + TQ5], start=True, stop=True)
                    mm = t.matmul(out=ppC[:, 0:TQ5], lhsT=sel16[:, n * 128:(n + 1) * 128],
                                  rhs=pool4[0:16, CCO:CCO + TQ5], start=True, stop=True)
                    C.inc(eT, mm)
            C.T(_)

            def _(s, tc=tc, ct=C_tok, dV0=dV0):
                C.wait(s, *ct)
                for n in range(DSTATE):
                    for dc in range(DCH):
                        k = n * 16 + dc
                        if k >= 2:
                            C.wait(s, (eV, dV0 + (k - 2) * 4 + 2))
                        C.inc(eS, s.activation(out=pool1[:, DA0 + (k % 2) * TQ5:DA0 + (k % 2) * TQ5 + TQ5],
                                               in_=pool3[:, dc * TQ5:(dc + 1) * TQ5], func=AF.Exp,
                                               scale=cvec[:, CV["A"] + dc * 16 + n:CV["A"] + dc * 16 + n + 1]))
            C.S(_)

            def _(v, tc=tc, dT0=dT0, dS0=dS0, dG0=dG0):
                for n in range(DSTATE):
                    C.wait(v, (eT, dT0 + n + 1))
                    ppB = pA if n % 2 == 0 else pB
                    ppC = pC if n % 2 == 0 else pD
                    for dc in range(DCH):
                        k = n * 16 + dc
                        def ed(instr):
                            C.wait(v, C.inc(eV, instr))
                        C.wait(v, (eS, dS0 + k + 1))
                        if k >= 2:
                            C.wait(v, (eG, dG0 + k - 1))
                        ed(v.tensor_tensor(out=pool1[:, DB0 + (k % 2) * TQ5:DB0 + (k % 2) * TQ5 + TQ5],
                                           in0=x_sb[:, DUO + dc * TQ5:DUO + (dc + 1) * TQ5],
                                           in1=ppB[:, 0:TQ5], op=ALU.mult))
                        ed(v.tensor_tensor_scan(out=pool1[:, H0 + (k % 2) * TQ5:H0 + (k % 2) * TQ5 + TQ5],
                                                data0=pool1[:, DA0 + (k % 2) * TQ5:DA0 + (k % 2) * TQ5 + TQ5],
                                                data1=pool1[:, DB0 + (k % 2) * TQ5:DB0 + (k % 2) * TQ5 + TQ5],
                                                initial=cvec[:, HCAR + k:HCAR + k + 1],
                                                op0=ALU.mult, op1=ALU.add))
                        C.inc(eV, v.tensor_copy(out=cvec[:, HCAR + k:HCAR + k + 1],
                                                in_=pool1[:, H0 + (k % 2) * TQ5 + TQ5 - 1:H0 + (k % 2) * TQ5 + TQ5]))
                        C.inc(eV, v.tensor_tensor(out=pool1[:, HC0 + (k % 2) * TQ5:HC0 + (k % 2) * TQ5 + TQ5],
                                                  in0=pool1[:, H0 + (k % 2) * TQ5:H0 + (k % 2) * TQ5 + TQ5],
                                                  in1=ppC[:, 0:TQ5], op=ALU.mult))
            C.V(_)

            def _(g, tc=tc, dV0=dV0):
                for n in range(DSTATE):
                    for dc in range(DCH):
                        k = n * 16 + dc
                        C.wait(g, (eV, dV0 + k * 4 + 4))
                        C.inc(eG, g.tensor_tensor(out=x_sb[:, YO + dc * TQ5:YO + (dc + 1) * TQ5],
                                                  in0=x_sb[:, YO + dc * TQ5:YO + (dc + 1) * TQ5],
                                                  in1=pool1[:, HC0 + (k % 2) * TQ5:HC0 + (k % 2) * TQ5 + TQ5],
                                                  op=ALU.add))
            C.G(_)
            D_tok = [C.tok(eG), C.tok(eV), C.tok(eS), C.tok(eT)]

            e_ld = []
            for dc in range(DCH):
                box = []
                def _(y, dc=dc, tc=tc, dt=D_tok):
                    if dc < 2:
                        C.wait(y, *dt)
                    else:
                        C.wait(y, (eT, e_guard[dc - 2]))
                    b = C.dma_batch(y)
                    box.append(C.qinc(b, y.dma_start(
                        out=w5a[:, (dc % 2) * 1024:(dc % 2) * 1024 + 1024].rearrange("p (a d) -> p a d", a=CCH),
                        in_=dram["bf_l1a_inproj_ra"][:, dc, 128:256].rearrange("(a p) d -> p a d", p=128))))
                C.Y(_)
                e_ld.append(box[0])
                if dc == 0:
                    e_guard = {}

                eT0 = C.val(eT)
                def _(t, dc=dc, tc=tc, wt=e_ld[dc], dt=D_tok, eS0v=None):
                    C.wait(t, wt)
                    if dc == 0:
                        C.wait(t, *dt)
                    if dc >= 2:
                        C.wait(t, (eS, e_sguard[dc - 2]))
                    pp = pA if dc % 2 == 0 else pB
                    for cc in range(CCH):
                        mm = t.matmul(out=pp[:, 0:TQ5],
                                      lhsT=w5a[:, (dc % 2) * 1024 + cc * 128:(dc % 2) * 1024 + (cc + 1) * 128],
                                      rhs=p2bf[:, cc * T + ts0:cc * T + ts0 + TQ5],
                                      start=(cc == 0), stop=(cc == CCH - 1))
                    C.inc(eT, mm)
                C.T(_)
                e_guard[dc] = C.val(eT)
                if dc == 0:
                    e_sguard = {}

                eV0 = C.val(eV)
                def _(s, dc=dc, eT0=eT0, eV0=eV0):
                    C.wait(s, (eT, eT0 + 1))
                    if dc >= 1:
                        C.wait(s, (eV, e_vguard[dc - 1]))
                    pp = pA if dc % 2 == 0 else pB
                    C.inc(eS, s.activation(out=pool4[:, SRS:SRS + TQ5],
                                           in_=pp[:, 0:TQ5], func=AF.Silu))
                C.S(_)
                e_sguard[dc] = C.val(eS)
                if dc == 0:
                    e_vguard = {}
                sE = C.val(eS)
                def _(v, dc=dc, sE=sE):
                    C.wait(v, (eS, sE))
                    C.inc(eV, v.tensor_tensor(out=p3bf[:, dc * TQ5:(dc + 1) * TQ5],
                                              in0=x_sb[:, YO + dc * TQ5:YO + (dc + 1) * TQ5],
                                              in1=pool4[:, SRS:SRS + TQ5],
                                              op=ALU.mult))
                C.V(_)
                e_vguard[dc] = C.val(eV)
            E_tok = [C.tok(eV), C.tok(eS)]

            f_ld = []
            for mc in range(CCH):
                box = []
                def _(y, mc=mc, tc=tc, et=E_tok):
                    if mc == 0:
                        C.wait(y, *et)
                        C.wait(y, cast_tok["bf_l1a_outproj_t"])
                    else:
                        C.wait(y, (eT, f_guard[mc - 1]))
                    b = C.dma_batch(y)
                    box.append(C.qinc(b, y.dma_start(
                        out=w5a[:, 0:2048].rearrange("p (a d) -> p a d", a=DCH),
                        in_=dram["bf_l1a_outproj_t"][:, mc * 128:(mc + 1) * 128].rearrange("(a p) d -> p a d", p=128))))
                C.Y(_)
                f_ld.append(box[0])
                if mc == 0:
                    f_guard = {}
                fT0 = C.val(eT)
                def _(t, mc=mc, wt=f_ld[mc], et=E_tok, fS=None):
                    C.wait(t, wt)
                    if mc == 0:
                        C.wait(t, *et)
                    if mc >= 2:
                        C.wait(t, (eS, f_sguard[mc - 2]))
                    pp = pA if mc % 2 == 0 else pB
                    for dc in range(DCH):
                        mm = t.matmul(out=pp[:, 0:TQ5],
                                      lhsT=w5a[:, dc * 128:(dc + 1) * 128],
                                      rhs=p3bf[:, dc * TQ5:(dc + 1) * TQ5],
                                      start=(dc == 0), stop=(dc == DCH - 1))
                    C.inc(eT, mm)
                C.T(_)
                f_guard[mc] = C.val(eT)
                if mc == 0:
                    f_sguard = {}
                def _(s, mc=mc, fT0=fT0):
                    C.wait(s, (eT, fT0 + 1))
                    C.inc(eS, s.copy(out=pool4[:, MB + mc * TQ5:MB + (mc + 1) * TQ5], in_=pA[:, 0:TQ5] if mc % 2 == 0 else pB[:, 0:TQ5]))
                C.S(_)
                f_sguard[mc] = C.val(eS)
            F_tok = C.tok(eS)

            mbox = []
            def _(y, tc=tc, ft=F_tok):
                C.wait(y, ft)
                b = C.dma_batch(y)
                mbox.append(C.qinc(b, y.dma_start(
                    out=spill_mc[:, :, ts0:ts0 + TQ5].rearrange("a p t -> p a t"),
                    in_=pool4[:, MB:MB + CCH * TQ5].rearrange("p (a t) -> p a t", a=CCH))))
            C.Y(_)
            phase_tok = mbox[0]

        scan_done = phase_tok
        return mamba_ffn(seq, scan_done)

    def mamba_ffn(seq, scan_done):
        xsbbf5 = x_sb.bitcast(BF16)
        TF = min(1024, T)
        NTC2 = T // TF
        NHALF = max(1, TF // 512)
        SW = min(512, TF)
        SC_C = 16
        SC_Q = 48
        scnt = [0]
        _sq_junk_toks = {}

        last = scan_done
        for tc2 in range(NTC2):
            tf0 = tc2 * TF
            box = []
            def _(y, tc2=tc2, lt=last):
                C.wait(y, lt)
                b = C.dma_batch(y)
                box.append(C.qinc(b, y.dma_start(
                    out=pool1[:, 0:CCH * TF].rearrange("p (a t) -> p a t", a=CCH),
                    in_=spill_mc[:, :, tf0:tf0 + TF].rearrange("a p t -> p a t"))))
            C.Y(_)
            m_tok = box[0]

            def _(s, mt=m_tok):
                C.wait(s, mt)
                for cc in range(CCH):
                    t = C.inc(eS, s.copy(out=p2bf[:, cc * TF:(cc + 1) * TF],
                                         in_=pool1[:, cc * TF:(cc + 1) * TF]))
            C.S(_)
            mbf_tok = C.tok(eS)

            h1_ld = []
            for hc in range(32):
                bx = []
                def _(y, hc=hc, mt=mbf_tok):
                    if hc < 2:
                        C.wait(y, mt)
                        C.wait(y, cast_tok["bf_l1a_w1t"])
                    else:
                        C.wait(y, (eT, h1_guard[hc - 2]))
                    b = C.dma_batch(y)
                    bx.append(C.qinc(b, y.dma_start(
                        out=w5a[:, (hc % 2) * 1024:(hc % 2) * 1024 + 1024].rearrange("p (a d) -> p a d", a=CCH),
                        in_=dram["bf_l1a_w1t"][:, hc * 128:(hc + 1) * 128].rearrange("(a p) d -> p a d", p=128))))
                C.Y(_)
                h1_ld.append(bx[0])
                if hc == 0:
                    h1_guard = {}
                    h1_sg = {}
                hT0 = C.val(eT)
                def _(t, hc=hc, wt=h1_ld[hc], mt=mbf_tok):
                    C.wait(t, wt)
                    if hc == 0:
                        C.wait(t, mt)
                    if hc >= 2:
                        C.wait(t, (eS, h1_sg[hc - 2]))
                    for hf in range(NHALF):
                        pp = (pA, pB)[hf] if hc % 2 == 0 else (pC, pD)[hf]
                        for cc in range(CCH):
                            mm = t.matmul(out=pp[:, 0:SW],
                                          lhsT=w5a[:, (hc % 2) * 1024 + cc * 128:(hc % 2) * 1024 + (cc + 1) * 128],
                                          rhs=p2bf[:, cc * TF + hf * SW:cc * TF + hf * SW + SW],
                                          start=(cc == 0), stop=(cc == CCH - 1))
                        C.inc(eT, mm)
                C.T(_)
                h1_guard[hc] = C.val(eT)
                def _(s, hc=hc, hT0=hT0):
                    for hf in range(NHALF):
                        C.wait(s, (eT, hT0 + hf + 1))
                        pp = (pA, pB)[hf] if hc % 2 == 0 else (pC, pD)[hf]
                        C.inc(eS, s.activation(out=xsbbf5[:, hc * TF + hf * SW:hc * TF + hf * SW + SW],
                                               in_=pp[:, 0:SW], func=AF.Gelu,
                                               bias=cvec[:, CV["b1"] + hc:CV["b1"] + hc + 1]))
                C.S(_)
                h1_sg[hc] = C.val(eS)
            h1_tok = C.tok(eS)

            f_prevT = None
            f_vg = {}
            for mc in range(CCH):
                for sl in range(2):
                    bx = []
                    def _(y, mc=mc, sl=sl, ht=h1_tok, fp=f_prevT):
                        if mc == 0 and sl == 0:
                            C.wait(y, ht)
                            C.wait(y, cast_tok["bf_l1a_w2t"])
                        else:
                            C.wait(y, (eT, fp))
                        b = C.dma_batch(y)
                        bx.append(C.qinc(b, y.dma_start(
                            out=w5a[:, 0:2048].rearrange("p (a d) -> p a d", a=16),
                            in_=dram["bf_l1a_w2t"][sl * 2048:(sl + 1) * 2048, mc * 128:(mc + 1) * 128].rearrange("(a p) d -> p a d", p=128))))
                    C.Y(_)
                    def _(t, mc=mc, sl=sl, wt=bx[0]):
                        C.wait(t, wt)
                        if mc >= 1 and sl == 0:
                            C.wait(t, f_vg[mc - 1])
                        for hh in range(16):
                            hc = sl * 16 + hh
                            for hf in range(NHALF):
                                pp = (pC, pD)[hf]
                                mm = t.matmul(out=pp[:, 0:SW],
                                              lhsT=w5a[:, hh * 128:(hh + 1) * 128],
                                              rhs=xsbbf5[:, hc * TF + hf * SW:hc * TF + hf * SW + SW],
                                              start=(hc == 0), stop=(hc == 31))
                            C.inc(eT, mm)
                    C.T(_)
                    f_prevT = C.val(eT)
                fT_end = C.val(eT)
                def _(v, mc=mc, tc2=tc2, fT_end=fT_end):
                    C.wait(v, (eT, fT_end))
                    for hf in range(NHALF):
                        pp = (pC, pD)[hf]
                        idx = (tc2 * CCH + mc) * NHALF + hf
                        tkn = C.inc(eV, v.scalar_tensor_tensor(
                            out=pool3[:, mc * TF + hf * SW:mc * TF + hf * SW + SW],
                            in0=pp[:, 0:SW], scalar=cvec[:, CV["b2"] + mc:CV["b2"] + mc + 1],
                            in1=pool1[:, mc * TF + hf * SW:mc * TF + hf * SW + SW],
                            op0=ALU.add, op1=ALU.add,
                            accum_out=stat[:, SC_C + idx:SC_C + idx + 1]))
                        scnt[0] = max(scnt[0], idx + 1)
                C.V(_)
                f_vg[mc] = C.tok(eV)
                def _(s, mc=mc, tc2=tc2, vt=f_vg[mc]):
                    C.wait(s, vt)
                    for hf in range(NHALF):
                        idx = (tc2 * CCH + mc) * NHALF + hf
                        if idx - 8 in _sq_junk_toks:
                            C.wait(s, _sq_junk_toks[idx - 8])
                        _sq_junk_toks[idx] = C.inc(eS, s.activation(
                            out=pool4[:, ((mc * NHALF + hf) % 8) * SW:((mc * NHALF + hf) % 8) * SW + SW],
                            in_=pool3[:, mc * TF + hf * SW:mc * TF + hf * SW + SW],
                            func=AF.Square,
                            accum_out=stat[:, SC_Q + idx:SC_Q + idx + 1]))
                C.S(_)
            csq_tok = C.tok(eS)
            cv_tok = C.tok(eV)

            box2 = []
            def _(y, tc2=tc2, ct=cv_tok, st=csq_tok):
                C.wait(y, ct, st)
                b = C.dma_batch(y)
                box2.append(C.qinc(b, y.dma_start(
                    out=spill_mc[:, :, tf0:tf0 + TF].rearrange("a p t -> p a t"),
                    in_=pool3[:, 0:CCH * TF].rearrange("p (a t) -> p a t", a=CCH))))
            C.Y(_)
            last = box2[0]

        NS = scnt[0]
        def _(t, lt=last, ct=C.tok(eS), cvt=C.tok(eV), NS=NS):
            C.wait(t, ct, cvt)
            t.matmul(out=pA[0:1, 0:NS], lhsT=cst[:, CS["ones_col"]:CS["ones_col"] + 1],
                     rhs=stat[:, SC_C:SC_C + NS], start=True, stop=True)
            mm = t.matmul(out=pA[0:1, 32:32 + NS], lhsT=cst[:, CS["ones_col"]:CS["ones_col"] + 1],
                          rhs=stat[:, SC_Q:SC_Q + NS], start=True, stop=True)
            C.inc(eT, mm)
        C.T(_)
        gs_T = C.tok(eT)
        def _(v, gt=gs_T, NS=NS):
            def ed(instr):
                C.wait(v, C.inc(eV, instr))
            C.wait(v, gt)
            ed(v.tensor_reduce(out=stat[0:1, 80:81], in_=pA[0:1, 0:NS],
                               axis=mybir.AxisListType.X, op=ALU.add))
            ed(v.tensor_reduce(out=stat[0:1, 81:82], in_=pA[0:1, 32:32 + NS],
                               axis=mybir.AxisListType.X, op=ALU.add))
            ed(v.tensor_scalar(out=stat[0:1, 82:83], in0=stat[0:1, 80:81],
                               scalar1=1.0 / (T * CDIM), scalar2=None, op0=ALU.mult))
            ed(v.tensor_tensor(out=stat[0:1, 83:84], in0=stat[0:1, 82:83],
                               in1=stat[0:1, 82:83], op=ALU.mult))
            ed(v.scalar_tensor_tensor(out=stat[0:1, 83:84], in0=stat[0:1, 81:82],
                                      scalar=1.0 / (T * CDIM), in1=stat[0:1, 83:84],
                                      op0=ALU.mult, op1=ALU.subtract))
        C.V(_)
        gv_tok = C.tok(eV)
        def _(s, gt=gv_tok):
            C.wait(s, gt)
            tkn = C.inc(eS, s.activation(out=stat[0:1, 84:85], in_=stat[0:1, 83:84], func=AF.Ln,
                                         bias=cst[0:1, CS["epsln"]:CS["epsln"] + 1]))
            C.wait(s, tkn)
            C.inc(eS, s.activation(out=stat[0:1, 84:85], in_=stat[0:1, 84:85], func=AF.Exp, scale=-0.5))
        C.S(_)
        gr_tok = C.tok(eS)
        def _(v, gt=gr_tok):
            C.wait(v, gt)
            C.inc(eV, v.scalar_tensor_tensor(out=stat[0:1, 85:86], in0=stat[0:1, 82:83],
                                             scalar=-1.0, in1=stat[0:1, 84:85],
                                             op0=ALU.mult, op1=ALU.mult))
        C.V(_)
        gn_tok = C.tok(eV)
        def _(t, gt=gn_tok):
            C.wait(t, gt)
            C.inc(eT, t.matmul(out=pB[:, 0:2], lhsT=cst[0:1, CS["ones_row"]:CS["ones_row"] + 128],
                               rhs=stat[0:1, 84:86], start=True, stop=True))
        C.T(_)
        gb_T = C.tok(eT)
        def _(s, gt=gb_T):
            C.wait(s, gt)
            C.inc(eS, s.copy(out=cvec[:, CV["gstat"]:CV["gstat"] + 2], in_=pB[:, 0:2]))
        C.S(_)
        gcol_tok = C.tok(eS)

        bx = []
        def _(y, lt=last):
            C.wait(y, lt)
            b = C.dma_batch(y)
            bx.append(C.qinc(b, y.dma_start(out=x_sb[:, 0:CCH * T].rearrange("p (a t) -> p a t", a=CCH),
                                            in_=spill_x.rearrange("a p t -> p a t"))))
        C.Y(_)
        xl_tok = bx[0]

        fin_tok = None
        for j in range(NFC):
            bx2 = []
            def _(y, j=j, ft=fin_tok):
                if ft is not None:
                    C.wait(y, ft)
                b = C.dma_batch(y)
                bx2.append(C.qinc(b, y.dma_start(
                    out=pool3[:, 0:CCH * TQ].rearrange("p (a t) -> p a t", a=CCH),
                    in_=spill_mc[:, :, j * TQ:(j + 1) * TQ].rearrange("a p t -> p a t"))))
            C.Y(_)
            c_tok = bx2[0]
            def _(v, j=j, ct=c_tok, xt=xl_tok, gt=gcol_tok):
                C.wait(v, ct, xt, gt)
                tk = None
                for cc in range(CCH):
                    xs = x_sb[:, cc * T + j * TQ: cc * T + (j + 1) * TQ]
                    tk = v.scalar_tensor_tensor(out=xs, in0=pool3[:, cc * TQ:(cc + 1) * TQ],
                                                scalar=cvec[:, CV["gstat"]:CV["gstat"] + 1],
                                                in1=xs, op0=ALU.mult, op1=ALU.add)
                C.wait(v, C.inc(eV, tk))
                for cc in range(CCH):
                    xs = x_sb[:, cc * T + j * TQ: cc * T + (j + 1) * TQ]
                    tk = v.tensor_scalar(out=xs, in0=xs,
                                         scalar1=cvec[:, CV["gstat"] + 1:CV["gstat"] + 2],
                                         scalar2=None, op0=ALU.add)
                C.inc(eV, tk)
            C.V(_)
            fin_tok = C.tok(eV)
        return [fin_tok]

    def load_x(seq, prev_toks, chunk_prev=None):
        toks = []
        def _(y):
            C.wait(y, *prev_toks)
            for cc in range(CCH):
                if chunk_prev is not None:
                    C.wait(y, chunk_prev[cc])
                b = C.dma_batch(y)
                toks.append(C.qinc(b, y.dma_start(out=x_sb[:, cc * T:(cc + 1) * T],
                                                  in_=x_in[seq, cc])))
        C.Y(_)
        return toks

    def store_x(seq, prev_toks):
        toks = []
        def _(y):
            C.wait(y, *prev_toks)
            for cc in range(CCH):
                b = C.dma_batch(y)
                toks.append(C.qinc(b, y.dma_start(out=out[seq, cc],
                                                  in_=x_sb[:, cc * T:(cc + 1) * T])))
        C.Y(_)
        return toks

    last = [C.tok(eV)]
    store_toks = []
    prev_store = None
    for seq in range(nseq):
        last = load_x(seq, (last + [tok_theta, tok_A, tok_rbbc, tok_rw]) if seq == 0 else [],
                      chunk_prev=prev_store)
        if 9 in stages:
            last = [ln_stats(last)]
        if 7 in stages or 8 in stages:
            stats_tok8 = ln_stats(last)
            def _(t, st=stats_tok8):
                ln_bcast(t, 0, st)
            C.T(_)
            bc8 = C.tok(eT)
            def _(v, bc=bc8):
                C.wait(v, bc)
                tk = None
                if 8 in stages:
                    for cc in range(CCH):
                        xs = x_sb[:, cc * T: cc * T + TQ]
                        tk = v.tensor_tensor(out=pool1[:, cc * TQ:(cc + 1) * TQ], in0=xs, in1=pSC[:, 0:TQ], op=ALU.subtract)
                    C.wait(v, C.inc(eV, tk))
                    for cc in range(CCH):
                        xs = x_sb[:, cc * T: cc * T + TQ]
                        tk = v.tensor_tensor(out=xs, in0=pool1[:, cc * TQ:(cc + 1) * TQ], in1=pSC[:, TQ:2 * TQ], op=ALU.mult)
                    C.inc(eV, tk)
                else:
                    C.inc(eV, v.tensor_copy(out=pool1[:, 0:TQ], in_=pSC[:, 0:TQ]))
            C.V(_)
            last = [C.tok(eV)]
        if 1 in stages:
            last = rwkv(seq, last)
        if 2 in stages:
            last = attention(seq, last)
        if 3 in stages:
            last = rwkv(seq, last)
        if 4 in stages:
            last = moe(seq, last)
        if 5 in stages:
            last = mamba(seq, last)
        if 6 in stages:
            last = moe(seq, last)
        st = store_x(seq, last)
        store_toks += st
        prev_store = st
        last = st

    def _(y):
        C.wait(y, *store_toks)
    C.Y(_)


_cache = {}


def _host_prep(inputs, T=TFULL):
    f32 = np.float32
    m = {}
    m["l0f_wr_t"] = np.ascontiguousarray(inputs["L0_ffn_W_r"].T)
    m["l0f_wk_t"] = np.ascontiguousarray(inputs["L0_ffn_W_k"].T)
    m["l0f_wv_t"] = np.ascontiguousarray(inputs["L0_ffn_W_v"].T)
    m["l0a_wq"] = np.ascontiguousarray(inputs["L0_attn_w_q"])
    m["l0a_wk"] = np.ascontiguousarray(inputs["L0_attn_w_k"])
    m["l0a_wv"] = np.ascontiguousarray(inputs["L0_attn_w_v"])
    m["l0a_wo"] = np.ascontiguousarray(inputs["L0_attn_w_o"])
    ip = inputs["L1_attn_in_proj_w"]
    ipt = ip.T
    ra = np.empty((CDIM, DCH, 256), dtype=f32)
    for dc in range(DCH):
        ra[:, dc, 0:128] = ipt[:, dc * 128:(dc + 1) * 128]
        ra[:, dc, 128:256] = ipt[:, DIN + dc * 128:DIN + (dc + 1) * 128]
    m["l1a_inproj_ra"] = np.ascontiguousarray(ra)
    m["l1a_xproj_t"] = np.ascontiguousarray(inputs["L1_attn_x_proj_w"].T)
    dtt = inputs["L1_attn_dt_proj_w"].T
    dtp = np.empty((128, 1024), dtype=f32)
    for h2 in range(2):
        for g in range(8):
            dtp[h2 * 64:(h2 + 1) * 64, g * 128:(g + 1) * 128] = dtt[:, (2 * g + h2) * 128:(2 * g + h2 + 1) * 128]
    m["l1a_dtproj_t"] = dtp
    m["l1a_outproj_t"] = np.ascontiguousarray(inputs["L1_attn_out_proj_w"].T)
    m["l1a_w1t"] = np.ascontiguousarray(inputs["L1_attn_ffn_w1"].T)
    m["l1a_w2t"] = np.ascontiguousarray(inputs["L1_attn_ffn_w2"].T)
    m["l1m_rwt"] = np.ascontiguousarray(inputs["L1_ffn_router_w"].T)
    m["l1m_ew"] = np.ascontiguousarray(inputs["L1_ffn_expert_weights"])
    def col(a, nch):
        return np.ascontiguousarray(a.reshape(nch, 128, -1).transpose(1, 0, 2).reshape(128, -1)).astype(f32)
    m["l0f_mur"] = col(inputs["L0_ffn_mu_r"], CCH)
    m["l0f_muk"] = col(inputs["L0_ffn_mu_k"], CCH)
    m["l0a_normw"] = col(inputs["L0_attn_norm_w"], CCH)
    m["l1a_convw"] = col(inputs["L1_attn_conv_w"].reshape(DIN, 4), DCH)
    m["l1a_convb"] = col(inputs["L1_attn_conv_b"], DCH)
    m["l1a_dtb"] = col(inputs["L1_attn_dt_proj_b"], DCH)
    m["l1a_alog"] = col(inputs["L1_attn_A_log"], DCH)
    m["l1a_dvec"] = col(inputs["L1_attn_D"], DCH)
    m["l1a_b1"] = col(inputs["L1_attn_ffn_b1"], 32)
    m["l1a_b2"] = col(inputs["L1_attn_ffn_b2"], CCH)
    m["l1m_rb"] = inputs["L1_ffn_router_b"].reshape(1, NEXP).astype(f32)
    m["pos_t"] = np.arange(T, dtype=np.float64).reshape(1, T).astype(f32)
    dh = np.arange(DH)
    dhexp = (-(dh // 2) / 32.0).astype(f32)
    m["dhexp"] = np.concatenate([dhexp, dhexp]).reshape(128, 1)
    sgn = np.where(np.arange(T) % 2 == 0, -1.0, 1.0).astype(f32).reshape(1, T)
    m["sgn_t"] = sgn
    import ml_dtypes
    m["ident_bf"] = np.eye(128, dtype=ml_dtypes.bfloat16)
    m["ident_f"] = np.eye(128, dtype=f32)
    mask = np.zeros((128, 128), dtype=np.float64)
    for a in range(128):
        mask[a, :a] = MASK_NEG
    m["maskT_bf"] = mask.astype(ml_dtypes.bfloat16)
    sel = np.zeros((16, 2048), dtype=f32)
    for n in range(16):
        sel[n, n * 128:(n + 1) * 128] = 1.0
    m["sel16"] = sel
    op = np.zeros((128, 8), dtype=ml_dtypes.bfloat16)
    op[0:64, 0:4] = 1
    op[64:128, 4:8] = 1
    m["onespair"] = op
    return m


def kernel(**inputs):
    T = TFULL
    x = np.asarray(inputs["L0_attn_x"])
    B = x.shape[0]
    shared = _host_prep(inputs, T)
    key = ("full", T)
    if key not in _cache:
        _cache[key] = build_program(T=T, nseq=NSEQ, stages=_STAGES)
    nc = _cache[key]
    in_maps = []
    for core in range(NCORES):
        im = dict(shared)
        xs = x[core * NSEQ:(core + 1) * NSEQ]
        xt = np.ascontiguousarray(xs.transpose(0, 2, 1))
        im["x"] = xt.reshape(NSEQ, CCH, 128, T)
        in_maps.append(im)
    res = run_bass_kernel_spmd(nc, in_maps, core_ids=list(range(NCORES)))
    outs = []
    for core in range(NCORES):
        o = res.results[core]["out"]
        o = o.reshape(NSEQ, CDIM, T).transpose(0, 2, 1)
        outs.append(o)
    return np.ascontiguousarray(np.concatenate(outs, axis=0)).astype(np.float32)


_STAGES = (1, 2, 3, 4, 5, 6)
_ATT_UPTO = 99
